# version 30 (final)
# baseline (speedup 1.0000x reference)
import dataclasses
import numpy as np
import ml_dtypes

import concourse.bass as bass
import concourse.mybir as mybir
from concourse.bass_utils import run_bass_kernel_spmd

F32 = mybir.dt.float32
BF16 = mybir.dt.bfloat16
U8 = mybir.dt.uint8
AF = mybir.ActivationFunctionType
ALU = mybir.AluOpType
AX = mybir.AxisListType
BFNP = ml_dtypes.bfloat16

MAGIC = float(np.float32(12582912.0))
INV7 = float(np.float32(1.0) / np.float32(7.0))
QMAX_INV = float(np.float32(1.0) / np.float32(32767.0))
EPS_LN = 1e-5
CMAX16 = float(np.float32(128.0) + np.float32(65535.0) / np.float32(65536.0))

NDC, NHC, NIC, NTC = 8, 16, 24, 4
NCH = 16

ENGS = ("tensor", "vector", "scalar", "gpsimd", "sync")


class Sched:

    def __init__(self, regmap):
        self.items = []
        self.reg = regmap

    def _names(self, aps):
        out = []
        for a in aps:
            if a is None:
                continue
            if isinstance(a, str):
                out.append(a)
            elif hasattr(a, "tensor"):
                out.append(a.tensor.name)
            else:
                out.append(a.name)
        return out

    def op(self, eng, fn, ins, outs, fence=False):
        self.items.append((eng, fn, self._names(ins), self._names(outs), fence))

    def t(self, fn, ins, outs, fence=False):
        self.op("tensor", fn, ins, outs, fence)

    def v(self, fn, ins, outs, fence=False):
        self.op("vector", fn, ins, outs, fence)

    def s(self, fn, ins, outs, fence=False):
        self.op("scalar", fn, ins, outs, fence)

    def g(self, fn, ins, outs, fence=False):
        self.op("gpsimd", fn, ins, outs, fence)

    def dma(self, out, in_, **kw):
        self.op("sync",
                lambda e, o=out, i=in_, k=kw: e.dma_start(out=o, in_=i, **k),
                [in_], [out])

    def gdma(self, out, in_, **kw):
        self.op("gdma",
                lambda e, o=out, i=in_, k=kw: e.dma_start(out=o, in_=i, **k),
                [in_], [out])

    def fence_all(self):
        pass

    SER_LO = -1
    SER_HI = -1

    def plan(self, nlanes=6):
        cnt = {e: 0 for e in ENGS}
        dma_idx = [0]
        gdma_idx = [0]
        nglanes = 2
        writer = {}
        readers = {}
        waited = {}
        reg = self.reg

        def rng(n):
            return reg.get(n, (n, 0, 1))

        def overlaps(n1, n2):
            r1, r2 = rng(n1), rng(n2)
            return r1[0] == r2[0] and r1[1] < r2[2] and r2[1] < r1[2]

        DMAE = ("sync", "gdma")
        state_names = []
        waits = []
        lanes = []
        throttle = []
        prev_key = None
        for idx_item, (eng, fn, ins, outs, fence) in enumerate(self.items):
            deps = {}

            def add_dep(dkey, dunits, deng):
                if deng == eng and eng not in DMAE:
                    return
                deps[dkey] = max(deps.get(dkey, 0), dunits)

            for n in ins:
                for sn in state_names:
                    if overlaps(n, sn) and sn in writer:
                        k, u, de = writer[sn]
                        add_dep(k, u, de)
            for n in outs:
                for sn in state_names:
                    if not overlaps(n, sn):
                        continue
                    if sn in writer:
                        k, u, de = writer[sn]
                        add_dep(k, u, de)
                    for k2, (u2, de2) in readers.get(sn, {}).items():
                        add_dep(k2, u2, de2)
            if (self.SER_LO <= idx_item < self.SER_HI and prev_key is not None
                    and prev_key[0] != eng):
                deps[prev_key[0]] = max(deps.get(prev_key[0], 0), prev_key[1])
            if fence and eng not in DMAE and cnt[eng] > 0:
                deps[eng] = max(deps.get(eng, 0), cnt[eng])
            wl = []
            for dkey, dunits in deps.items():
                if waited.get((eng, dkey), 0) >= dunits:
                    continue
                waited[(eng, dkey)] = dunits
                wl.append((dkey, dunits))
            waits.append(wl)
            if eng in DMAE:
                if eng == "sync":
                    i = dma_idx[0]
                    lane = f"lane{i % nlanes}"
                    k = i // nlanes
                    dma_idx[0] += 1
                else:
                    i = gdma_idx[0]
                    lane = f"glane{i % nglanes}"
                    k = i // nglanes
                    gdma_idx[0] += 1
                throttle.append((lane, 16 * k) if k > 0 else None)
                lanes.append(lane)
                mykey, after = lane, 16 * (k + 1)
            else:
                cnt[eng] += 1
                lanes.append(None)
                throttle.append(None)
                mykey, after = eng, cnt[eng]
            for n in outs:
                if n not in writer and n not in readers:
                    state_names.append(n)
                writer[n] = (mykey, after, eng)
                readers[n] = {}
            for n in ins:
                if n not in writer and n not in readers:
                    state_names.append(n)
                readers.setdefault(n, {})[mykey] = (after, eng)
            prev_key = (mykey, after)
        lane_tot = {}
        for i, ln in enumerate(lanes):
            if ln is not None:
                lane_tot[ln] = lane_tot.get(ln, 0) + 16
        return waits, cnt, lanes, throttle, lane_tot, dma_idx[0] and nlanes or 0


def build_program():
    nc = bass.Bass("TRN2", target_bir_lowering=False,
                   dynamic_dma_scratch_size=4096)

    xt_h = nc.dram_tensor("xt", [1024, 2048], F32, kind="ExternalInput").ap()
    winp_h = nc.dram_tensor("winp", [32, 128, 1024], F32, kind="ExternalInput").ap()
    woutp_h = nc.dram_tensor("woutp", [8, 128, 2048], F32, kind="ExternalInput").ap()
    wssmp_h = nc.dram_tensor("wssmp", [128, 512], F32, kind="ExternalInput").ap()
    wu_n_h = nc.dram_tensor("wu_nat", [24, 128, 1024], F32, kind="ExternalInput").ap()
    wg_n_h = nc.dram_tensor("wg_nat", [24, 128, 1024], F32, kind="ExternalInput").ap()
    wd_n_h = nc.dram_tensor("wd_nat", [8, 128, 3072], F32, kind="ExternalInput").ap()
    wu_t_h = nc.dram_tensor("wu_t", [8, 128, 3072], F32, kind="ExternalInput").ap()
    wg_t_h = nc.dram_tensor("wg_t", [8, 128, 3072], F32, kind="ExternalInput").ap()
    wd_t_h = nc.dram_tensor("wd_t", [24, 128, 1024], F32, kind="ExternalInput").ap()
    binc_h = nc.dram_tensor("binc", [128, 32], F32, kind="ExternalInput").ap()
    convw_h = nc.dram_tensor("convw", [128, 64], F32, kind="ExternalInput").ap()
    convb_h = nc.dram_tensor("convb", [128, 16], F32, kind="ExternalInput").ap()
    boutc_h = nc.dram_tensor("boutc", [128, 8], F32, kind="ExternalInput").ap()
    dvc_h = nc.dram_tensor("dvc", [128, 16], F32, kind="ExternalInput").ap()
    bssm_h = nc.dram_tensor("bssm", [32, 1], F32, kind="ExternalInput").ap()
    amat_h = nc.dram_tensor("amat", [32, 32], F32, kind="ExternalInput").ap()
    bvc_h = nc.dram_tensor("bvc", [32, 1], F32, kind="ExternalInput").ap()
    cvc_h = nc.dram_tensor("cvc", [32, 1], F32, kind="ExternalInput").ap()

    out_h = nc.dram_tensor("out_t", [1024, 2048], F32, kind="ExternalOutput").ap()

    spup_h = nc.dram_tensor("sp_up", [24, 128, 2048], BF16, kind="Internal").ap()
    spgt_h = nc.dram_tensor("sp_gt", [24, 128, 2048], BF16, kind="Internal").ap()
    x1_h = nc.dram_tensor("sp_x1", [8, 128, 2048], F32, kind="Internal").ap()
    spg2_h = nc.dram_tensor("sp_g2", [24, 128, 2048], BF16, kind="Internal").ap()
    yd_h = nc.dram_tensor("sp_yd", [8, 128, 2048], BF16, kind="Internal").ap()
    cc_in = [nc.dram_tensor(f"cc_in{i}", [1, 8], F32, kind="Internal")
             for i in range(5)]
    cc_out = [nc.dram_tensor(f"cc_out{i}", [1, 8], F32, kind="Internal")
              for i in range(5)]

    idm = nc.inline_tensor(np.eye(128, dtype=np.float32), name="idm").ap()
    revm = nc.inline_tensor(np.eye(128, dtype=np.float32)[:, ::-1].copy(),
                            name="revm").ap()
    onesb = nc.inline_tensor(np.ones((128, 1), dtype=BFNP), name="onesb").ap()
    onesr = nc.inline_tensor(np.ones((1, 128), dtype=np.float32), name="onesr").ap()
    e4np = np.zeros((4, 128), np.float32)
    for bb in range(4):
        e4np[bb, bb * 32:(bb + 1) * 32] = 1.0
    e4i = nc.inline_tensor(e4np, name="e4i").ap()
    id4i = nc.inline_tensor(np.tile(np.eye(128, dtype=np.float32), (1, 4)),
                            name="id4i").ap()
    selnp = np.zeros((64, 1024), np.float32)
    for r in range(64):
        for j in range(8):
            p0 = (r % 32 - 4 * j) * 32
            if 0 <= p0 < 128:
                selnp[r, j * 128 + p0: j * 128 + p0 + 32] = 1.0
    selsi = nc.inline_tensor(selnp, name="selsi").ap()

    REG = {}
    sch = Sched(REG)

    from contextlib import ExitStack
    with ExitStack() as _ctx:
        def sb(name, shape, dtype):
            return _ctx.enter_context(nc.sbuf_tensor(name, shape, dtype))

        ID = sb("ID", [128, 128], F32)
        REV = sb("REV", [128, 128], F32)
        ONESB = sb("ONESB", [128, 1], BF16)
        ONESR = sb("ONESR", [1, 128], F32)
        E4 = sb("E4", [4, 128], F32)
        SELS = sb("SELS", [64, 1024], F32)
        BINC = sb("BINC", [128, 32], F32)
        CONVW = sb("CONVW", [128, 64], F32)
        CONVB = sb("CONVB", [128, 16], F32)
        BOUTC = sb("BOUTC", [128, 8], F32)
        DVC = sb("DVC", [128, 16], F32)
        BSSM = sb("BSSM", [32, 1], F32)
        AT = sb("AT", [32, 32], F32)
        BVC = sb("BVC", [32, 1], F32)
        CVC = sb("CVC", [32, 1], F32)
        WSSMB = sb("WSSMB", [128, 512], BF16)
        ROWS = sb("ROWS", [1, 6144], F32)
        MXC = sb("MXC", [128, 8], F32)
        COLS = sb("COLS", [128, 24], F32)
        SCL = sb("SCL", [1, 16], F32)
        CCSB = sb("CCSB", [1, 8], F32)
        MROW = sb("MROW", [1, 128], F32)
        UTB = sb("UTB", [128, 512], F32)
        HSB = sb("HSB", [32, 17], F32)
        GTB = sb("GTB", [32, 16], F32)
        PCB = sb("PCB", [32, 32], F32)
        QTB = sb("QTB", [32, 64], F32)
        SSB = sb("SSB", [128, 16], F32)
        SR16 = sb("SR16", [16, 128], F32)
        CBUF = sb("CBUF", [32, 132], F32)
        CRB = sb("CRB", [32, 128], F32)
        WRB = sb("WRB", [32, 256], F32)
        FSLN = sb("FSLN", [128, 512], F32)
        FSLN2 = sb("FSLN2", [128, 512], F32)
        FSLN3 = sb("FSLN3", [128, 512], F32)
        SQS = sb("SQS", [128, 512], BF16)
        SQSb = sb("SQSb", [128, 512], BF16)
        SQS2 = sb("SQS2", [128, 512], BF16)
        SQS2b = sb("SQS2b", [128, 512], BF16)
        R1G = sb("R1G", [128, 102400], U8)
        R2G = sb("R2G", [128, 73728], U8)
        PS_BIG = _ctx.enter_context(nc.psum_tensor("PS_BIG", [128, 1024], F32))
        PS_BIG2 = _ctx.enter_context(nc.psum_tensor("PS_BIG2", [128, 1024], F32))
        PS_MM = _ctx.enter_context(nc.psum_tensor("PS_MM", [128, 512], F32))
        PS_MM2 = _ctx.enter_context(nc.psum_tensor("PS_MM2", [128, 512], F32))
        PS_SM = _ctx.enter_context(nc.psum_tensor("PS_SM", [128, 512], F32))
        PS_SM2 = _ctx.enter_context(nc.psum_tensor("PS_SM2", [128, 512], F32))
        sems = {e: _ctx.enter_context(nc.semaphore(f"s_{e}")) for e in ENGS
                if e != "sync"}
        NLANES = 6
        for _li in range(NLANES):
            sems[f"lane{_li}"] = _ctx.enter_context(nc.semaphore(f"lane{_li}"))
        for _li in range(2):
            sems[f"glane{_li}"] = _ctx.enter_context(nc.semaphore(f"glane{_li}"))
        ccsem = _ctx.enter_context(nc.semaphore("ccsem"))
        block = _ctx.enter_context(nc.Block())

        r1a = nc.lookup_mloc(R1G).addr
        r2a = nc.lookup_mloc(R2G).addr
        _n = [0]

        def overlay(region, base, off, shape, dtype, nbytes):
            from concourse.bass_types import SBTensorHandle
            name = f"ovl{_n[0]}"
            _n[0] += 1
            nc._tensor(name, list(shape), dtype, type="SB")
            mloc = nc.lookup_mloc(name)
            mloc.allocated = True
            mloc.addr = base + off
            REG[name] = (region, off, off + nbytes)
            return SBTensorHandle(name, tuple(shape), dtype).ap()

        def o1(off, shape, dtype):
            n = int(np.prod(shape[1:])) * (2 if dtype == BF16 else 4)
            return overlay("r1", r1a, off, shape, dtype, n)

        def o2(off, shape, dtype):
            n = int(np.prod(shape[1:])) * (2 if dtype == BF16 else 4)
            return overlay("r2", r2a, off, shape, dtype, n)

        LN1Bs = [o1(dc * 4096, (128, 2048), BF16) for dc in range(8)]
        XF = o1(32768, (128, 16384), F32)
        YPRs = [o1(32768 + s * 4096, (128, 2048), BF16) for s in range(17)]
        X2Fs = [o1(dc * 8192, (128, 2048), F32) for dc in range(8)]
        X2Q = o1(65536, (128, 16384), BF16)
        DEQAs = [o1(kc * 6144, (128, 3072), BF16) for kc in range(8)]
        G2Ps = [o1(ic * 4096, (128, 2048), BF16) for ic in range(24)]
        YDB = o1(0, (128, 16384), BF16)
        SWT = o1(98304, (128, 768), F32)
        FDB = o1(98304, (128, 1024), F32)
        G2Rs = [o1(49152 + j * 4096, (128, 2048), BF16) for j in range(4)]
        XHB = o2(0, (128, 2052), BF16)
        ID4 = o2(8192, (128, 512), F32)
        DG1 = o2(10240, (128, 512), BF16)
        DG2 = o2(11264, (128, 512), BF16)
        WQ3 = o2(12288, (128, 1024), F32)
        WQ4 = o2(16384, (128, 1024), F32)
        WQB3 = o2(20480, (128, 1024), BF16)
        WQB4 = o2(22528, (128, 1024), BF16)
        XS = o2(24576, (128, 2048), F32)
        XSB = o2(32768, (128, 2048), BF16)
        GS = o2(36864, (128, 2048), F32)
        WST = o2(45056, (128, 1024), F32)
        WST2 = o2(49152, (128, 1024), F32)
        WSB = o2(53248, (128, 1024), BF16)
        WSB2 = o2(55296, (128, 1024), BF16)
        XSSM = o2(57344, (32, 2048), F32)
        PBUF = o2(0, (32, 4096), F32)
        PBS = o2(16384, (32, 4096), F32)
        HKT = o2(32768, (128, 4096), F32)
        PST = o2(49152, (128, 1024), F32)
        MOUTs = [o2(dc * 8192, (128, 2048), F32) for dc in range(8)]
        WOC = o2(65536, (128, 1024), F32)
        WOB1 = o2(69632, (128, 1024), BF16)
        WOB2 = o2(71680, (128, 1024), BF16)
        ST32 = o2(12288, (64, 3072), F32)
        FSA = o2(49152, (128, 1024), F32)
        FSB = o2(53248, (128, 1024), F32)
        NST = o2(57344, (128, 512), F32)
        NSTb = o2(59392, (128, 512), F32)
        SPT = o2(61440, (128, 2048), BF16)
        WSTEs = o2(65536, (128, 768), F32)
        SPT2 = o2(68608, (128, 2048), BF16)
        LDT = o2(0, (128, 2048), BF16)
        LDT2 = o2(4096, (128, 2048), BF16)
        FG1 = o2(8192, (128, 2048), F32)
        FG2 = o2(16384, (128, 2048), F32)
        FG3 = o2(24576, (128, 2048), F32)
        FG4 = o2(32768, (128, 2048), F32)
        FG5 = o2(40960, (128, 2048), F32)
        FG6 = o2(49152, (128, 2048), F32)
        NSTD = o2(57344, (128, 768), F32)
        LDG = o2(57344, (128, 2048), BF16)
        LDG2 = o2(61440, (128, 2048), BF16)
        SDW = o2(65536, (128, 768), F32)
        FDAe = o2(57344, (128, 768), F32)
        STD = o2(68608, (96, 1024), F32)
        ST4B = o2(0, (32, 1024), F32)
        ST4BI = o2(4096, (32, 1024), F32)
        DEQDs = [o2(8192 + kc * 2048, (128, 1024), BF16) for kc in range(24)]
        WDST = o2(57344, (128, 1024), F32)
        FDA = o2(61440, (128, 1024), F32)
        QF = o2(0, (128, 2048), F32)

        RM, RR, RS = 0, 1, 2

        def row(slot, lo=0, hi=2048):
            return ROWS[0:1, slot * 2048 + lo: slot * 2048 + hi]

        for dst, src in [
            (ID[:, :], idm[:, :]), (REV[:, :], revm[:, :]),
            (ONESB[:, :], onesb[:, :]), (ONESR[:, :], onesr[:, :]),
            (E4[:, :], e4i[:, :]), (SELS[:, :], selsi[:, :]),
            (BINC[:, :], binc_h[:, :]),
            (CONVW[:, :], convw_h[:, :]), (CONVB[:, :], convb_h[:, :]),
            (BOUTC[:, :], boutc_h[:, :]), (DVC[:, :], dvc_h[:, :]),
            (BSSM[:, :], bssm_h[:, :]), (AT[:, :], amat_h[:, :]),
            (BVC[:, :], bvc_h[:, :]), (CVC[:, :], cvc_h[:, :]),
        ]:
            sch.dma(dst, src)
        sch.dma(WST[:, 0:512], wssmp_h[:, :])
        sch.v(lambda e: e.tensor_copy(WSSMB[:, :], WST[:, 0:512]),
              [WST], [WSSMB])
        MAGCOL = COLS[:, 20:21]
        N128COL = COLS[:, 21:22]
        sch.v(lambda e: e.memset(MAGCOL, MAGIC), [], [COLS])
        sch.v(lambda e: e.memset(N128COL, -128.0), [], [COLS])

        def ln_rows():
            frow = FSLN[0:1, 0:512]
            rt = FSLN2[0:1, 0:512]
            for tc in range(NTC):
                mrow = row(RM, tc * 512, (tc + 1) * 512)
                rrow = row(RR, tc * 512, (tc + 1) * 512)
                sch.v(lambda e, m=mrow: e.tensor_mul(rt, m, m),
                      [ROWS], [FSLN2])
                sch.v(lambda e, r=rrow: e.tensor_sub(frow, r, rt),
                      [ROWS, FSLN2], [FSLN])
                sch.v(lambda e: e.tensor_scalar(rt, frow, EPS_LN, None,
                                                op0=ALU.add), [FSLN], [FSLN2])
                sch.s(lambda e: e.activation(frow, rt, AF.Sqrt),
                      [FSLN2], [FSLN])
                sch.v(lambda e, r=rrow: e.reciprocal(r, frow), [FSLN], [ROWS])

        def ln_stats(src_fn, n_chunks=NDC):
            inv = 1.0 / (128.0 * n_chunks)
            for tc in range(NTC):
                for ch in range(n_chunks):
                    src = src_fn(ch, tc)
                    sq = SQS if ch % 2 == 0 else SQSb
                    sch.s(lambda e, s=src, q=sq: e.activation(
                        q[:, :], s, AF.Identity), [src], [sq])
                    sch.t(lambda e, q=sq, st=(ch == 0), sp=(ch == n_chunks - 1):
                          e.matmul(PS_SM[0:1, 0:512], ONESB[:, :], q[:, :],
                                   start=st, stop=sp),
                          [ONESB, sq], [PS_SM])
                sch.s(lambda e, tc=tc: e.activation(
                    row(RM, tc * 512, (tc + 1) * 512), PS_SM[0:1, 0:512],
                    AF.Identity, bias=0.0, scale=inv), [PS_SM], [ROWS])
                for ch in range(n_chunks):
                    src = src_fn(ch, tc)
                    sq = SQS2 if ch % 2 == 0 else SQS2b
                    sch.s(lambda e, s=src, q=sq: e.activation(q[:, :], s, AF.Square),
                          [src], [sq])
                    sch.t(lambda e, q=sq, st=(ch == 0), sp=(ch == n_chunks - 1):
                          e.matmul(PS_SM[0:1, 0:512], ONESB[:, :], q[:, :],
                                   start=st, stop=sp),
                          [ONESB, sq], [PS_SM])
                sch.s(lambda e, tc=tc: e.activation(
                    row(RR, tc * 512, (tc + 1) * 512), PS_SM[0:1, 0:512],
                    AF.Identity, bias=0.0, scale=inv), [PS_SM], [ROWS])
            ln_rows()

        def bcast_mr(tc):
            sch.t(lambda e, tc=tc: e.matmul(
                PS_SM[:, 0:512], ONESR[:, :], row(RM, tc * 512, (tc + 1) * 512),
                start=True, stop=True), [ONESR, ROWS], [PS_SM])
            sch.t(lambda e, tc=tc: e.matmul(
                PS_SM2[:, 0:512], ONESR[:, :], row(RR, tc * 512, (tc + 1) * 512),
                start=True, stop=True), [ONESR, ROWS], [PS_SM2])

        for dc in range(NDC):
            sch.dma(XF[:, dc * 2048:(dc + 1) * 2048],
                    xt_h[dc * 128:(dc + 1) * 128, :])
        ln_stats(lambda dc, tc:
                 XF[:, dc * 2048 + tc * 512: dc * 2048 + (tc + 1) * 512])
        for tc in range(NTC):
            bcast_mr(tc)
            for dc in range(NDC):
                src = XF[:, dc * 2048 + tc * 512: dc * 2048 + (tc + 1) * 512]
                dst = LN1Bs[dc][:, tc * 512:(tc + 1) * 512]
                sch.v(lambda e, s=src: e.tensor_sub(FSLN[:, :], s, PS_SM[:, 0:512]),
                      [src, PS_SM], [FSLN])
                sch.v(lambda e, d=dst: e.tensor_mul(d, FSLN[:, :], PS_SM2[:, 0:512]),
                      [FSLN, PS_SM2], [LN1Bs[dc]])

        PSMMs = [PS_MM, PS_MM2]
        pidx = [0]

        def next_ps():
            pidx[0] ^= 1
            return PSMMs[pidx[0]]

        sch.dma(ID4[:, :], id4i[:, :])
        sch.v(lambda e: e.memset(XHB[:, 0:4], 0.0), [], [XHB])
        STG4 = [WST, WST2, WQ3, WQ4]
        STB4 = [WSB, WSB2, WQB3, WQB4]

        def wload(li, src_ap):
            wst, wsb = STG4[li % 4], STB4[li % 4]
            sch.dma(wst[:, :], src_ap)
            sch.s(lambda e, a=wst, b=wsb: e.activation(
                b[:, :], a[:, :], AF.Identity), [wst], [wsb])
            return wsb

        for hc in range(NHC):
            wsb = wload(2 * hc, winp_h[hc, :, :])
            dg = DG1 if hc % 2 == 0 else DG2
            for k in range(4):
                wbc = dataclasses.replace(CONVW[:, :], offset=hc * 4 + k,
                                          ap=[[64, 128], [0, 128]])
                sch.v(lambda e, k=k, w=wbc, d=dg: e.tensor_tensor(
                    d[:, k * 128:(k + 1) * 128], ID4[:, k * 128:(k + 1) * 128],
                    w, op=ALU.mult), [ID4, CONVW], [dg])
            for tc in range(NTC):
                ps = next_ps()
                for kc in range(NDC):
                    lhsT = wsb[:, kc * 128:(kc + 1) * 128]
                    rhs = LN1Bs[kc][:, tc * 512:(tc + 1) * 512]
                    sch.t(lambda e, l=lhsT, r=rhs, p=ps, st=(kc == 0),
                          sp=(kc == NDC - 1):
                          e.matmul(p[:, :], l, r, start=st, stop=sp),
                          [wsb, LN1Bs[kc]], [ps])
                sch.s(lambda e, tc=tc, hc=hc, p=ps: e.activation(
                    XHB[:, 4 + tc * 512: 4 + (tc + 1) * 512], p[:, :], AF.Identity,
                    bias=BINC[:, hc:hc + 1], scale=1.0), [ps, BINC], [XHB])
            for tc in range(NTC):
                psc = PS_SM if tc % 2 == 0 else PS_SM2
                for k in range(4):
                    rhs = XHB[:, 1 + k + tc * 512: 1 + k + tc * 512 + 512]
                    sch.t(lambda e, r=rhs, k=k, p=psc, d=dg:
                          e.matmul(p[:, 0:512], d[:, k * 128:(k + 1) * 128], r,
                                   start=(k == 0), stop=(k == 3)),
                          [dg, XHB], [psc])
                sch.s(lambda e, tc=tc, hc=hc, p=psc: e.activation(
                    XS[:, tc * 512:(tc + 1) * 512], p[:, 0:512], AF.Silu,
                    bias=CONVB[:, hc:hc + 1], scale=1.0), [psc, CONVB], [XS])
                sch.s(lambda e, tc=tc, hc=hc, p=psc: e.activation(
                    XSB[:, tc * 512:(tc + 1) * 512], p[:, 0:512], AF.Silu,
                    bias=CONVB[:, hc:hc + 1], scale=1.0), [psc, CONVB], [XSB])
            for tc in range(NTC):
                ps = PS_BIG if tc < 2 else PS_BIG2
                pcol = (tc % 2) * 512
                lhsT = WSSMB[:, hc * 32:(hc + 1) * 32]
                sch.t(lambda e, l=lhsT, tc=tc, p=ps, pc=pcol,
                      st=(hc == 0), sp=(hc == NHC - 1):
                      e.matmul(p[0:32, pc:pc + 512], l,
                               XSB[:, tc * 512:(tc + 1) * 512], start=st, stop=sp),
                      [WSSMB, XSB], [ps])
            wsb2 = wload(2 * hc + 1, winp_h[16 + hc, :, :])
            for tc in range(NTC):
                ps = next_ps()
                for kc in range(NDC):
                    lhsT = wsb2[:, kc * 128:(kc + 1) * 128]
                    rhs = LN1Bs[kc][:, tc * 512:(tc + 1) * 512]
                    sch.t(lambda e, l=lhsT, r=rhs, p=ps, st=(kc == 0),
                          sp=(kc == NDC - 1):
                          e.matmul(p[:, :], l, r, start=st, stop=sp),
                          [wsb2, LN1Bs[kc]], [ps])
                sch.s(lambda e, tc=tc, hc=hc, p=ps: e.activation(
                    GS[:, tc * 512:(tc + 1) * 512], p[:, :], AF.Silu,
                    bias=BINC[:, 16 + hc:17 + hc], scale=1.0), [ps, BINC], [GS])
            sch.v(lambda e, hc=hc: e.tensor_mul(
                YPRs[hc][:, :], XS[:, :], GS[:, :]),
                [XS, GS], [YPRs[hc]])
        sch.s(lambda e: e.activation(XSSM[:, 0:1024], PS_BIG[0:32, :],
                                     AF.Identity, bias=BSSM[:, :], scale=1.0),
              [PS_BIG, BSSM], [XSSM])
        sch.s(lambda e: e.activation(XSSM[:, 1024:2048], PS_BIG2[0:32, :],
                                     AF.Identity, bias=BSSM[:, :], scale=1.0),
              [PS_BIG2, BSSM], [XSSM])

        def ptile(d):
            return PBUF[:, (127 - d) * 32:(128 - d) * 32]

        sch.v(lambda e: e.tensor_copy(ptile(0), ID[0:32, 0:32]), [ID], [PBUF])
        sch.v(lambda e: e.tensor_copy(QTB[:, 0:32], AT[:, :]), [AT], [QTB])
        sch.v(lambda e: e.tensor_copy(CBUF[:, 0:1], CVC[:, :]), [CVC], [CBUF])
        qcur = 0
        delta = 1
        while delta <= 64:
            q = QTB[:, qcur * 32:(qcur + 1) * 32]
            width = delta * 32
            src0 = (128 - delta) * 32
            dst0 = (128 - 2 * delta) * 32
            off = 0
            while off < width:
                w = min(512, width - off)
                sch.t(lambda e, q=q, s=src0 + off, w=w: e.matmul(
                    PS_SM[0:32, 0:w], q, PBUF[:, s:s + w], start=True, stop=True),
                    [QTB, PBUF], [PS_SM])
                sch.v(lambda e, d=dst0 + off, w=w: e.tensor_copy(
                    PBUF[:, d:d + w], PS_SM[0:32, 0:w]), [PS_SM], [PBUF])
                off += w
            sch.t(lambda e, q=q, dl=delta: e.matmul(
                PS_SM[0:32, 0:dl], q, CBUF[:, 0:dl], start=True, stop=True),
                [QTB, CBUF], [PS_SM])
            sch.v(lambda e, dl=delta: e.tensor_copy(
                CBUF[:, dl:2 * dl], PS_SM[0:32, 0:dl]), [PS_SM], [CBUF])
            if delta < 64:
                sch.t(lambda e, q=q, dl=delta: e.matmul(
                    PS_SM[0:32, 32:64], ptile(dl), q, start=True, stop=True),
                    [PBUF, QTB], [PS_SM])
                sch.v(lambda e, qn=1 - qcur: e.tensor_copy(
                    QTB[:, qn * 32:(qn + 1) * 32], PS_SM[0:32, 32:64]),
                    [PS_SM], [QTB])
                qcur = 1 - qcur
            delta *= 2
        q64 = QTB[:, qcur * 32:(qcur + 1) * 32]
        sch.t(lambda e, q=q64: e.matmul(PS_SM[0:32, 0:32], q, ptile(64),
                                        start=True, stop=True),
              [QTB, PBUF], [PS_SM])
        sch.v(lambda e: e.tensor_copy(PCB[:, :], PS_SM[0:32, 0:32]),
              [PS_SM], [PCB])
        sch.t(lambda e, q=q64: e.matmul(PS_SM[0:32, 32:33], q, CBUF[:, 64:65],
                                        start=True, stop=True),
              [QTB, CBUF], [PS_SM])
        sch.v(lambda e: e.tensor_copy(CBUF[:, 128:129], PS_SM[0:32, 32:33]),
              [PS_SM], [CBUF])
        sch.v(lambda e: e.tensor_copy(
            CRB[:, :], dataclasses.replace(CBUF[:, :], offset=128,
                                           ap=[[132, 32], [-1, 128]])),
            [CBUF], [CRB], fence=True)
        sch.v(lambda e: e.tensor_scalar(
            WRB[:, 0:128], dataclasses.replace(CBUF[:, :], offset=127,
                                               ap=[[132, 32], [-1, 128]]),
            BVC[:, 0:1], None, op0=ALU.mult), [CBUF, BVC], [WRB], fence=True)
        sch.v(lambda e: e.memset(WRB[:, 128:256], 0.0), [], [WRB])
        sch.v(lambda e: e.tensor_scalar(PBS[:, :], PBUF[:, :], BVC[:, 0:1], None,
                                        op0=ALU.mult), [PBUF, BVC], [PBS],
              fence=True)
        for m in range(32):
            src = dataclasses.replace(PBS[:, :], offset=m * 4096,
                                      ap=[[4096, 1], [32, 128], [1, 32]])
            sch.dma(PST[:, m * 32:(m + 1) * 32], src)
        for m in range(32):
            src = dataclasses.replace(WRB[0:1, :], offset=m * 256,
                                      ap=[[256, 1], [1, 128], [1, 128]])
            sch.dma(HKT[:, m * 128:(m + 1) * 128], src)
        for i in range(NCH):
            sch.t(lambda e, i=i: e.transpose(
                PS_SM[:, 0:32], XSSM[:, i * 128:(i + 1) * 128], ID[0:32, 0:32]),
                [XSSM, ID], [PS_SM])
            sch.v(lambda e, i=i: e.tensor_copy(UTB[:, i * 32:(i + 1) * 32],
                                               PS_SM[:, 0:32]), [PS_SM], [UTB])
        for m in range(32):
            rhs = dataclasses.replace(UTB[:, :], offset=m, ap=[[512, 128], [32, 16]])
            sch.t(lambda e, m=m, r=rhs, st=(m == 0), sp=(m == 31): e.matmul(
                PS_SM2[0:32, 0:16], PST[:, m * 32:(m + 1) * 32], r,
                start=st, stop=sp), [PST, UTB], [PS_SM2])
        sch.v(lambda e: e.tensor_copy(GTB[:, :], PS_SM2[0:32, 0:16]),
              [PS_SM2], [GTB])
        sch.v(lambda e: e.memset(HSB[:, 0:1], 0.0), [], [HSB])
        for i in range(NCH):
            sch.t(lambda e, i=i: e.matmul(PS_SM[0:32, 0:1], PCB[:, :],
                                          HSB[:, i:i + 1], start=True, stop=False),
                  [PCB, HSB], [PS_SM])
            sch.t(lambda e, i=i: e.matmul(PS_SM[0:32, 0:1], ID[0:32, 0:32],
                                          GTB[:, i:i + 1], start=False, stop=True),
                  [ID, GTB], [PS_SM])
            sch.v(lambda e, i=i: e.tensor_copy(HSB[:, i + 1:i + 2],
                                               PS_SM[0:32, 0:1]), [PS_SM], [HSB])
        for m in range(32):
            rhs = dataclasses.replace(UTB[:, :], offset=m, ap=[[512, 128], [32, 16]])
            sch.t(lambda e, m=m, r=rhs, st=(m == 0): e.matmul(
                PS_SM2[:, 0:16], HKT[:, m * 128:(m + 1) * 128], r,
                start=st, stop=False), [HKT, UTB], [PS_SM2])
        sch.t(lambda e: e.matmul(PS_SM2[:, 0:16], CRB[:, :], HSB[:, 0:16],
                                 start=False, stop=True), [CRB, HSB], [PS_SM2])
        sch.v(lambda e: e.tensor_copy(SSB[:, :], PS_SM2[:, 0:16]),
              [PS_SM2], [SSB])
        sch.t(lambda e: e.transpose(PS_SM[0:16, 0:128], SSB[:, :], REV[:, :]),
              [SSB, REV], [PS_SM])
        sch.v(lambda e: e.tensor_copy(SR16[:, :], PS_SM[0:16, 0:128]),
              [PS_SM], [SR16])
        sch.dma(row(RS), SR16[:, :])

        def sbc(tc):
            pb = PS_BIG if tc < 2 else PS_BIG2
            return pb[:, (tc % 2) * 512:(tc % 2) * 512 + 512]

        for tc in range(NTC):
            sch.t(lambda e, tc=tc: e.matmul(
                sbc(tc), ONESR[:, :], row(RS, tc * 512, (tc + 1) * 512),
                start=True, stop=True),
                [ONESR, ROWS], [PS_BIG if tc < 2 else PS_BIG2])
        for hc in range(NHC):
            dst_slot = (hc + 16) % 17
            for tc in range(NTC):
                src = YPRs[hc][:, tc * 512:(tc + 1) * 512]
                dst = YPRs[dst_slot][:, tc * 512:(tc + 1) * 512]
                sch.v(lambda e, s=src, d=dst, hc=hc, tc=tc:
                      e.scalar_tensor_tensor(
                          d, sbc(tc), DVC[:, hc:hc + 1], s,
                          op0=ALU.add, op1=ALU.mult),
                      [PS_BIG if tc < 2 else PS_BIG2, DVC, YPRs[hc]],
                      [YPRs[dst_slot]])
        def wload2(li, src_ap):
            wsb = WOB1 if li % 2 == 0 else WOB2
            sch.dma(WOC[:, :], src_ap)
            sch.s(lambda e, b=wsb: e.activation(
                b[:, :], WOC[:, :], AF.Identity), [WOC], [wsb])
            return wsb

        for oc in range(NDC):
            wsbA = wload2(0, woutp_h[oc, :, 0:1024])
            wsbB = wload2(1, woutp_h[oc, :, 1024:2048])
            for tc in range(NTC):
                ps = next_ps()
                for hc in range(NHC):
                    slot = (hc + 16) % 17
                    wsb_ = wsbA if hc < 8 else wsbB
                    lhsT = wsb_[:, (hc % 8) * 128:(hc % 8 + 1) * 128]
                    rhs = YPRs[slot][:, tc * 512:(tc + 1) * 512]
                    sch.t(lambda e, l=lhsT, r=rhs, p=ps, st=(hc == 0),
                          sp=(hc == NHC - 1), w=wsb_:
                          e.matmul(p[:, :], l, r, start=st, stop=sp),
                          [wsb_, YPRs[slot]], [ps])
                dst = MOUTs[oc][:, tc * 512:(tc + 1) * 512]
                l1 = LN1Bs[oc][:, tc * 512:(tc + 1) * 512]
                sch.v(lambda e, d=dst, l=l1, oc=oc, p=ps: e.scalar_tensor_tensor(
                    d, p[:, :], BOUTC[:, oc:oc + 1], l, op0=ALU.add, op1=ALU.add),
                    [ps, BOUTC, LN1Bs[oc]], [MOUTs[oc]])
                sq = SQS if (oc + tc) % 2 == 0 else SQSb
                sch.s(lambda e, d=dst, q=sq: e.activation(
                    q[:, :], d, AF.Identity), [MOUTs[oc]], [sq])
                pb = PS_BIG if tc < 2 else PS_BIG2
                sch.t(lambda e, q=sq, tc=tc, pb=pb, st=(oc == 0), sp=(oc == 7):
                      e.matmul(pb[0:1, (tc % 2) * 512:(tc % 2) * 512 + 512],
                               ONESB[:, :], q[:, :], start=st, stop=sp),
                      [ONESB, sq], [pb])
        inv1k = 1.0 / 1024.0
        for tc in range(NTC):
            pb = PS_BIG if tc < 2 else PS_BIG2
            sch.s(lambda e, tc=tc, pb=pb: e.activation(
                row(RM, tc * 512, (tc + 1) * 512),
                pb[0:1, (tc % 2) * 512:(tc % 2) * 512 + 512],
                AF.Identity, bias=0.0, scale=inv1k), [pb], [ROWS])
            for ch in range(NDC):
                srcq = MOUTs[ch][:, tc * 512:(tc + 1) * 512]
                sq2 = SQS2 if ch % 2 == 0 else SQS2b
                sch.s(lambda e, s=srcq, q=sq2: e.activation(q[:, :], s,
                                                            AF.Square),
                      [MOUTs[ch]], [sq2])
                sch.t(lambda e, q=sq2, st=(ch == 0), sp=(ch == NDC - 1):
                      e.matmul(PS_SM[0:1, 0:512], ONESB[:, :], q[:, :],
                               start=st, stop=sp), [ONESB, sq2], [PS_SM])
            sch.s(lambda e, tc=tc: e.activation(
                row(RR, tc * 512, (tc + 1) * 512), PS_SM[0:1, 0:512],
                AF.Identity, bias=0.0, scale=inv1k), [PS_SM], [ROWS])
        ln_rows()
        for tc in range(NTC):
            bcast_mr(tc)
            for dc in range(NDC):
                mo = MOUTs[dc][:, tc * 512:(tc + 1) * 512]
                sch.dma(FSLN2[:, :],
                        xt_h[dc * 128:(dc + 1) * 128, tc * 512:(tc + 1) * 512])
                sch.v(lambda e, m=mo: e.tensor_sub(FSLN[:, :], m, PS_SM[:, 0:512]),
                      [MOUTs[dc], PS_SM], [FSLN])
                sch.v(lambda e: e.tensor_mul(FSLN3[:, :], FSLN[:, :],
                                             PS_SM2[:, 0:512]),
                      [FSLN, PS_SM2], [FSLN3])
                sch.v(lambda e, m=mo: e.tensor_add(m, FSLN2[:, :], FSLN3[:, :]),
                      [FSLN2, FSLN3], [MOUTs[dc]])
        for dc in range(NDC):
            sch.dma(x1_h[dc, :, :], MOUTs[dc][:, :])

        def maxred(src_ap, src_names):
            sch.v(lambda e, x=src_ap: e.tensor_reduce(
                MXC[:, 4:5], x, AX.X, ALU.max, apply_absolute_value=True),
                src_names, [MXC], fence=True)

        mstate = {}

        def maxcombine(key, pair):
            if key not in mstate:
                sch.v(lambda e, a=pair[0]: e.tensor_copy(MXC[:, a:a + 1],
                                                         MXC[:, 4:5]),
                      [MXC], [MXC], fence=True)
                mstate[key] = pair[0]
                return
            cur = mstate[key]
            nxt = pair[0] + pair[1] - cur
            sch.v(lambda e, a=cur, b=nxt: e.tensor_max(
                MXC[:, b:b + 1], MXC[:, a:a + 1], MXC[:, 4:5]), [MXC], [MXC],
                fence=True)
            mstate[key] = nxt

        CCN = [0]

        def cross_core_max(idx, srccols):
            for i, c in enumerate(srccols):
                sch.dma(MROW[:, :], MXC[:, c:c + 1])
                sch.v(lambda e, i=i: e.tensor_reduce(
                    CCSB[0:1, i:i + 1], MROW[:, :], AX.X, ALU.max),
                    [MROW], [CCSB], fence=True)
            for i in range(len(srccols), 8):
                sch.v(lambda e, i=i: e.memset(CCSB[0:1, i:i + 1], 0.0),
                      [], [CCSB])
            sch.dma(cc_in[idx].ap()[:, :], CCSB[:, :])

            def do_cc(eng, idx=idx):
                eng.collective_compute(
                    "AllReduce", ALU.max, replica_groups=[list(range(8))],
                    ins=[cc_in[idx].ap().opt()], outs=[cc_out[idx].ap().opt()],
                ).then_inc(ccsem, 1)
                eng.wait_ge(ccsem, CCN[0] + 1)
                CCN[0] += 1
                return eng.engine_nop()
            sch.g(do_cc, [cc_in[idx].ap()], [cc_out[idx].ap()])
            sch.dma(CCSB[:, :], cc_out[idx].ap()[:, :])

        def make_scale_cols(srci, colbase):
            sch.v(lambda e, i=srci: e.tensor_scalar(
                SCL[0:1, 4:5], CCSB[0:1, i:i + 1], QMAX_INV, None, op0=ALU.mult),
                [CCSB], [SCL], fence=True)
            sch.v(lambda e: e.tensor_scalar(
                SCL[0:1, 0:1], SCL[0:1, 4:5], 1e-12, None, op0=ALU.max),
                [SCL], [SCL], fence=True)
            sch.v(lambda e: e.reciprocal(SCL[0:1, 1:2], SCL[0:1, 0:1]),
                  [SCL], [SCL], fence=True)
            sch.v(lambda e: e.tensor_scalar(
                SCL[0:1, 2:3], SCL[0:1, 0:1], MAGIC, None, op0=ALU.mult),
                [SCL], [SCL], fence=True)
            sch.v(lambda e: e.tensor_scalar(
                SCL[0:1, 3:4], SCL[0:1, 0:1], -MAGIC, None, op0=ALU.mult),
                [SCL], [SCL], fence=True)
            sch.t(lambda e: e.matmul(PS_SM[:, 0:4], ONESR[:, :], SCL[0:1, 0:4],
                                     start=True, stop=True),
                  [ONESR, SCL], [PS_SM])
            sch.v(lambda e, cb=colbase: e.tensor_copy(COLS[:, cb:cb + 4],
                                                      PS_SM[:, 0:4]),
                  [PS_SM], [COLS])
            sch.v(lambda e, cb=colbase: e.tensor_copy(SCL[0:1, 4:5],
                                                      COLS[0:1, cb:cb + 1]),
                  [COLS], [SCL], fence=True)


        def scales_unit(nat_h, oc, hf):
            nst = NST if hf % 2 == 0 else NSTb
            sch.dma(nst[:, :], nat_h[oc, :, hf * 512:(hf + 1) * 512])
            src = dataclasses.replace(nst[:, :],
                                      ap=[[512, 128], [32, 16], [1, 32]])
            dst = dataclasses.replace(
                SWT[:, :], offset=oc * 32 + hf * 16,
                ap=[[768, 128], [1, 16], [0, 1]])
            sch.v(lambda e, s=src, d=dst: e.tensor_reduce(
                d, s, AX.X, ALU.max, apply_absolute_value=True),
                [nst], [SWT])

        def swt_finalize():
            sch.v(lambda e: e.tensor_scalar(FSA[:, 0:768], SWT[:, :], INV7,
                                            1e-12, op0=ALU.mult, op1=ALU.max),
                  [SWT], [FSA], fence=True)
            sch.v(lambda e: e.tensor_copy(SWT[:, :], FSA[:, 0:768]),
                  [FSA], [SWT], fence=True)
            sch.v(lambda e: e.reciprocal(FSA[:, 0:768], SWT[:, :]),
                  [SWT], [FSA], fence=True)

        def st_unit(oc):
            src = dataclasses.replace(SWT[:, :], offset=oc * 32,
                                      ap=[[768, 128], [1, 32]])
            srci = dataclasses.replace(FSA[:, :], offset=oc * 32,
                                       ap=[[1024, 128], [1, 32]])
            sch.t(lambda e, s=src: e.transpose(
                PS_SM[0:32, 0:128], s, ID[:, :]), [SWT, ID], [PS_SM])
            sch.t(lambda e, s=srci: e.transpose(
                PS_SM2[0:32, 0:128], s, ID[:, :]), [FSA, ID], [PS_SM2])
            sch.v(lambda e, oc=oc: e.tensor_copy(
                ST32[0:32, oc * 128:(oc + 1) * 128], PS_SM[0:32, 0:128]),
                [PS_SM], [ST32])
            sch.v(lambda e, oc=oc: e.tensor_copy(
                ST32[32:64, oc * 128:(oc + 1) * 128], PS_SM2[0:32, 0:128]),
                [PS_SM2], [ST32])

        def dequant_unit(wt_h, kc, sub):
            sch.dma(WSTEs[:, :], wt_h[kc, :, sub * 768:(sub + 1) * 768])
            for lo2, w2 in ((0, 512), (512, 256)):
                lo = sub * 768 + lo2
                sch.t(lambda e, lo=lo, lo2=lo2, w2=w2, kc=kc: e.matmul(
                    PS_BIG[:, lo2:lo2 + w2],
                    SELS[32:64, kc * 128:(kc + 1) * 128],
                    ST32[32:64, lo:lo + w2], start=True, stop=True),
                    [SELS, ST32], [PS_BIG])
                sch.t(lambda e, lo=lo, lo2=lo2, w2=w2, kc=kc: e.matmul(
                    PS_BIG2[:, lo2:lo2 + w2],
                    SELS[0:32, kc * 128:(kc + 1) * 128],
                    ST32[0:32, lo:lo + w2], start=True, stop=True),
                    [SELS, ST32], [PS_BIG2])
            sch.v(lambda e: e.tensor_mul(
                FSA[:, 0:768], WSTEs[:, :],
                PS_BIG[:, 0:768]), [WSTEs, PS_BIG], [FSA])
            sch.v(lambda e: e.tensor_scalar(
                FSB[:, 0:768], FSA[:, 0:768], MAGIC, -MAGIC,
                op0=ALU.add, op1=ALU.add), [FSA], [FSB])
            sch.v(lambda e, kc=kc, sub=sub: e.tensor_mul(
                DEQAs[kc][:, sub * 768:(sub + 1) * 768],
                FSB[:, 0:768], PS_BIG2[:, 0:768]),
                [FSB, PS_BIG2], [DEQAs[kc]])

        def ffn_prep_units(wn_h, wt_h):
            units = []
            for oc in range(24):
                for hf in range(2):
                    units.append(lambda oc=oc, hf=hf:
                                 scales_unit(wn_h, oc, hf))
            units.append(swt_finalize)
            for oc in range(24):
                units.append(lambda oc=oc: st_unit(oc))
            for kc in range(NDC):
                for sub in range(4):
                    units.append(lambda kc=kc, sub=sub:
                                 dequant_unit(wt_h, kc, sub))
            return units


        ln_stats(lambda dc, tc: MOUTs[dc][:, tc * 512:(tc + 1) * 512])
        scales_q = [lambda oc=oc, hf=hf: scales_unit(wg_n_h, oc, hf)
                    for oc in range(24) for hf in range(2)] + [swt_finalize]
        st_q = [lambda oc=oc: st_unit(oc) for oc in range(24)]
        deq_q = [(kc, (6144 * (kc + 1) - 1) // 8192,
                  lambda kc=kc, sub=sub: dequant_unit(wg_t_h, kc, sub))
                 for kc in range(NDC) for sub in range(4)]

        for it, dc in enumerate(range(NDC - 1, -1, -1)):
            for tc in range(NTC):
                bcast_mr(tc)
                mo = MOUTs[dc][:, tc * 512:(tc + 1) * 512]
                x2 = X2Fs[dc][:, tc * 512:(tc + 1) * 512]
                sch.v(lambda e, m=mo: e.tensor_sub(FSLN[:, :], m, PS_SM[:, 0:512]),
                      [MOUTs[dc], PS_SM], [FSLN])
                sch.v(lambda e, x=x2: e.tensor_mul(x, FSLN[:, :], PS_SM2[:, 0:512]),
                      [FSLN, PS_SM2], [X2Fs[dc]])
            maxred(X2Fs[dc][:, :], [X2Fs[dc]])
            maxcombine("x2", (0, 1))
            if it >= 1:
                for _ in range(8):
                    if scales_q:
                        scales_q.pop(0)()

        while scales_q:
            scales_q.pop(0)()
        cross_core_max(0, [mstate["x2"]])
        make_scale_cols(0, 0)
        for dc in range(NDC):
            x2 = X2Fs[dc][:, :]
            xq = X2Q[:, dc * 2048:(dc + 1) * 2048]
            sch.s(lambda e, x=x2: e.activation(
                QF[:, :], x, AF.Identity, bias=MAGCOL, scale=COLS[:, 1:2]),
                [X2Fs[dc], COLS], [QF])
            sch.v(lambda e, q=xq: e.tensor_scalar(
                q, QF[:, :], COLS[:, 0:1], COLS[:, 2:3],
                op0=ALU.mult, op1=ALU.subtract), [QF, COLS], [X2Q])
            while st_q:
                st_q.pop(0)()
            while deq_q and deq_q[0][1] <= dc:
                deq_q.pop(0)[2]()
        while st_q:
            st_q.pop(0)()
        while deq_q:
            deq_q.pop(0)[2]()

        def ffn_weight(wn_h, wt_h, sp_h, mkey, pair, after_oc=None,
                       prep_done=False):
            if not prep_done:
                for u in ffn_prep_units(wn_h, wt_h):
                    u()
            for oc in range(24):
                spt = SPT if oc % 2 == 0 else SPT2
                for tc in range(NTC):
                    ps = next_ps()
                    for j in range(NDC):
                        kc = (oc + j) % NDC
                        lhsT = DEQAs[kc][:, oc * 128:(oc + 1) * 128]
                        rhs = X2Q[:, kc * 2048 + tc * 512:
                                  kc * 2048 + (tc + 1) * 512]
                        sch.t(lambda e, l=lhsT, r=rhs, p=ps, st=(j == 0),
                              sp=(j == NDC - 1), kc=kc:
                              e.matmul(p[:, :], l, r, start=st, stop=sp),
                              [DEQAs[kc], X2Q], [ps])
                    sch.s(lambda e, tc=tc, p=ps, t=spt: e.activation(
                        t[:, tc * 512:(tc + 1) * 512], p[:, :], AF.Identity),
                        [ps], [spt])
                maxred(spt[:, :], [spt])
                maxcombine(mkey, pair)
                sch.dma(sp_h[oc, :, :], spt[:, :])
                if after_oc is not None:
                    after_oc(oc)

        ffn_weight(wg_n_h, wg_t_h, spgt_h, "gt", (2, 3), prep_done=True)
        cross_core_max(1, [mstate["gt"]])
        make_scale_cols(0, 4)

        def wd_prep_one(oc, q):
            nst = NSTD
            sch.dma(nst[:, :], wd_n_h[oc, :, q * 768:(q + 1) * 768])
            src = dataclasses.replace(nst[:, :],
                                      ap=[[768, 128], [32, 24], [1, 32]])
            dst = dataclasses.replace(SDW[:, :], offset=oc * 96 + q * 24,
                                      ap=[[768, 128], [1, 24], [0, 1]])
            sch.v(lambda e, s=src, d=dst: e.tensor_reduce(
                d, s, AX.X, ALU.max, apply_absolute_value=True),
                [nst], [SDW])

        wd_prep = [(oc, q) for oc in range(8) for q in range(4)]

        def wd_prep_finish():
            sch.v(lambda e: e.tensor_scalar(FDAe[:, :], SDW[:, :], INV7,
                                            1e-12, op0=ALU.mult, op1=ALU.max),
                  [SDW], [FDAe], fence=True)
            sch.v(lambda e: e.tensor_copy(SDW[:, :], FDAe[:, :]),
                  [FDAe], [SDW], fence=True)
            for oc in range(8):
                src = SDW[:, oc * 96:(oc + 1) * 96]
                sch.t(lambda e, s=src: e.transpose(PS_SM[0:96, 0:128], s,
                                                   ID[:, :]), [SDW, ID], [PS_SM])
                sch.v(lambda e, oc=oc: e.tensor_copy(
                    STD[:, oc * 128:(oc + 1) * 128], PS_SM[0:96, 0:128]),
                    [PS_SM], [STD])

        def g_chain(ic):
            ldt = LDT if ic % 2 == 0 else LDT2
            fga = FG1 if ic % 2 == 0 else FG3
            fgb = FG2 if ic % 2 == 0 else FG4
            fgc = FG5 if ic % 2 == 0 else FG6
            g2r = G2Rs[ic % 4]
            g2 = g2r[:, :]
            sch.dma(ldt[:, :], spgt_h[ic, :, :])
            sch.s(lambda e, l=ldt, f=fga: e.activation(
                f[:, :], l[:, :], AF.Identity, bias=MAGCOL,
                scale=COLS[:, 5:6]), [ldt, COLS], [fga])
            sch.s(lambda e, f=fga, g=fgb: e.activation(
                g[:, :], f[:, :], AF.Identity, bias=COLS[:, 7:8],
                scale=COLS[:, 4:5]), [fga, COLS], [fgb], fence=True)
            sch.s(lambda e, g=fgb, f=fgc: e.activation(
                f[:, :], g[:, :], AF.Sigmoid), [fgb], [fgc], fence=True)
            sch.v(lambda e, c=fgc, f=fga: e.tensor_scalar(
                f[:, :], c[:, :], 128.0, CMAX16, op0=ALU.add, op1=ALU.min),
                [fgc], [fga])
            sch.v(lambda e, f=fga, g=fgb, d=g2: e.scalar_tensor_tensor(
                d, f[:, :], N128COL, g[:, :], op0=ALU.add, op1=ALU.mult),
                [fga, fgb, COLS], [g2r])
            maxred(g2, [g2r])
            maxcombine("g2", (2, 3))
            sch.dma(spg2_h[ic, :, :], g2)

        ffn_weight(wu_n_h, wu_t_h, spup_h, "up", (5, 6), after_oc=g_chain)
        while wd_prep:
            wd_prep_one(*wd_prep.pop(0))
        wd_prep_finish()
        cross_core_max(4, [mstate["up"], mstate["g2"]])
        make_scale_cols(0, 8)
        make_scale_cols(1, 12)

        for ic in range(NIC):
            ldt = LDT if ic % 2 == 0 else LDT2
            fga = FG1 if ic % 2 == 0 else FG3
            fgb = FG2 if ic % 2 == 0 else FG4
            fgc = FG5 if ic % 2 == 0 else FG6
            ldg = LDG if ic % 2 == 0 else LDG2
            g2 = G2Ps[ic][:, :]
            sch.dma(ldt[:, :], spup_h[ic, :, :])
            sch.dma(ldg[:, :], spg2_h[ic, :, :])
            sch.s(lambda e, l=ldt, f=fga: e.activation(
                f[:, :], l[:, :], AF.Identity, bias=MAGCOL,
                scale=COLS[:, 9:10]), [ldt, COLS], [fga])
            sch.s(lambda e, f=fga, g=fgb: e.activation(
                g[:, :], f[:, :], AF.Identity, bias=COLS[:, 11:12],
                scale=COLS[:, 8:9]), [fga, COLS], [fgb], fence=True)
            sch.s(lambda e, g=ldg, f=fgc: e.activation(
                f[:, :], g[:, :], AF.Identity, bias=MAGCOL,
                scale=COLS[:, 13:14]), [ldg, COLS], [fgc])
            sch.v(lambda e, c=fgc, f=fga: e.tensor_scalar(
                f[:, :], c[:, :], COLS[:, 12:13], COLS[:, 14:15],
                op0=ALU.mult, op1=ALU.subtract), [fgc, COLS], [fga])
            sch.v(lambda e, a=fgb, b=fga, d=g2: e.tensor_mul(d, a[:, :], b[:, :]),
                  [fgb, fga], [G2Ps[ic]])
            maxred(g2, [G2Ps[ic]])
            maxcombine("pr", (2, 3))
        cross_core_max(3, [mstate["pr"]])
        make_scale_cols(0, 16)
        for ic in range(NIC):
            pr = G2Ps[ic][:, :]
            fg = FG1 if ic % 2 == 0 else FG3
            sch.s(lambda e, p=pr, f=fg: e.activation(
                f[:, :], p, AF.Identity, bias=MAGCOL, scale=COLS[:, 17:18]),
                [G2Ps[ic], COLS], [fg])
            sch.v(lambda e, p=pr, f=fg: e.tensor_scalar(
                p, f[:, :], COLS[:, 16:17], COLS[:, 18:19],
                op0=ALU.mult, op1=ALU.subtract), [fg, COLS], [G2Ps[ic]])

        for kc in range(24):
            if kc % 8 == 0:
                b = kc // 8
                sch.dma(ST4B[:, :], STD[32 * b:32 * b + 32, :])
                sch.v(lambda e: e.reciprocal(ST4BI[:, :], ST4B[:, :]),
                      [ST4B], [ST4BI])
            sch.dma(WDST[:, :], wd_t_h[kc, :, :])
            for half in range(2):
                j = kc % 8
                sch.t(lambda e, h=half, j=j: e.matmul(
                    PS_SM[:, 0:512], SELS[0:32, j * 128:(j + 1) * 128],
                    ST4BI[0:32, h * 512:(h + 1) * 512], start=True, stop=True),
                    [SELS, ST4BI], [PS_SM])
                sch.t(lambda e, h=half, j=j: e.matmul(
                    PS_SM2[:, 0:512], SELS[0:32, j * 128:(j + 1) * 128],
                    ST4B[0:32, h * 512:(h + 1) * 512], start=True, stop=True),
                    [SELS, ST4B], [PS_SM2])
                sch.v(lambda e, h=half: e.tensor_mul(
                    FDA[:, 0:512], WDST[:, h * 512:(h + 1) * 512],
                    PS_SM[:, 0:512]), [WDST, PS_SM], [FDA])
                sch.v(lambda e: e.tensor_scalar(
                    FDB[:, 0:512], FDA[:, 0:512], MAGIC, -MAGIC,
                    op0=ALU.add, op1=ALU.add), [FDA], [FDB])
                sch.v(lambda e, kc=kc, h=half: e.tensor_mul(
                    DEQDs[kc][:, h * 512:(h + 1) * 512], FDB[:, 0:512],
                    PS_SM2[:, 0:512]), [FDB, PS_SM2], [DEQDs[kc]])
        for oc in range(8):
            for tc in range(NTC):
                ps = next_ps()
                for kc in range(24):
                    lhsT = DEQDs[kc][:, oc * 128:(oc + 1) * 128]
                    rhs = G2Ps[kc][:, tc * 512:(tc + 1) * 512]
                    sch.t(lambda e, l=lhsT, r=rhs, p=ps, st=(kc == 0),
                          sp=(kc == 23):
                          e.matmul(p[:, :], l, r, start=st, stop=sp),
                          [DEQDs[kc], G2Ps[kc]], [ps])
                sq = [SQS, SQSb, SQS2, SQS2b][tc]
                sch.s(lambda e, p=ps, q=sq: e.activation(
                    q[:, :], p[:, :], AF.Identity), [ps], [sq])
                sch.dma(yd_h[oc, :, tc * 512:(tc + 1) * 512], sq[:, :])
                pb = PS_BIG if tc < 2 else PS_BIG2
                sch.t(lambda e, q=sq, tc=tc, pb=pb, st=(oc == 0), sp=(oc == 7):
                      e.matmul(pb[0:1, (tc % 2) * 512:(tc % 2) * 512 + 512],
                               ONESB[:, :], q[:, :], start=st, stop=sp),
                      [ONESB, sq], [pb])
        for oc in range(8):
            sch.dma(YDB[:, oc * 2048:(oc + 1) * 2048], yd_h[oc, :, :])
        inv = 1.0 / 1024.0
        for tc in range(NTC):
            pb = PS_BIG if tc < 2 else PS_BIG2
            sch.s(lambda e, tc=tc, pb=pb: e.activation(
                row(RM, tc * 512, (tc + 1) * 512),
                pb[0:1, (tc % 2) * 512:(tc % 2) * 512 + 512],
                AF.Identity, bias=0.0, scale=inv), [pb], [ROWS])
            for oc in range(8):
                ybt = YDB[:, oc * 2048 + tc * 512: oc * 2048 + (tc + 1) * 512]
                sq2 = SQS2 if oc % 2 == 0 else SQS2b
                sch.s(lambda e, q=ybt, q2=sq2: e.activation(q2[:, :], q,
                                                            AF.Square),
                      [YDB], [sq2])
                sch.t(lambda e, q2=sq2, st=(oc == 0), sp=(oc == 7): e.matmul(
                    PS_SM[0:1, 0:512], ONESB[:, :], q2[:, :], start=st, stop=sp),
                    [ONESB, sq2], [PS_SM])
            sch.s(lambda e, tc=tc: e.activation(
                row(RR, tc * 512, (tc + 1) * 512), PS_SM[0:1, 0:512],
                AF.Identity, bias=0.0, scale=inv), [PS_SM], [ROWS])
        ln_rows()
        for tc in range(NTC):
            bcast_mr(tc)
            for dc in range(NDC):
                ybt = YDB[:, dc * 2048 + tc * 512: dc * 2048 + (tc + 1) * 512]
                sch.dma(FSLN2[:, :], x1_h[dc, :, tc * 512:(tc + 1) * 512])
                sch.v(lambda e, q=ybt: e.tensor_sub(FSLN[:, :], q,
                                                    PS_SM[:, 0:512]),
                      [YDB, PS_SM], [FSLN])
                sch.v(lambda e: e.tensor_mul(FSLN3[:, :], FSLN[:, :],
                                             PS_SM2[:, 0:512]),
                      [FSLN, PS_SM2], [FSLN3])
                sch.v(lambda e: e.tensor_add(FSLN[:, :], FSLN2[:, :],
                                             FSLN3[:, :]),
                      [FSLN2, FSLN3], [FSLN])
                sch.dma(out_h[dc * 128:(dc + 1) * 128, tc * 512:(tc + 1) * 512],
                        FSLN[:, :])

        waits, totals, lanes, throttle, lane_tot, _ = sch.plan(NLANES)
        items = sch.items

        def run(eng_obj, enames):
            for i, (en, fn, ins, outs, fence) in enumerate(items):
                if en not in enames:
                    continue
                if throttle[i] is not None:
                    eng_obj.wait_ge(sems[throttle[i][0]], throttle[i][1])
                for (dkey, dunits) in waits[i]:
                    eng_obj.wait_ge(sems[dkey], dunits)
                r = fn(eng_obj)
                if lanes[i] is not None:
                    r.then_inc(sems[lanes[i]], 16)
                else:
                    r.then_inc(sems[en], 1)

        @block.tensor
        def _(eng):
            run(eng, ("tensor",))

        @block.vector
        def _(eng):
            run(eng, ("vector",))

        @block.scalar
        def _(eng):
            run(eng, ("scalar",))

        @block.gpsimd
        def _(eng):
            run(eng, ("gpsimd", "gdma"))

        @block.sync
        def _(eng):
            run(eng, ("sync",))
            for e2 in ("tensor", "vector", "scalar", "gpsimd"):
                eng.wait_ge(sems[e2], totals[e2])
            for ln, tot in lane_tot.items():
                eng.wait_ge(sems[ln], tot)

    return nc


_CACHE = {}


def kernel(**inputs):
    f32 = np.float32

    def a(x):
        return np.ascontiguousarray(np.asarray(x, dtype=f32))

    x = a(inputs["L0_attn_x"])
    W_in = a(inputs["L0_attn_W_in"])
    b_in = a(inputs["L0_attn_b_in"])
    conv_w = a(inputs["L0_attn_conv_w"])[:, 0, :]
    conv_b = a(inputs["L0_attn_conv_b"])
    W_ssm = a(inputs["L0_attn_W_ssm"])
    b_ssm = a(inputs["L0_attn_b_ssm"])
    A = a(inputs["L0_attn_A"])
    Bv = a(inputs["L0_attn_Bv"])
    Cv = a(inputs["L0_attn_Cv"])
    Dv = a(inputs["L0_attn_Dv"])
    W_out = a(inputs["L0_attn_W_out"])
    b_out = a(inputs["L0_attn_b_out"])
    Wg = a(inputs["L0_ffn_w_gate"])
    Wu = a(inputs["L0_ffn_w_up"])
    Wd = a(inputs["L0_ffn_w_down"])

    shared = {
        "winp": np.ascontiguousarray(
            W_in.reshape(32, 128, 8, 128).transpose(0, 3, 2, 1)
        ).reshape(32, 128, 1024),
        "woutp": np.ascontiguousarray(
            W_out.reshape(8, 128, 16, 128).transpose(0, 3, 2, 1)
        ).reshape(8, 128, 2048),
        "wssmp": np.ascontiguousarray(
            W_ssm.reshape(32, 16, 128).transpose(2, 1, 0)).reshape(128, 512),
        "wu_nat": Wu.reshape(24, 128, 1024).copy(),
        "wg_nat": Wg.reshape(24, 128, 1024).copy(),
        "wd_nat": Wd.reshape(8, 128, 3072).copy(),
        "wu_t": np.ascontiguousarray(
            Wu.reshape(24, 128, 8, 128).transpose(2, 3, 0, 1)
        ).reshape(8, 128, 3072),
        "wg_t": np.ascontiguousarray(
            Wg.reshape(24, 128, 8, 128).transpose(2, 3, 0, 1)
        ).reshape(8, 128, 3072),
        "wd_t": np.ascontiguousarray(
            Wd.reshape(8, 128, 24, 128).transpose(2, 3, 0, 1)
        ).reshape(24, 128, 1024),
        "binc": np.ascontiguousarray(b_in.reshape(32, 128).T),
        "convw": np.ascontiguousarray(
            conv_w.reshape(16, 128, 4).transpose(1, 0, 2)).reshape(128, 64),
        "convb": np.ascontiguousarray(conv_b.reshape(16, 128).T),
        "boutc": np.ascontiguousarray(b_out.reshape(8, 128).T),
        "dvc": np.ascontiguousarray(Dv.reshape(16, 128).T),
        "bssm": b_ssm.reshape(32, 1).copy(),
        "amat": A.copy(),
        "bvc": Bv.reshape(32, 1).copy(),
        "cvc": Cv.reshape(32, 1).copy(),
    }

    if "nc" not in _CACHE:
        _CACHE["nc"] = build_program()
    nc = _CACHE["nc"]

    in_maps = []
    for b in range(8):
        m = dict(shared)
        m["xt"] = np.ascontiguousarray(x[b].T)
        in_maps.append(m)
    res = run_bass_kernel_spmd(nc, in_maps, core_ids=list(range(8)))
    out = np.stack([np.asarray(res.results[b]["out_t"]).T for b in range(8)], axis=0)
    return np.ascontiguousarray(out)


# version 31
# speedup vs baseline: 1.0061x; 1.0061x over previous
import dataclasses
import numpy as np
import ml_dtypes

import concourse.bass as bass
import concourse.mybir as mybir
from concourse.bass_utils import run_bass_kernel_spmd

F32 = mybir.dt.float32
BF16 = mybir.dt.bfloat16
U8 = mybir.dt.uint8
AF = mybir.ActivationFunctionType
ALU = mybir.AluOpType
AX = mybir.AxisListType
BFNP = ml_dtypes.bfloat16

MAGIC = float(np.float32(12582912.0))
INV7 = float(np.float32(1.0) / np.float32(7.0))
QMAX_INV = float(np.float32(1.0) / np.float32(32767.0))
EPS_LN = 1e-5
CMAX16 = float(np.float32(128.0) + np.float32(65535.0) / np.float32(65536.0))

NDC, NHC, NIC, NTC = 8, 16, 24, 4
NCH = 16

ENGS = ("tensor", "vector", "scalar", "gpsimd", "sync")


class Sched:

    def __init__(self, regmap):
        self.items = []
        self.reg = regmap

    def _names(self, aps):
        out = []
        for a in aps:
            if a is None:
                continue
            if isinstance(a, str):
                out.append(a)
            elif hasattr(a, "tensor"):
                out.append(a.tensor.name)
            else:
                out.append(a.name)
        return out

    def op(self, eng, fn, ins, outs, fence=False):
        self.items.append((eng, fn, self._names(ins), self._names(outs), fence))

    def t(self, fn, ins, outs, fence=False):
        self.op("tensor", fn, ins, outs, fence)

    def v(self, fn, ins, outs, fence=False):
        self.op("vector", fn, ins, outs, fence)

    def s(self, fn, ins, outs, fence=False):
        self.op("scalar", fn, ins, outs, fence)

    def g(self, fn, ins, outs, fence=False):
        self.op("gpsimd", fn, ins, outs, fence)

    def dma(self, out, in_, **kw):
        self.op("sync",
                lambda e, o=out, i=in_, k=kw: e.dma_start(out=o, in_=i, **k),
                [in_], [out])

    def gdma(self, out, in_, **kw):
        self.op("gdma",
                lambda e, o=out, i=in_, k=kw: e.dma_start(out=o, in_=i, **k),
                [in_], [out])

    def fence_all(self):
        pass

    SER_LO = -1
    SER_HI = -1

    def plan(self, nlanes=6):
        cnt = {e: 0 for e in ENGS}
        dma_idx = [0]
        gdma_idx = [0]
        nglanes = 2
        writer = {}
        readers = {}
        waited = {}
        reg = self.reg

        def rng(n):
            return reg.get(n, (n, 0, 1))

        def overlaps(n1, n2):
            r1, r2 = rng(n1), rng(n2)
            return r1[0] == r2[0] and r1[1] < r2[2] and r2[1] < r1[2]

        DMAE = ("sync", "gdma")
        state_names = []
        waits = []
        lanes = []
        throttle = []
        prev_key = None
        for idx_item, (eng, fn, ins, outs, fence) in enumerate(self.items):
            deps = {}

            def add_dep(dkey, dunits, deng):
                if deng == eng and eng not in DMAE:
                    return
                deps[dkey] = max(deps.get(dkey, 0), dunits)

            for n in ins:
                for sn in state_names:
                    if overlaps(n, sn) and sn in writer:
                        k, u, de = writer[sn]
                        add_dep(k, u, de)
            for n in outs:
                for sn in state_names:
                    if not overlaps(n, sn):
                        continue
                    if sn in writer:
                        k, u, de = writer[sn]
                        add_dep(k, u, de)
                    for k2, (u2, de2) in readers.get(sn, {}).items():
                        add_dep(k2, u2, de2)
            if (self.SER_LO <= idx_item < self.SER_HI and prev_key is not None
                    and prev_key[0] != eng):
                deps[prev_key[0]] = max(deps.get(prev_key[0], 0), prev_key[1])
            if fence and eng not in DMAE and cnt[eng] > 0:
                deps[eng] = max(deps.get(eng, 0), cnt[eng])
            wl = []
            for dkey, dunits in deps.items():
                if waited.get((eng, dkey), 0) >= dunits:
                    continue
                waited[(eng, dkey)] = dunits
                wl.append((dkey, dunits))
            waits.append(wl)
            if eng in DMAE:
                if eng == "sync":
                    i = dma_idx[0]
                    lane = f"lane{i % nlanes}"
                    k = i // nlanes
                    dma_idx[0] += 1
                else:
                    i = gdma_idx[0]
                    lane = f"glane{i % nglanes}"
                    k = i // nglanes
                    gdma_idx[0] += 1
                throttle.append((lane, 16 * k) if k > 0 else None)
                lanes.append(lane)
                mykey, after = lane, 16 * (k + 1)
            else:
                cnt[eng] += 1
                lanes.append(None)
                throttle.append(None)
                mykey, after = eng, cnt[eng]
            for n in outs:
                if n not in writer and n not in readers:
                    state_names.append(n)
                writer[n] = (mykey, after, eng)
                readers[n] = {}
            for n in ins:
                if n not in writer and n not in readers:
                    state_names.append(n)
                readers.setdefault(n, {})[mykey] = (after, eng)
            prev_key = (mykey, after)
        lane_tot = {}
        for i, ln in enumerate(lanes):
            if ln is not None:
                lane_tot[ln] = lane_tot.get(ln, 0) + 16
        return waits, cnt, lanes, throttle, lane_tot, dma_idx[0] and nlanes or 0


def build_program():
    nc = bass.Bass("TRN2", target_bir_lowering=False,
                   dynamic_dma_scratch_size=4096)

    xt_h = nc.dram_tensor("xt", [1024, 2048], F32, kind="ExternalInput").ap()
    winp_h = nc.dram_tensor("winp", [32, 128, 1024], F32, kind="ExternalInput").ap()
    woutp_h = nc.dram_tensor("woutp", [8, 128, 2048], F32, kind="ExternalInput").ap()
    wssmp_h = nc.dram_tensor("wssmp", [128, 512], F32, kind="ExternalInput").ap()
    wu_n_h = nc.dram_tensor("wu_nat", [24, 128, 1024], F32, kind="ExternalInput").ap()
    wg_n_h = nc.dram_tensor("wg_nat", [24, 128, 1024], F32, kind="ExternalInput").ap()
    wd_n_h = nc.dram_tensor("wd_nat", [8, 128, 3072], F32, kind="ExternalInput").ap()
    wu_t_h = nc.dram_tensor("wu_t", [8, 128, 3072], F32, kind="ExternalInput").ap()
    wg_t_h = nc.dram_tensor("wg_t", [8, 128, 3072], F32, kind="ExternalInput").ap()
    wd_t_h = nc.dram_tensor("wd_t", [24, 128, 1024], F32, kind="ExternalInput").ap()
    binc_h = nc.dram_tensor("binc", [128, 32], F32, kind="ExternalInput").ap()
    convw_h = nc.dram_tensor("convw", [128, 64], F32, kind="ExternalInput").ap()
    convb_h = nc.dram_tensor("convb", [128, 16], F32, kind="ExternalInput").ap()
    boutc_h = nc.dram_tensor("boutc", [128, 8], F32, kind="ExternalInput").ap()
    dvc_h = nc.dram_tensor("dvc", [128, 16], F32, kind="ExternalInput").ap()
    bssm_h = nc.dram_tensor("bssm", [32, 1], F32, kind="ExternalInput").ap()
    amat_h = nc.dram_tensor("amat", [32, 32], F32, kind="ExternalInput").ap()
    bvc_h = nc.dram_tensor("bvc", [32, 1], F32, kind="ExternalInput").ap()
    cvc_h = nc.dram_tensor("cvc", [32, 1], F32, kind="ExternalInput").ap()

    out_h = nc.dram_tensor("out_t", [1024, 2048], F32, kind="ExternalOutput").ap()

    spup_h = nc.dram_tensor("sp_up", [24, 128, 2048], BF16, kind="Internal").ap()
    spgt_h = nc.dram_tensor("sp_gt", [24, 128, 2048], BF16, kind="Internal").ap()
    x1_h = nc.dram_tensor("sp_x1", [8, 128, 2048], F32, kind="Internal").ap()
    spg2_h = nc.dram_tensor("sp_g2", [24, 128, 2048], BF16, kind="Internal").ap()
    yd_h = nc.dram_tensor("sp_yd", [8, 128, 2048], BF16, kind="Internal").ap()
    cc_in = [nc.dram_tensor(f"cc_in{i}", [1, 8], F32, kind="Internal")
             for i in range(5)]
    cc_out = [nc.dram_tensor(f"cc_out{i}", [1, 8], F32, kind="Internal")
              for i in range(5)]

    idm = nc.inline_tensor(np.eye(128, dtype=np.float32), name="idm").ap()
    revm = nc.inline_tensor(np.eye(128, dtype=np.float32)[:, ::-1].copy(),
                            name="revm").ap()
    onesb = nc.inline_tensor(np.ones((128, 1), dtype=BFNP), name="onesb").ap()
    onesr = nc.inline_tensor(np.ones((1, 128), dtype=np.float32), name="onesr").ap()
    e4np = np.zeros((4, 128), np.float32)
    for bb in range(4):
        e4np[bb, bb * 32:(bb + 1) * 32] = 1.0
    e4i = nc.inline_tensor(e4np, name="e4i").ap()
    id4i = nc.inline_tensor(np.tile(np.eye(128, dtype=np.float32), (1, 4)),
                            name="id4i").ap()
    selnp = np.zeros((64, 1024), np.float32)
    for r in range(64):
        for j in range(8):
            p0 = (r % 32 - 4 * j) * 32
            if 0 <= p0 < 128:
                selnp[r, j * 128 + p0: j * 128 + p0 + 32] = 1.0
    selsi = nc.inline_tensor(selnp, name="selsi").ap()

    REG = {}
    sch = Sched(REG)

    from contextlib import ExitStack
    with ExitStack() as _ctx:
        def sb(name, shape, dtype):
            return _ctx.enter_context(nc.sbuf_tensor(name, shape, dtype))

        ID = sb("ID", [128, 128], F32)
        REV = sb("REV", [128, 128], F32)
        ONESB = sb("ONESB", [128, 1], BF16)
        ONESR = sb("ONESR", [1, 128], F32)
        E4 = sb("E4", [4, 128], F32)
        SELS = sb("SELS", [64, 1024], F32)
        BINC = sb("BINC", [128, 32], F32)
        CONVW = sb("CONVW", [128, 64], F32)
        CONVB = sb("CONVB", [128, 16], F32)
        BOUTC = sb("BOUTC", [128, 8], F32)
        DVC = sb("DVC", [128, 16], F32)
        BSSM = sb("BSSM", [32, 1], F32)
        AT = sb("AT", [32, 32], F32)
        BVC = sb("BVC", [32, 1], F32)
        CVC = sb("CVC", [32, 1], F32)
        WSSMB = sb("WSSMB", [128, 512], BF16)
        ROWS = sb("ROWS", [1, 6144], F32)
        MXC = sb("MXC", [128, 8], F32)
        COLS = sb("COLS", [128, 24], F32)
        SCL = sb("SCL", [1, 16], F32)
        CCSB = sb("CCSB", [1, 8], F32)
        MROW = sb("MROW", [1, 128], F32)
        UTB = sb("UTB", [128, 512], F32)
        HSB = sb("HSB", [32, 17], F32)
        GTB = sb("GTB", [32, 16], F32)
        PCB = sb("PCB", [32, 32], F32)
        QTB = sb("QTB", [32, 64], F32)
        SSB = sb("SSB", [128, 16], F32)
        SR16 = sb("SR16", [16, 128], F32)
        CBUF = sb("CBUF", [32, 132], F32)
        CRB = sb("CRB", [32, 128], F32)
        WRB = sb("WRB", [32, 256], F32)
        FSLN = sb("FSLN", [128, 512], F32)
        FSLN2 = sb("FSLN2", [128, 512], F32)
        FSLN3 = sb("FSLN3", [128, 512], F32)
        SQS = sb("SQS", [128, 512], BF16)
        SQSb = sb("SQSb", [128, 512], BF16)
        SQS2 = sb("SQS2", [128, 512], BF16)
        SQS2b = sb("SQS2b", [128, 512], BF16)
        R1G = sb("R1G", [128, 102400], U8)
        R2G = sb("R2G", [128, 73728], U8)
        PS_BIG = _ctx.enter_context(nc.psum_tensor("PS_BIG", [128, 1024], F32))
        PS_BIG2 = _ctx.enter_context(nc.psum_tensor("PS_BIG2", [128, 1024], F32))
        PS_MM = _ctx.enter_context(nc.psum_tensor("PS_MM", [128, 512], F32))
        PS_MM2 = _ctx.enter_context(nc.psum_tensor("PS_MM2", [128, 512], F32))
        PS_SM = _ctx.enter_context(nc.psum_tensor("PS_SM", [128, 512], F32))
        PS_SM2 = _ctx.enter_context(nc.psum_tensor("PS_SM2", [128, 512], F32))
        sems = {e: _ctx.enter_context(nc.semaphore(f"s_{e}")) for e in ENGS
                if e != "sync"}
        NLANES = 8
        for _li in range(NLANES):
            sems[f"lane{_li}"] = _ctx.enter_context(nc.semaphore(f"lane{_li}"))
        for _li in range(2):
            sems[f"glane{_li}"] = _ctx.enter_context(nc.semaphore(f"glane{_li}"))
        ccsem = _ctx.enter_context(nc.semaphore("ccsem"))
        block = _ctx.enter_context(nc.Block())

        r1a = nc.lookup_mloc(R1G).addr
        r2a = nc.lookup_mloc(R2G).addr
        _n = [0]

        def overlay(region, base, off, shape, dtype, nbytes):
            from concourse.bass_types import SBTensorHandle
            name = f"ovl{_n[0]}"
            _n[0] += 1
            nc._tensor(name, list(shape), dtype, type="SB")
            mloc = nc.lookup_mloc(name)
            mloc.allocated = True
            mloc.addr = base + off
            REG[name] = (region, off, off + nbytes)
            return SBTensorHandle(name, tuple(shape), dtype).ap()

        def o1(off, shape, dtype):
            n = int(np.prod(shape[1:])) * (2 if dtype == BF16 else 4)
            return overlay("r1", r1a, off, shape, dtype, n)

        def o2(off, shape, dtype):
            n = int(np.prod(shape[1:])) * (2 if dtype == BF16 else 4)
            return overlay("r2", r2a, off, shape, dtype, n)

        LN1Bs = [o1(dc * 4096, (128, 2048), BF16) for dc in range(8)]
        XF = o1(32768, (128, 16384), F32)
        YPRs = [o1(32768 + s * 4096, (128, 2048), BF16) for s in range(17)]
        X2Fs = [o1(dc * 8192, (128, 2048), F32) for dc in range(8)]
        X2Q = o1(65536, (128, 16384), BF16)
        DEQAs = [o1(kc * 6144, (128, 3072), BF16) for kc in range(8)]
        G2Ps = [o1(ic * 4096, (128, 2048), BF16) for ic in range(24)]
        YDB = o1(0, (128, 16384), BF16)
        SWT = o1(98304, (128, 768), F32)
        FDB = o1(98304, (128, 1024), F32)
        G2Rs = [o1(49152 + j * 4096, (128, 2048), BF16) for j in range(4)]
        XHB = o2(0, (128, 2052), BF16)
        ID4 = o2(8192, (128, 512), F32)
        DG1 = o2(10240, (128, 512), BF16)
        DG2 = o2(11264, (128, 512), BF16)
        WQ3 = o2(12288, (128, 1024), F32)
        WQ4 = o2(16384, (128, 1024), F32)
        WQB3 = o2(20480, (128, 1024), BF16)
        WQB4 = o2(22528, (128, 1024), BF16)
        XS = o2(24576, (128, 2048), F32)
        XSB = o2(32768, (128, 2048), BF16)
        GS = o2(36864, (128, 2048), F32)
        WST = o2(45056, (128, 1024), F32)
        WST2 = o2(49152, (128, 1024), F32)
        WSB = o2(53248, (128, 1024), BF16)
        WSB2 = o2(55296, (128, 1024), BF16)
        XSSM = o2(57344, (32, 2048), F32)
        PBUF = o2(0, (32, 4096), F32)
        PBS = o2(16384, (32, 4096), F32)
        HKT = o2(32768, (128, 4096), F32)
        PST = o2(49152, (128, 1024), F32)
        MOUTs = [o2(dc * 8192, (128, 2048), F32) for dc in range(8)]
        WOC = o2(65536, (128, 1024), F32)
        WOB1 = o2(69632, (128, 1024), BF16)
        WOB2 = o2(71680, (128, 1024), BF16)
        ST32 = o2(12288, (64, 3072), F32)
        FSA = o2(49152, (128, 1024), F32)
        FSB = o2(53248, (128, 1024), F32)
        NST = o2(57344, (128, 512), F32)
        NSTb = o2(59392, (128, 512), F32)
        SPT = o2(61440, (128, 2048), BF16)
        WSTEs = o2(65536, (128, 768), F32)
        SPT2 = o2(68608, (128, 2048), BF16)
        LDT = o2(0, (128, 2048), BF16)
        LDT2 = o2(4096, (128, 2048), BF16)
        FG1 = o2(8192, (128, 2048), F32)
        FG2 = o2(16384, (128, 2048), F32)
        FG3 = o2(24576, (128, 2048), F32)
        FG4 = o2(32768, (128, 2048), F32)
        FG5 = o2(40960, (128, 2048), F32)
        FG6 = o2(49152, (128, 2048), F32)
        NSTD = o2(57344, (128, 768), F32)
        LDG = o2(57344, (128, 2048), BF16)
        LDG2 = o2(61440, (128, 2048), BF16)
        SDW = o2(65536, (128, 768), F32)
        FDAe = o2(57344, (128, 768), F32)
        STD = o2(68608, (96, 1024), F32)
        ST4B = o2(0, (32, 1024), F32)
        ST4BI = o2(4096, (32, 1024), F32)
        DEQDs = [o2(8192 + kc * 2048, (128, 1024), BF16) for kc in range(24)]
        WDST = o2(57344, (128, 1024), F32)
        FDA = o2(61440, (128, 1024), F32)
        QF = o2(0, (128, 2048), F32)

        RM, RR, RS = 0, 1, 2

        def row(slot, lo=0, hi=2048):
            return ROWS[0:1, slot * 2048 + lo: slot * 2048 + hi]

        for dst, src in [
            (ID[:, :], idm[:, :]), (REV[:, :], revm[:, :]),
            (ONESB[:, :], onesb[:, :]), (ONESR[:, :], onesr[:, :]),
            (E4[:, :], e4i[:, :]), (SELS[:, :], selsi[:, :]),
            (BINC[:, :], binc_h[:, :]),
            (CONVW[:, :], convw_h[:, :]), (CONVB[:, :], convb_h[:, :]),
            (BOUTC[:, :], boutc_h[:, :]), (DVC[:, :], dvc_h[:, :]),
            (BSSM[:, :], bssm_h[:, :]), (AT[:, :], amat_h[:, :]),
            (BVC[:, :], bvc_h[:, :]), (CVC[:, :], cvc_h[:, :]),
        ]:
            sch.dma(dst, src)
        sch.dma(WST[:, 0:512], wssmp_h[:, :])
        sch.v(lambda e: e.tensor_copy(WSSMB[:, :], WST[:, 0:512]),
              [WST], [WSSMB])
        MAGCOL = COLS[:, 20:21]
        N128COL = COLS[:, 21:22]
        sch.v(lambda e: e.memset(MAGCOL, MAGIC), [], [COLS])
        sch.v(lambda e: e.memset(N128COL, -128.0), [], [COLS])

        def ln_rows():
            frow = FSLN[0:1, 0:512]
            rt = FSLN2[0:1, 0:512]
            for tc in range(NTC):
                mrow = row(RM, tc * 512, (tc + 1) * 512)
                rrow = row(RR, tc * 512, (tc + 1) * 512)
                sch.v(lambda e, m=mrow: e.tensor_mul(rt, m, m),
                      [ROWS], [FSLN2])
                sch.v(lambda e, r=rrow: e.tensor_sub(frow, r, rt),
                      [ROWS, FSLN2], [FSLN])
                sch.v(lambda e: e.tensor_scalar(rt, frow, EPS_LN, None,
                                                op0=ALU.add), [FSLN], [FSLN2])
                sch.s(lambda e: e.activation(frow, rt, AF.Sqrt),
                      [FSLN2], [FSLN])
                sch.v(lambda e, r=rrow: e.reciprocal(r, frow), [FSLN], [ROWS])

        def ln_stats(src_fn, n_chunks=NDC):
            inv = 1.0 / (128.0 * n_chunks)
            for tc in range(NTC):
                for ch in range(n_chunks):
                    src = src_fn(ch, tc)
                    sq = SQS if ch % 2 == 0 else SQSb
                    sch.s(lambda e, s=src, q=sq: e.activation(
                        q[:, :], s, AF.Identity), [src], [sq])
                    sch.t(lambda e, q=sq, st=(ch == 0), sp=(ch == n_chunks - 1):
                          e.matmul(PS_SM[0:1, 0:512], ONESB[:, :], q[:, :],
                                   start=st, stop=sp),
                          [ONESB, sq], [PS_SM])
                sch.s(lambda e, tc=tc: e.activation(
                    row(RM, tc * 512, (tc + 1) * 512), PS_SM[0:1, 0:512],
                    AF.Identity, bias=0.0, scale=inv), [PS_SM], [ROWS])
                for ch in range(n_chunks):
                    src = src_fn(ch, tc)
                    sq = SQS2 if ch % 2 == 0 else SQS2b
                    sch.s(lambda e, s=src, q=sq: e.activation(q[:, :], s, AF.Square),
                          [src], [sq])
                    sch.t(lambda e, q=sq, st=(ch == 0), sp=(ch == n_chunks - 1):
                          e.matmul(PS_SM[0:1, 0:512], ONESB[:, :], q[:, :],
                                   start=st, stop=sp),
                          [ONESB, sq], [PS_SM])
                sch.s(lambda e, tc=tc: e.activation(
                    row(RR, tc * 512, (tc + 1) * 512), PS_SM[0:1, 0:512],
                    AF.Identity, bias=0.0, scale=inv), [PS_SM], [ROWS])
            ln_rows()

        def bcast_mr(tc):
            sch.t(lambda e, tc=tc: e.matmul(
                PS_SM[:, 0:512], ONESR[:, :], row(RM, tc * 512, (tc + 1) * 512),
                start=True, stop=True), [ONESR, ROWS], [PS_SM])
            sch.t(lambda e, tc=tc: e.matmul(
                PS_SM2[:, 0:512], ONESR[:, :], row(RR, tc * 512, (tc + 1) * 512),
                start=True, stop=True), [ONESR, ROWS], [PS_SM2])

        for dc in range(NDC):
            sch.dma(XF[:, dc * 2048:(dc + 1) * 2048],
                    xt_h[dc * 128:(dc + 1) * 128, :])
        ln_stats(lambda dc, tc:
                 XF[:, dc * 2048 + tc * 512: dc * 2048 + (tc + 1) * 512])
        for tc in range(NTC):
            bcast_mr(tc)
            for dc in range(NDC):
                src = XF[:, dc * 2048 + tc * 512: dc * 2048 + (tc + 1) * 512]
                dst = LN1Bs[dc][:, tc * 512:(tc + 1) * 512]
                sch.v(lambda e, s=src: e.tensor_sub(FSLN[:, :], s, PS_SM[:, 0:512]),
                      [src, PS_SM], [FSLN])
                sch.v(lambda e, d=dst: e.tensor_mul(d, FSLN[:, :], PS_SM2[:, 0:512]),
                      [FSLN, PS_SM2], [LN1Bs[dc]])

        PSMMs = [PS_MM, PS_MM2]
        pidx = [0]

        def next_ps():
            pidx[0] ^= 1
            return PSMMs[pidx[0]]

        sch.dma(ID4[:, :], id4i[:, :])
        sch.v(lambda e: e.memset(XHB[:, 0:4], 0.0), [], [XHB])
        STG4 = [WST, WST2, WQ3, WQ4]
        STB4 = [WSB, WSB2, WQB3, WQB4]

        def wload(li, src_ap):
            wst, wsb = STG4[li % 4], STB4[li % 4]
            sch.dma(wst[:, :], src_ap)
            sch.s(lambda e, a=wst, b=wsb: e.activation(
                b[:, :], a[:, :], AF.Identity), [wst], [wsb])
            return wsb

        for hc in range(NHC):
            wsb = wload(2 * hc, winp_h[hc, :, :])
            dg = DG1 if hc % 2 == 0 else DG2
            for k in range(4):
                wbc = dataclasses.replace(CONVW[:, :], offset=hc * 4 + k,
                                          ap=[[64, 128], [0, 128]])
                sch.v(lambda e, k=k, w=wbc, d=dg: e.tensor_tensor(
                    d[:, k * 128:(k + 1) * 128], ID4[:, k * 128:(k + 1) * 128],
                    w, op=ALU.mult), [ID4, CONVW], [dg])
            for tc in range(NTC):
                ps = next_ps()
                for kc in range(NDC):
                    lhsT = wsb[:, kc * 128:(kc + 1) * 128]
                    rhs = LN1Bs[kc][:, tc * 512:(tc + 1) * 512]
                    sch.t(lambda e, l=lhsT, r=rhs, p=ps, st=(kc == 0),
                          sp=(kc == NDC - 1):
                          e.matmul(p[:, :], l, r, start=st, stop=sp),
                          [wsb, LN1Bs[kc]], [ps])
                sch.s(lambda e, tc=tc, hc=hc, p=ps: e.activation(
                    XHB[:, 4 + tc * 512: 4 + (tc + 1) * 512], p[:, :], AF.Identity,
                    bias=BINC[:, hc:hc + 1], scale=1.0), [ps, BINC], [XHB])
            wsb2 = wload(2 * hc + 1, winp_h[16 + hc, :, :])
            for tc in range(NTC):
                ps = next_ps()
                for kc in range(NDC):
                    lhsT = wsb2[:, kc * 128:(kc + 1) * 128]
                    rhs = LN1Bs[kc][:, tc * 512:(tc + 1) * 512]
                    sch.t(lambda e, l=lhsT, r=rhs, p=ps, st=(kc == 0),
                          sp=(kc == NDC - 1):
                          e.matmul(p[:, :], l, r, start=st, stop=sp),
                          [wsb2, LN1Bs[kc]], [ps])
                sch.s(lambda e, tc=tc, hc=hc, p=ps: e.activation(
                    GS[:, tc * 512:(tc + 1) * 512], p[:, :], AF.Silu,
                    bias=BINC[:, 16 + hc:17 + hc], scale=1.0), [ps, BINC], [GS])
            for tc in range(NTC):
                psc = PS_SM if tc % 2 == 0 else PS_SM2
                for k in range(4):
                    rhs = XHB[:, 1 + k + tc * 512: 1 + k + tc * 512 + 512]
                    sch.t(lambda e, r=rhs, k=k, p=psc, d=dg:
                          e.matmul(p[:, 0:512], d[:, k * 128:(k + 1) * 128], r,
                                   start=(k == 0), stop=(k == 3)),
                          [dg, XHB], [psc])
                sch.s(lambda e, tc=tc, hc=hc, p=psc: e.activation(
                    XS[:, tc * 512:(tc + 1) * 512], p[:, 0:512], AF.Silu,
                    bias=CONVB[:, hc:hc + 1], scale=1.0), [psc, CONVB], [XS])
                sch.s(lambda e, tc=tc, hc=hc, p=psc: e.activation(
                    XSB[:, tc * 512:(tc + 1) * 512], p[:, 0:512], AF.Silu,
                    bias=CONVB[:, hc:hc + 1], scale=1.0), [psc, CONVB], [XSB])
            for tc in range(NTC):
                ps = PS_BIG if tc < 2 else PS_BIG2
                pcol = (tc % 2) * 512
                lhsT = WSSMB[:, hc * 32:(hc + 1) * 32]
                sch.t(lambda e, l=lhsT, tc=tc, p=ps, pc=pcol,
                      st=(hc == 0), sp=(hc == NHC - 1):
                      e.matmul(p[0:32, pc:pc + 512], l,
                               XSB[:, tc * 512:(tc + 1) * 512], start=st, stop=sp),
                      [WSSMB, XSB], [ps])
            sch.v(lambda e, hc=hc: e.tensor_mul(
                YPRs[hc][:, :], XS[:, :], GS[:, :]),
                [XS, GS], [YPRs[hc]])
        sch.s(lambda e: e.activation(XSSM[:, 0:1024], PS_BIG[0:32, :],
                                     AF.Identity, bias=BSSM[:, :], scale=1.0),
              [PS_BIG, BSSM], [XSSM])
        sch.s(lambda e: e.activation(XSSM[:, 1024:2048], PS_BIG2[0:32, :],
                                     AF.Identity, bias=BSSM[:, :], scale=1.0),
              [PS_BIG2, BSSM], [XSSM])

        def ptile(d):
            return PBUF[:, (127 - d) * 32:(128 - d) * 32]

        sch.v(lambda e: e.tensor_copy(ptile(0), ID[0:32, 0:32]), [ID], [PBUF])
        sch.v(lambda e: e.tensor_copy(QTB[:, 0:32], AT[:, :]), [AT], [QTB])
        sch.v(lambda e: e.tensor_copy(CBUF[:, 0:1], CVC[:, :]), [CVC], [CBUF])
        qcur = 0
        delta = 1
        while delta <= 64:
            q = QTB[:, qcur * 32:(qcur + 1) * 32]
            width = delta * 32
            src0 = (128 - delta) * 32
            dst0 = (128 - 2 * delta) * 32
            off = 0
            while off < width:
                w = min(512, width - off)
                sch.t(lambda e, q=q, s=src0 + off, w=w: e.matmul(
                    PS_SM[0:32, 0:w], q, PBUF[:, s:s + w], start=True, stop=True),
                    [QTB, PBUF], [PS_SM])
                sch.v(lambda e, d=dst0 + off, w=w: e.tensor_copy(
                    PBUF[:, d:d + w], PS_SM[0:32, 0:w]), [PS_SM], [PBUF])
                off += w
            sch.t(lambda e, q=q, dl=delta: e.matmul(
                PS_SM[0:32, 0:dl], q, CBUF[:, 0:dl], start=True, stop=True),
                [QTB, CBUF], [PS_SM])
            sch.v(lambda e, dl=delta: e.tensor_copy(
                CBUF[:, dl:2 * dl], PS_SM[0:32, 0:dl]), [PS_SM], [CBUF])
            if delta < 64:
                sch.t(lambda e, q=q, dl=delta: e.matmul(
                    PS_SM[0:32, 32:64], ptile(dl), q, start=True, stop=True),
                    [PBUF, QTB], [PS_SM])
                sch.v(lambda e, qn=1 - qcur: e.tensor_copy(
                    QTB[:, qn * 32:(qn + 1) * 32], PS_SM[0:32, 32:64]),
                    [PS_SM], [QTB])
                qcur = 1 - qcur
            delta *= 2
        q64 = QTB[:, qcur * 32:(qcur + 1) * 32]
        sch.t(lambda e, q=q64: e.matmul(PS_SM[0:32, 0:32], q, ptile(64),
                                        start=True, stop=True),
              [QTB, PBUF], [PS_SM])
        sch.v(lambda e: e.tensor_copy(PCB[:, :], PS_SM[0:32, 0:32]),
              [PS_SM], [PCB])
        sch.t(lambda e, q=q64: e.matmul(PS_SM[0:32, 32:33], q, CBUF[:, 64:65],
                                        start=True, stop=True),
              [QTB, CBUF], [PS_SM])
        sch.v(lambda e: e.tensor_copy(CBUF[:, 128:129], PS_SM[0:32, 32:33]),
              [PS_SM], [CBUF])
        sch.v(lambda e: e.tensor_copy(
            CRB[:, :], dataclasses.replace(CBUF[:, :], offset=128,
                                           ap=[[132, 32], [-1, 128]])),
            [CBUF], [CRB], fence=True)
        sch.v(lambda e: e.tensor_scalar(
            WRB[:, 0:128], dataclasses.replace(CBUF[:, :], offset=127,
                                               ap=[[132, 32], [-1, 128]]),
            BVC[:, 0:1], None, op0=ALU.mult), [CBUF, BVC], [WRB], fence=True)
        sch.v(lambda e: e.memset(WRB[:, 128:256], 0.0), [], [WRB])
        sch.v(lambda e: e.tensor_scalar(PBS[:, :], PBUF[:, :], BVC[:, 0:1], None,
                                        op0=ALU.mult), [PBUF, BVC], [PBS],
              fence=True)
        for m in range(32):
            src = dataclasses.replace(PBS[:, :], offset=m * 4096,
                                      ap=[[4096, 1], [32, 128], [1, 32]])
            sch.dma(PST[:, m * 32:(m + 1) * 32], src)
        for m in range(32):
            src = dataclasses.replace(WRB[0:1, :], offset=m * 256,
                                      ap=[[256, 1], [1, 128], [1, 128]])
            sch.dma(HKT[:, m * 128:(m + 1) * 128], src)
        for i in range(NCH):
            sch.t(lambda e, i=i: e.transpose(
                PS_SM[:, 0:32], XSSM[:, i * 128:(i + 1) * 128], ID[0:32, 0:32]),
                [XSSM, ID], [PS_SM])
            sch.v(lambda e, i=i: e.tensor_copy(UTB[:, i * 32:(i + 1) * 32],
                                               PS_SM[:, 0:32]), [PS_SM], [UTB])
        for m in range(32):
            rhs = dataclasses.replace(UTB[:, :], offset=m, ap=[[512, 128], [32, 16]])
            sch.t(lambda e, m=m, r=rhs, st=(m == 0), sp=(m == 31): e.matmul(
                PS_SM2[0:32, 0:16], PST[:, m * 32:(m + 1) * 32], r,
                start=st, stop=sp), [PST, UTB], [PS_SM2])
        sch.v(lambda e: e.tensor_copy(GTB[:, :], PS_SM2[0:32, 0:16]),
              [PS_SM2], [GTB])
        sch.v(lambda e: e.memset(HSB[:, 0:1], 0.0), [], [HSB])
        for i in range(NCH):
            sch.t(lambda e, i=i: e.matmul(PS_SM[0:32, 0:1], PCB[:, :],
                                          HSB[:, i:i + 1], start=True, stop=False),
                  [PCB, HSB], [PS_SM])
            sch.t(lambda e, i=i: e.matmul(PS_SM[0:32, 0:1], ID[0:32, 0:32],
                                          GTB[:, i:i + 1], start=False, stop=True),
                  [ID, GTB], [PS_SM])
            sch.v(lambda e, i=i: e.tensor_copy(HSB[:, i + 1:i + 2],
                                               PS_SM[0:32, 0:1]), [PS_SM], [HSB])
        for m in range(32):
            rhs = dataclasses.replace(UTB[:, :], offset=m, ap=[[512, 128], [32, 16]])
            sch.t(lambda e, m=m, r=rhs, st=(m == 0): e.matmul(
                PS_SM2[:, 0:16], HKT[:, m * 128:(m + 1) * 128], r,
                start=st, stop=False), [HKT, UTB], [PS_SM2])
        sch.t(lambda e: e.matmul(PS_SM2[:, 0:16], CRB[:, :], HSB[:, 0:16],
                                 start=False, stop=True), [CRB, HSB], [PS_SM2])
        sch.v(lambda e: e.tensor_copy(SSB[:, :], PS_SM2[:, 0:16]),
              [PS_SM2], [SSB])
        sch.t(lambda e: e.transpose(PS_SM[0:16, 0:128], SSB[:, :], REV[:, :]),
              [SSB, REV], [PS_SM])
        sch.v(lambda e: e.tensor_copy(SR16[:, :], PS_SM[0:16, 0:128]),
              [PS_SM], [SR16])
        sch.dma(row(RS), SR16[:, :])

        def sbc(tc):
            pb = PS_BIG if tc < 2 else PS_BIG2
            return pb[:, (tc % 2) * 512:(tc % 2) * 512 + 512]

        for tc in range(NTC):
            sch.t(lambda e, tc=tc: e.matmul(
                sbc(tc), ONESR[:, :], row(RS, tc * 512, (tc + 1) * 512),
                start=True, stop=True),
                [ONESR, ROWS], [PS_BIG if tc < 2 else PS_BIG2])
        for hc in range(NHC):
            dst_slot = (hc + 16) % 17
            for tc in range(NTC):
                src = YPRs[hc][:, tc * 512:(tc + 1) * 512]
                dst = YPRs[dst_slot][:, tc * 512:(tc + 1) * 512]
                sch.v(lambda e, s=src, d=dst, hc=hc, tc=tc:
                      e.scalar_tensor_tensor(
                          d, sbc(tc), DVC[:, hc:hc + 1], s,
                          op0=ALU.add, op1=ALU.mult),
                      [PS_BIG if tc < 2 else PS_BIG2, DVC, YPRs[hc]],
                      [YPRs[dst_slot]])
        def wload2(li, src_ap):
            wsb = WOB1 if li % 2 == 0 else WOB2
            sch.dma(WOC[:, :], src_ap)
            sch.s(lambda e, b=wsb: e.activation(
                b[:, :], WOC[:, :], AF.Identity), [WOC], [wsb])
            return wsb

        for oc in range(NDC):
            wsbA = wload2(0, woutp_h[oc, :, 0:1024])
            wsbB = wload2(1, woutp_h[oc, :, 1024:2048])
            for tc in range(NTC):
                ps = next_ps()
                for hc in range(NHC):
                    slot = (hc + 16) % 17
                    wsb_ = wsbA if hc < 8 else wsbB
                    lhsT = wsb_[:, (hc % 8) * 128:(hc % 8 + 1) * 128]
                    rhs = YPRs[slot][:, tc * 512:(tc + 1) * 512]
                    sch.t(lambda e, l=lhsT, r=rhs, p=ps, st=(hc == 0),
                          sp=(hc == NHC - 1), w=wsb_:
                          e.matmul(p[:, :], l, r, start=st, stop=sp),
                          [wsb_, YPRs[slot]], [ps])
                dst = MOUTs[oc][:, tc * 512:(tc + 1) * 512]
                l1 = LN1Bs[oc][:, tc * 512:(tc + 1) * 512]
                sch.v(lambda e, d=dst, l=l1, oc=oc, p=ps: e.scalar_tensor_tensor(
                    d, p[:, :], BOUTC[:, oc:oc + 1], l, op0=ALU.add, op1=ALU.add),
                    [ps, BOUTC, LN1Bs[oc]], [MOUTs[oc]])
                sq = SQS if (oc + tc) % 2 == 0 else SQSb
                sch.s(lambda e, d=dst, q=sq: e.activation(
                    q[:, :], d, AF.Identity), [MOUTs[oc]], [sq])
                pb = PS_BIG if tc < 2 else PS_BIG2
                sch.t(lambda e, q=sq, tc=tc, pb=pb, st=(oc == 0), sp=(oc == 7):
                      e.matmul(pb[0:1, (tc % 2) * 512:(tc % 2) * 512 + 512],
                               ONESB[:, :], q[:, :], start=st, stop=sp),
                      [ONESB, sq], [pb])
        inv1k = 1.0 / 1024.0
        for tc in range(NTC):
            pb = PS_BIG if tc < 2 else PS_BIG2
            sch.s(lambda e, tc=tc, pb=pb: e.activation(
                row(RM, tc * 512, (tc + 1) * 512),
                pb[0:1, (tc % 2) * 512:(tc % 2) * 512 + 512],
                AF.Identity, bias=0.0, scale=inv1k), [pb], [ROWS])
            for ch in range(NDC):
                srcq = MOUTs[ch][:, tc * 512:(tc + 1) * 512]
                sq2 = SQS2 if ch % 2 == 0 else SQS2b
                sch.s(lambda e, s=srcq, q=sq2: e.activation(q[:, :], s,
                                                            AF.Square),
                      [MOUTs[ch]], [sq2])
                sch.t(lambda e, q=sq2, st=(ch == 0), sp=(ch == NDC - 1):
                      e.matmul(PS_SM[0:1, 0:512], ONESB[:, :], q[:, :],
                               start=st, stop=sp), [ONESB, sq2], [PS_SM])
            sch.s(lambda e, tc=tc: e.activation(
                row(RR, tc * 512, (tc + 1) * 512), PS_SM[0:1, 0:512],
                AF.Identity, bias=0.0, scale=inv1k), [PS_SM], [ROWS])
        ln_rows()
        for tc in range(NTC):
            bcast_mr(tc)
            for dc in range(NDC):
                mo = MOUTs[dc][:, tc * 512:(tc + 1) * 512]
                sch.dma(FSLN2[:, :],
                        xt_h[dc * 128:(dc + 1) * 128, tc * 512:(tc + 1) * 512])
                sch.v(lambda e, m=mo: e.tensor_sub(FSLN[:, :], m, PS_SM[:, 0:512]),
                      [MOUTs[dc], PS_SM], [FSLN])
                sch.v(lambda e: e.tensor_mul(FSLN3[:, :], FSLN[:, :],
                                             PS_SM2[:, 0:512]),
                      [FSLN, PS_SM2], [FSLN3])
                sch.v(lambda e, m=mo: e.tensor_add(m, FSLN2[:, :], FSLN3[:, :]),
                      [FSLN2, FSLN3], [MOUTs[dc]])
        for dc in range(NDC):
            sch.dma(x1_h[dc, :, :], MOUTs[dc][:, :])

        def maxred(src_ap, src_names):
            sch.v(lambda e, x=src_ap: e.tensor_reduce(
                MXC[:, 4:5], x, AX.X, ALU.max, apply_absolute_value=True),
                src_names, [MXC], fence=True)

        mstate = {}

        def maxcombine(key, pair):
            if key not in mstate:
                sch.v(lambda e, a=pair[0]: e.tensor_copy(MXC[:, a:a + 1],
                                                         MXC[:, 4:5]),
                      [MXC], [MXC], fence=True)
                mstate[key] = pair[0]
                return
            cur = mstate[key]
            nxt = pair[0] + pair[1] - cur
            sch.v(lambda e, a=cur, b=nxt: e.tensor_max(
                MXC[:, b:b + 1], MXC[:, a:a + 1], MXC[:, 4:5]), [MXC], [MXC],
                fence=True)
            mstate[key] = nxt

        CCN = [0]

        def cross_core_max(idx, srccols):
            for i, c in enumerate(srccols):
                sch.dma(MROW[:, :], MXC[:, c:c + 1])
                sch.v(lambda e, i=i: e.tensor_reduce(
                    CCSB[0:1, i:i + 1], MROW[:, :], AX.X, ALU.max),
                    [MROW], [CCSB], fence=True)
            for i in range(len(srccols), 8):
                sch.v(lambda e, i=i: e.memset(CCSB[0:1, i:i + 1], 0.0),
                      [], [CCSB])
            sch.dma(cc_in[idx].ap()[:, :], CCSB[:, :])

            def do_cc(eng, idx=idx):
                eng.collective_compute(
                    "AllReduce", ALU.max, replica_groups=[list(range(8))],
                    ins=[cc_in[idx].ap().opt()], outs=[cc_out[idx].ap().opt()],
                ).then_inc(ccsem, 1)
                eng.wait_ge(ccsem, CCN[0] + 1)
                CCN[0] += 1
                return eng.engine_nop()
            sch.g(do_cc, [cc_in[idx].ap()], [cc_out[idx].ap()])
            sch.dma(CCSB[:, :], cc_out[idx].ap()[:, :])

        def make_scale_cols(srci, colbase):
            sch.v(lambda e, i=srci: e.tensor_scalar(
                SCL[0:1, 4:5], CCSB[0:1, i:i + 1], QMAX_INV, None, op0=ALU.mult),
                [CCSB], [SCL], fence=True)
            sch.v(lambda e: e.tensor_scalar(
                SCL[0:1, 0:1], SCL[0:1, 4:5], 1e-12, None, op0=ALU.max),
                [SCL], [SCL], fence=True)
            sch.v(lambda e: e.reciprocal(SCL[0:1, 1:2], SCL[0:1, 0:1]),
                  [SCL], [SCL], fence=True)
            sch.v(lambda e: e.tensor_scalar(
                SCL[0:1, 2:3], SCL[0:1, 0:1], MAGIC, None, op0=ALU.mult),
                [SCL], [SCL], fence=True)
            sch.v(lambda e: e.tensor_scalar(
                SCL[0:1, 3:4], SCL[0:1, 0:1], -MAGIC, None, op0=ALU.mult),
                [SCL], [SCL], fence=True)
            sch.t(lambda e: e.matmul(PS_SM[:, 0:4], ONESR[:, :], SCL[0:1, 0:4],
                                     start=True, stop=True),
                  [ONESR, SCL], [PS_SM])
            sch.v(lambda e, cb=colbase: e.tensor_copy(COLS[:, cb:cb + 4],
                                                      PS_SM[:, 0:4]),
                  [PS_SM], [COLS])
            sch.v(lambda e, cb=colbase: e.tensor_copy(SCL[0:1, 4:5],
                                                      COLS[0:1, cb:cb + 1]),
                  [COLS], [SCL], fence=True)


        def scales_unit(nat_h, oc, hf):
            nst = NST if hf % 2 == 0 else NSTb
            sch.dma(nst[:, :], nat_h[oc, :, hf * 512:(hf + 1) * 512])
            src = dataclasses.replace(nst[:, :],
                                      ap=[[512, 128], [32, 16], [1, 32]])
            dst = dataclasses.replace(
                SWT[:, :], offset=oc * 32 + hf * 16,
                ap=[[768, 128], [1, 16], [0, 1]])
            sch.v(lambda e, s=src, d=dst: e.tensor_reduce(
                d, s, AX.X, ALU.max, apply_absolute_value=True),
                [nst], [SWT])

        def swt_finalize():
            sch.v(lambda e: e.tensor_scalar(FSA[:, 0:768], SWT[:, :], INV7,
                                            1e-12, op0=ALU.mult, op1=ALU.max),
                  [SWT], [FSA], fence=True)
            sch.v(lambda e: e.tensor_copy(SWT[:, :], FSA[:, 0:768]),
                  [FSA], [SWT], fence=True)
            sch.v(lambda e: e.reciprocal(FSA[:, 0:768], SWT[:, :]),
                  [SWT], [FSA], fence=True)

        def st_unit(oc):
            src = dataclasses.replace(SWT[:, :], offset=oc * 32,
                                      ap=[[768, 128], [1, 32]])
            srci = dataclasses.replace(FSA[:, :], offset=oc * 32,
                                       ap=[[1024, 128], [1, 32]])
            sch.t(lambda e, s=src: e.transpose(
                PS_SM[0:32, 0:128], s, ID[:, :]), [SWT, ID], [PS_SM])
            sch.t(lambda e, s=srci: e.transpose(
                PS_SM2[0:32, 0:128], s, ID[:, :]), [FSA, ID], [PS_SM2])
            sch.v(lambda e, oc=oc: e.tensor_copy(
                ST32[0:32, oc * 128:(oc + 1) * 128], PS_SM[0:32, 0:128]),
                [PS_SM], [ST32])
            sch.v(lambda e, oc=oc: e.tensor_copy(
                ST32[32:64, oc * 128:(oc + 1) * 128], PS_SM2[0:32, 0:128]),
                [PS_SM2], [ST32])

        def dequant_unit(wt_h, kc, sub):
            sch.dma(WSTEs[:, :], wt_h[kc, :, sub * 768:(sub + 1) * 768])
            for lo2, w2 in ((0, 512), (512, 256)):
                lo = sub * 768 + lo2
                sch.t(lambda e, lo=lo, lo2=lo2, w2=w2, kc=kc: e.matmul(
                    PS_BIG[:, lo2:lo2 + w2],
                    SELS[32:64, kc * 128:(kc + 1) * 128],
                    ST32[32:64, lo:lo + w2], start=True, stop=True),
                    [SELS, ST32], [PS_BIG])
                sch.t(lambda e, lo=lo, lo2=lo2, w2=w2, kc=kc: e.matmul(
                    PS_BIG2[:, lo2:lo2 + w2],
                    SELS[0:32, kc * 128:(kc + 1) * 128],
                    ST32[0:32, lo:lo + w2], start=True, stop=True),
                    [SELS, ST32], [PS_BIG2])
            sch.v(lambda e: e.tensor_mul(
                FSA[:, 0:768], WSTEs[:, :],
                PS_BIG[:, 0:768]), [WSTEs, PS_BIG], [FSA])
            sch.v(lambda e: e.tensor_scalar(
                FSB[:, 0:768], FSA[:, 0:768], MAGIC, -MAGIC,
                op0=ALU.add, op1=ALU.add), [FSA], [FSB])
            sch.v(lambda e, kc=kc, sub=sub: e.tensor_mul(
                DEQAs[kc][:, sub * 768:(sub + 1) * 768],
                FSB[:, 0:768], PS_BIG2[:, 0:768]),
                [FSB, PS_BIG2], [DEQAs[kc]])

        def ffn_prep_units(wn_h, wt_h):
            units = []
            for oc in range(24):
                for hf in range(2):
                    units.append(lambda oc=oc, hf=hf:
                                 scales_unit(wn_h, oc, hf))
            units.append(swt_finalize)
            for oc in range(24):
                units.append(lambda oc=oc: st_unit(oc))
            for kc in range(NDC):
                for sub in range(4):
                    units.append(lambda kc=kc, sub=sub:
                                 dequant_unit(wt_h, kc, sub))
            return units


        ln_stats(lambda dc, tc: MOUTs[dc][:, tc * 512:(tc + 1) * 512])
        scales_q = [lambda oc=oc, hf=hf: scales_unit(wg_n_h, oc, hf)
                    for oc in range(24) for hf in range(2)] + [swt_finalize]
        st_q = [lambda oc=oc: st_unit(oc) for oc in range(24)]
        deq_q = [(kc, (6144 * (kc + 1) - 1) // 8192,
                  lambda kc=kc, sub=sub: dequant_unit(wg_t_h, kc, sub))
                 for kc in range(NDC) for sub in range(4)]

        for it, dc in enumerate(range(NDC - 1, -1, -1)):
            for tc in range(NTC):
                bcast_mr(tc)
                mo = MOUTs[dc][:, tc * 512:(tc + 1) * 512]
                x2 = X2Fs[dc][:, tc * 512:(tc + 1) * 512]
                sch.v(lambda e, m=mo: e.tensor_sub(FSLN[:, :], m, PS_SM[:, 0:512]),
                      [MOUTs[dc], PS_SM], [FSLN])
                sch.v(lambda e, x=x2: e.tensor_mul(x, FSLN[:, :], PS_SM2[:, 0:512]),
                      [FSLN, PS_SM2], [X2Fs[dc]])
            maxred(X2Fs[dc][:, :], [X2Fs[dc]])
            maxcombine("x2", (0, 1))
            if it >= 1:
                for _ in range(8):
                    if scales_q:
                        scales_q.pop(0)()

        while scales_q:
            scales_q.pop(0)()
        cross_core_max(0, [mstate["x2"]])
        make_scale_cols(0, 0)
        for dc in range(NDC):
            x2 = X2Fs[dc][:, :]
            xq = X2Q[:, dc * 2048:(dc + 1) * 2048]
            sch.s(lambda e, x=x2: e.activation(
                QF[:, :], x, AF.Identity, bias=MAGCOL, scale=COLS[:, 1:2]),
                [X2Fs[dc], COLS], [QF])
            sch.v(lambda e, q=xq: e.tensor_scalar(
                q, QF[:, :], COLS[:, 0:1], COLS[:, 2:3],
                op0=ALU.mult, op1=ALU.subtract), [QF, COLS], [X2Q])
            while st_q:
                st_q.pop(0)()
            while deq_q and deq_q[0][1] <= dc:
                deq_q.pop(0)[2]()
        while st_q:
            st_q.pop(0)()
        while deq_q:
            deq_q.pop(0)[2]()

        def ffn_weight(wn_h, wt_h, sp_h, mkey, pair, after_oc=None,
                       prep_done=False):
            if not prep_done:
                for u in ffn_prep_units(wn_h, wt_h):
                    u()
            for oc in range(24):
                spt = SPT if oc % 2 == 0 else SPT2
                for tc in range(NTC):
                    ps = next_ps()
                    for j in range(NDC):
                        kc = (oc + j) % NDC
                        lhsT = DEQAs[kc][:, oc * 128:(oc + 1) * 128]
                        rhs = X2Q[:, kc * 2048 + tc * 512:
                                  kc * 2048 + (tc + 1) * 512]
                        sch.t(lambda e, l=lhsT, r=rhs, p=ps, st=(j == 0),
                              sp=(j == NDC - 1), kc=kc:
                              e.matmul(p[:, :], l, r, start=st, stop=sp),
                              [DEQAs[kc], X2Q], [ps])
                    sch.s(lambda e, tc=tc, p=ps, t=spt: e.activation(
                        t[:, tc * 512:(tc + 1) * 512], p[:, :], AF.Identity),
                        [ps], [spt])
                maxred(spt[:, :], [spt])
                maxcombine(mkey, pair)
                sch.dma(sp_h[oc, :, :], spt[:, :])
                if after_oc is not None:
                    after_oc(oc)

        ffn_weight(wg_n_h, wg_t_h, spgt_h, "gt", (2, 3), prep_done=True)
        cross_core_max(1, [mstate["gt"]])
        make_scale_cols(0, 4)

        def wd_prep_one(oc, q):
            nst = NSTD
            sch.dma(nst[:, :], wd_n_h[oc, :, q * 768:(q + 1) * 768])
            src = dataclasses.replace(nst[:, :],
                                      ap=[[768, 128], [32, 24], [1, 32]])
            dst = dataclasses.replace(SDW[:, :], offset=oc * 96 + q * 24,
                                      ap=[[768, 128], [1, 24], [0, 1]])
            sch.v(lambda e, s=src, d=dst: e.tensor_reduce(
                d, s, AX.X, ALU.max, apply_absolute_value=True),
                [nst], [SDW])

        wd_prep = [(oc, q) for oc in range(8) for q in range(4)]

        def wd_prep_finish():
            sch.v(lambda e: e.tensor_scalar(FDAe[:, :], SDW[:, :], INV7,
                                            1e-12, op0=ALU.mult, op1=ALU.max),
                  [SDW], [FDAe], fence=True)
            sch.v(lambda e: e.tensor_copy(SDW[:, :], FDAe[:, :]),
                  [FDAe], [SDW], fence=True)
            for oc in range(8):
                src = SDW[:, oc * 96:(oc + 1) * 96]
                sch.t(lambda e, s=src: e.transpose(PS_SM[0:96, 0:128], s,
                                                   ID[:, :]), [SDW, ID], [PS_SM])
                sch.v(lambda e, oc=oc: e.tensor_copy(
                    STD[:, oc * 128:(oc + 1) * 128], PS_SM[0:96, 0:128]),
                    [PS_SM], [STD])

        def g_chain(ic):
            ldt = LDT if ic % 2 == 0 else LDT2
            fga = FG1 if ic % 2 == 0 else FG3
            fgb = FG2 if ic % 2 == 0 else FG4
            fgc = FG5 if ic % 2 == 0 else FG6
            g2r = G2Rs[ic % 4]
            g2 = g2r[:, :]
            sch.dma(ldt[:, :], spgt_h[ic, :, :])
            sch.s(lambda e, l=ldt, f=fga: e.activation(
                f[:, :], l[:, :], AF.Identity, bias=MAGCOL,
                scale=COLS[:, 5:6]), [ldt, COLS], [fga])
            sch.s(lambda e, f=fga, g=fgb: e.activation(
                g[:, :], f[:, :], AF.Identity, bias=COLS[:, 7:8],
                scale=COLS[:, 4:5]), [fga, COLS], [fgb], fence=True)
            sch.s(lambda e, g=fgb, f=fgc: e.activation(
                f[:, :], g[:, :], AF.Sigmoid), [fgb], [fgc], fence=True)
            sch.v(lambda e, c=fgc, f=fga: e.tensor_scalar(
                f[:, :], c[:, :], 128.0, CMAX16, op0=ALU.add, op1=ALU.min),
                [fgc], [fga])
            sch.v(lambda e, f=fga, g=fgb, d=g2: e.scalar_tensor_tensor(
                d, f[:, :], N128COL, g[:, :], op0=ALU.add, op1=ALU.mult),
                [fga, fgb, COLS], [g2r])
            maxred(g2, [g2r])
            maxcombine("g2", (2, 3))
            sch.dma(spg2_h[ic, :, :], g2)

        ffn_weight(wu_n_h, wu_t_h, spup_h, "up", (5, 6), after_oc=g_chain)
        while wd_prep:
            wd_prep_one(*wd_prep.pop(0))
        wd_prep_finish()
        cross_core_max(4, [mstate["up"], mstate["g2"]])
        make_scale_cols(0, 8)
        make_scale_cols(1, 12)

        for ic in range(NIC):
            ldt = LDT if ic % 2 == 0 else LDT2
            fga = FG1 if ic % 2 == 0 else FG3
            fgb = FG2 if ic % 2 == 0 else FG4
            fgc = FG5 if ic % 2 == 0 else FG6
            ldg = LDG if ic % 2 == 0 else LDG2
            g2 = G2Ps[ic][:, :]
            sch.dma(ldt[:, :], spup_h[ic, :, :])
            sch.dma(ldg[:, :], spg2_h[ic, :, :])
            sch.s(lambda e, l=ldt, f=fga: e.activation(
                f[:, :], l[:, :], AF.Identity, bias=MAGCOL,
                scale=COLS[:, 9:10]), [ldt, COLS], [fga])
            sch.s(lambda e, f=fga, g=fgb: e.activation(
                g[:, :], f[:, :], AF.Identity, bias=COLS[:, 11:12],
                scale=COLS[:, 8:9]), [fga, COLS], [fgb], fence=True)
            sch.s(lambda e, g=ldg, f=fgc: e.activation(
                f[:, :], g[:, :], AF.Identity, bias=MAGCOL,
                scale=COLS[:, 13:14]), [ldg, COLS], [fgc])
            sch.v(lambda e, c=fgc, f=fga: e.tensor_scalar(
                f[:, :], c[:, :], COLS[:, 12:13], COLS[:, 14:15],
                op0=ALU.mult, op1=ALU.subtract), [fgc, COLS], [fga])
            sch.v(lambda e, a=fgb, b=fga, d=g2: e.tensor_mul(d, a[:, :], b[:, :]),
                  [fgb, fga], [G2Ps[ic]])
            maxred(g2, [G2Ps[ic]])
            maxcombine("pr", (2, 3))
        cross_core_max(3, [mstate["pr"]])
        make_scale_cols(0, 16)
        for ic in range(NIC):
            pr = G2Ps[ic][:, :]
            fg = FG1 if ic % 2 == 0 else FG3
            sch.s(lambda e, p=pr, f=fg: e.activation(
                f[:, :], p, AF.Identity, bias=MAGCOL, scale=COLS[:, 17:18]),
                [G2Ps[ic], COLS], [fg])
            sch.v(lambda e, p=pr, f=fg: e.tensor_scalar(
                p, f[:, :], COLS[:, 16:17], COLS[:, 18:19],
                op0=ALU.mult, op1=ALU.subtract), [fg, COLS], [G2Ps[ic]])

        for kc in range(24):
            if kc % 8 == 0:
                b = kc // 8
                sch.dma(ST4B[:, :], STD[32 * b:32 * b + 32, :])
                sch.v(lambda e: e.reciprocal(ST4BI[:, :], ST4B[:, :]),
                      [ST4B], [ST4BI])
            sch.dma(WDST[:, :], wd_t_h[kc, :, :])
            for half in range(2):
                j = kc % 8
                sch.t(lambda e, h=half, j=j: e.matmul(
                    PS_SM[:, 0:512], SELS[0:32, j * 128:(j + 1) * 128],
                    ST4BI[0:32, h * 512:(h + 1) * 512], start=True, stop=True),
                    [SELS, ST4BI], [PS_SM])
                sch.t(lambda e, h=half, j=j: e.matmul(
                    PS_SM2[:, 0:512], SELS[0:32, j * 128:(j + 1) * 128],
                    ST4B[0:32, h * 512:(h + 1) * 512], start=True, stop=True),
                    [SELS, ST4B], [PS_SM2])
                sch.v(lambda e, h=half: e.tensor_mul(
                    FDA[:, 0:512], WDST[:, h * 512:(h + 1) * 512],
                    PS_SM[:, 0:512]), [WDST, PS_SM], [FDA])
                sch.v(lambda e: e.tensor_scalar(
                    FDB[:, 0:512], FDA[:, 0:512], MAGIC, -MAGIC,
                    op0=ALU.add, op1=ALU.add), [FDA], [FDB])
                sch.v(lambda e, kc=kc, h=half: e.tensor_mul(
                    DEQDs[kc][:, h * 512:(h + 1) * 512], FDB[:, 0:512],
                    PS_SM2[:, 0:512]), [FDB, PS_SM2], [DEQDs[kc]])
        for oc in range(8):
            for tc in range(NTC):
                ps = next_ps()
                for kc in range(24):
                    lhsT = DEQDs[kc][:, oc * 128:(oc + 1) * 128]
                    rhs = G2Ps[kc][:, tc * 512:(tc + 1) * 512]
                    sch.t(lambda e, l=lhsT, r=rhs, p=ps, st=(kc == 0),
                          sp=(kc == 23):
                          e.matmul(p[:, :], l, r, start=st, stop=sp),
                          [DEQDs[kc], G2Ps[kc]], [ps])
                sq = [SQS, SQSb, SQS2, SQS2b][tc]
                sch.s(lambda e, p=ps, q=sq: e.activation(
                    q[:, :], p[:, :], AF.Identity), [ps], [sq])
                sch.dma(yd_h[oc, :, tc * 512:(tc + 1) * 512], sq[:, :])
                pb = PS_BIG if tc < 2 else PS_BIG2
                sch.t(lambda e, q=sq, tc=tc, pb=pb, st=(oc == 0), sp=(oc == 7):
                      e.matmul(pb[0:1, (tc % 2) * 512:(tc % 2) * 512 + 512],
                               ONESB[:, :], q[:, :], start=st, stop=sp),
                      [ONESB, sq], [pb])
        for oc in range(8):
            sch.dma(YDB[:, oc * 2048:(oc + 1) * 2048], yd_h[oc, :, :])
        inv = 1.0 / 1024.0
        for tc in range(NTC):
            pb = PS_BIG if tc < 2 else PS_BIG2
            sch.s(lambda e, tc=tc, pb=pb: e.activation(
                row(RM, tc * 512, (tc + 1) * 512),
                pb[0:1, (tc % 2) * 512:(tc % 2) * 512 + 512],
                AF.Identity, bias=0.0, scale=inv), [pb], [ROWS])
            for oc in range(8):
                ybt = YDB[:, oc * 2048 + tc * 512: oc * 2048 + (tc + 1) * 512]
                sq2 = SQS2 if oc % 2 == 0 else SQS2b
                sch.s(lambda e, q=ybt, q2=sq2: e.activation(q2[:, :], q,
                                                            AF.Square),
                      [YDB], [sq2])
                sch.t(lambda e, q2=sq2, st=(oc == 0), sp=(oc == 7): e.matmul(
                    PS_SM[0:1, 0:512], ONESB[:, :], q2[:, :], start=st, stop=sp),
                    [ONESB, sq2], [PS_SM])
            sch.s(lambda e, tc=tc: e.activation(
                row(RR, tc * 512, (tc + 1) * 512), PS_SM[0:1, 0:512],
                AF.Identity, bias=0.0, scale=inv), [PS_SM], [ROWS])
        ln_rows()
        for tc in range(NTC):
            bcast_mr(tc)
            for dc in range(NDC):
                ybt = YDB[:, dc * 2048 + tc * 512: dc * 2048 + (tc + 1) * 512]
                sch.dma(FSLN2[:, :], x1_h[dc, :, tc * 512:(tc + 1) * 512])
                sch.v(lambda e, q=ybt: e.tensor_sub(FSLN[:, :], q,
                                                    PS_SM[:, 0:512]),
                      [YDB, PS_SM], [FSLN])
                sch.v(lambda e: e.tensor_mul(FSLN3[:, :], FSLN[:, :],
                                             PS_SM2[:, 0:512]),
                      [FSLN, PS_SM2], [FSLN3])
                sch.v(lambda e: e.tensor_add(FSLN[:, :], FSLN2[:, :],
                                             FSLN3[:, :]),
                      [FSLN2, FSLN3], [FSLN])
                sch.dma(out_h[dc * 128:(dc + 1) * 128, tc * 512:(tc + 1) * 512],
                        FSLN[:, :])

        waits, totals, lanes, throttle, lane_tot, _ = sch.plan(NLANES)
        items = sch.items

        def run(eng_obj, enames):
            for i, (en, fn, ins, outs, fence) in enumerate(items):
                if en not in enames:
                    continue
                if throttle[i] is not None:
                    eng_obj.wait_ge(sems[throttle[i][0]], throttle[i][1])
                for (dkey, dunits) in waits[i]:
                    eng_obj.wait_ge(sems[dkey], dunits)
                r = fn(eng_obj)
                if lanes[i] is not None:
                    r.then_inc(sems[lanes[i]], 16)
                else:
                    r.then_inc(sems[en], 1)

        @block.tensor
        def _(eng):
            run(eng, ("tensor",))

        @block.vector
        def _(eng):
            run(eng, ("vector",))

        @block.scalar
        def _(eng):
            run(eng, ("scalar",))

        @block.gpsimd
        def _(eng):
            run(eng, ("gpsimd", "gdma"))

        @block.sync
        def _(eng):
            run(eng, ("sync",))
            for e2 in ("tensor", "vector", "scalar", "gpsimd"):
                eng.wait_ge(sems[e2], totals[e2])
            for ln, tot in lane_tot.items():
                eng.wait_ge(sems[ln], tot)

    return nc


_CACHE = {}


def kernel(**inputs):
    f32 = np.float32

    def a(x):
        return np.ascontiguousarray(np.asarray(x, dtype=f32))

    x = a(inputs["L0_attn_x"])
    W_in = a(inputs["L0_attn_W_in"])
    b_in = a(inputs["L0_attn_b_in"])
    conv_w = a(inputs["L0_attn_conv_w"])[:, 0, :]
    conv_b = a(inputs["L0_attn_conv_b"])
    W_ssm = a(inputs["L0_attn_W_ssm"])
    b_ssm = a(inputs["L0_attn_b_ssm"])
    A = a(inputs["L0_attn_A"])
    Bv = a(inputs["L0_attn_Bv"])
    Cv = a(inputs["L0_attn_Cv"])
    Dv = a(inputs["L0_attn_Dv"])
    W_out = a(inputs["L0_attn_W_out"])
    b_out = a(inputs["L0_attn_b_out"])
    Wg = a(inputs["L0_ffn_w_gate"])
    Wu = a(inputs["L0_ffn_w_up"])
    Wd = a(inputs["L0_ffn_w_down"])

    shared = {
        "winp": np.ascontiguousarray(
            W_in.reshape(32, 128, 8, 128).transpose(0, 3, 2, 1)
        ).reshape(32, 128, 1024),
        "woutp": np.ascontiguousarray(
            W_out.reshape(8, 128, 16, 128).transpose(0, 3, 2, 1)
        ).reshape(8, 128, 2048),
        "wssmp": np.ascontiguousarray(
            W_ssm.reshape(32, 16, 128).transpose(2, 1, 0)).reshape(128, 512),
        "wu_nat": Wu.reshape(24, 128, 1024).copy(),
        "wg_nat": Wg.reshape(24, 128, 1024).copy(),
        "wd_nat": Wd.reshape(8, 128, 3072).copy(),
        "wu_t": np.ascontiguousarray(
            Wu.reshape(24, 128, 8, 128).transpose(2, 3, 0, 1)
        ).reshape(8, 128, 3072),
        "wg_t": np.ascontiguousarray(
            Wg.reshape(24, 128, 8, 128).transpose(2, 3, 0, 1)
        ).reshape(8, 128, 3072),
        "wd_t": np.ascontiguousarray(
            Wd.reshape(8, 128, 24, 128).transpose(2, 3, 0, 1)
        ).reshape(24, 128, 1024),
        "binc": np.ascontiguousarray(b_in.reshape(32, 128).T),
        "convw": np.ascontiguousarray(
            conv_w.reshape(16, 128, 4).transpose(1, 0, 2)).reshape(128, 64),
        "convb": np.ascontiguousarray(conv_b.reshape(16, 128).T),
        "boutc": np.ascontiguousarray(b_out.reshape(8, 128).T),
        "dvc": np.ascontiguousarray(Dv.reshape(16, 128).T),
        "bssm": b_ssm.reshape(32, 1).copy(),
        "amat": A.copy(),
        "bvc": Bv.reshape(32, 1).copy(),
        "cvc": Cv.reshape(32, 1).copy(),
    }

    if "nc" not in _CACHE:
        _CACHE["nc"] = build_program()
    nc = _CACHE["nc"]

    in_maps = []
    for b in range(8):
        m = dict(shared)
        m["xt"] = np.ascontiguousarray(x[b].T)
        in_maps.append(m)
    res = run_bass_kernel_spmd(nc, in_maps, core_ids=list(range(8)))
    out = np.stack([np.asarray(res.results[b]["out_t"]).T for b in range(8)], axis=0)
    return np.ascontiguousarray(out)


# version 32
# speedup vs baseline: 1.0158x; 1.0097x over previous
import dataclasses
import numpy as np
import ml_dtypes

import concourse.bass as bass
import concourse.mybir as mybir
from concourse.bass_utils import run_bass_kernel_spmd

F32 = mybir.dt.float32
BF16 = mybir.dt.bfloat16
U8 = mybir.dt.uint8
AF = mybir.ActivationFunctionType
ALU = mybir.AluOpType
AX = mybir.AxisListType
BFNP = ml_dtypes.bfloat16

MAGIC = float(np.float32(12582912.0))
INV7 = float(np.float32(1.0) / np.float32(7.0))
QMAX_INV = float(np.float32(1.0) / np.float32(32767.0))
EPS_LN = 1e-5
CMAX16 = float(np.float32(128.0) + np.float32(65535.0) / np.float32(65536.0))

NDC, NHC, NIC, NTC = 8, 16, 24, 4
NCH = 16

ENGS = ("tensor", "vector", "scalar", "gpsimd", "sync")


class Sched:

    def __init__(self, regmap):
        self.items = []
        self.reg = regmap

    def _names(self, aps):
        out = []
        for a in aps:
            if a is None:
                continue
            if isinstance(a, str):
                out.append(a)
            elif hasattr(a, "tensor"):
                out.append(a.tensor.name)
            else:
                out.append(a.name)
        return out

    def op(self, eng, fn, ins, outs, fence=False):
        self.items.append((eng, fn, self._names(ins), self._names(outs), fence))

    def t(self, fn, ins, outs, fence=False):
        self.op("tensor", fn, ins, outs, fence)

    def v(self, fn, ins, outs, fence=False):
        self.op("vector", fn, ins, outs, fence)

    def s(self, fn, ins, outs, fence=False):
        self.op("scalar", fn, ins, outs, fence)

    def g(self, fn, ins, outs, fence=False):
        self.op("gpsimd", fn, ins, outs, fence)

    def dma(self, out, in_, **kw):
        self.op("sync",
                lambda e, o=out, i=in_, k=kw: e.dma_start(out=o, in_=i, **k),
                [in_], [out])

    def gdma(self, out, in_, **kw):
        self.op("gdma",
                lambda e, o=out, i=in_, k=kw: e.dma_start(out=o, in_=i, **k),
                [in_], [out])

    def fence_all(self):
        pass

    SER_LO = -1
    SER_HI = -1

    def plan(self, nlanes=6):
        cnt = {e: 0 for e in ENGS}
        dma_idx = [0]
        gdma_idx = [0]
        nglanes = 2
        writer = {}
        readers = {}
        waited = {}
        reg = self.reg

        def rng(n):
            return reg.get(n, (n, 0, 1))

        def overlaps(n1, n2):
            r1, r2 = rng(n1), rng(n2)
            return r1[0] == r2[0] and r1[1] < r2[2] and r2[1] < r1[2]

        DMAE = ("sync", "gdma")
        state_names = []
        waits = []
        lanes = []
        throttle = []
        prev_key = None
        for idx_item, (eng, fn, ins, outs, fence) in enumerate(self.items):
            deps = {}

            def add_dep(dkey, dunits, deng):
                if deng == eng and eng not in DMAE:
                    return
                deps[dkey] = max(deps.get(dkey, 0), dunits)

            for n in ins:
                for sn in state_names:
                    if overlaps(n, sn) and sn in writer:
                        k, u, de = writer[sn]
                        add_dep(k, u, de)
            for n in outs:
                for sn in state_names:
                    if not overlaps(n, sn):
                        continue
                    if sn in writer:
                        k, u, de = writer[sn]
                        add_dep(k, u, de)
                    for k2, (u2, de2) in readers.get(sn, {}).items():
                        add_dep(k2, u2, de2)
            if (self.SER_LO <= idx_item < self.SER_HI and prev_key is not None
                    and prev_key[0] != eng):
                deps[prev_key[0]] = max(deps.get(prev_key[0], 0), prev_key[1])
            if fence and eng not in DMAE and cnt[eng] > 0:
                deps[eng] = max(deps.get(eng, 0), cnt[eng])
            wl = []
            for dkey, dunits in deps.items():
                if waited.get((eng, dkey), 0) >= dunits:
                    continue
                waited[(eng, dkey)] = dunits
                wl.append((dkey, dunits))
            waits.append(wl)
            if eng in DMAE:
                if eng == "sync":
                    i = dma_idx[0]
                    lane = f"lane{i % nlanes}"
                    k = i // nlanes
                    dma_idx[0] += 1
                else:
                    i = gdma_idx[0]
                    lane = f"glane{i % nglanes}"
                    k = i // nglanes
                    gdma_idx[0] += 1
                throttle.append((lane, 16 * k) if k > 0 else None)
                lanes.append(lane)
                mykey, after = lane, 16 * (k + 1)
            else:
                cnt[eng] += 1
                lanes.append(None)
                throttle.append(None)
                mykey, after = eng, cnt[eng]
            for n in outs:
                if n not in writer and n not in readers:
                    state_names.append(n)
                writer[n] = (mykey, after, eng)
                readers[n] = {}
            for n in ins:
                if n not in writer and n not in readers:
                    state_names.append(n)
                readers.setdefault(n, {})[mykey] = (after, eng)
            prev_key = (mykey, after)
        lane_tot = {}
        for i, ln in enumerate(lanes):
            if ln is not None:
                lane_tot[ln] = lane_tot.get(ln, 0) + 16
        return waits, cnt, lanes, throttle, lane_tot, dma_idx[0] and nlanes or 0


def build_program():
    nc = bass.Bass("TRN2", target_bir_lowering=False,
                   dynamic_dma_scratch_size=4096)

    xt_h = nc.dram_tensor("xt", [1024, 2048], F32, kind="ExternalInput").ap()
    winp_h = nc.dram_tensor("winp", [32, 128, 1024], F32, kind="ExternalInput").ap()
    woutp_h = nc.dram_tensor("woutp", [8, 128, 2048], F32, kind="ExternalInput").ap()
    wssmp_h = nc.dram_tensor("wssmp", [128, 512], F32, kind="ExternalInput").ap()
    wu_n_h = nc.dram_tensor("wu_nat", [24, 128, 1024], F32, kind="ExternalInput").ap()
    wg_n_h = nc.dram_tensor("wg_nat", [24, 128, 1024], F32, kind="ExternalInput").ap()
    wd_n_h = nc.dram_tensor("wd_nat", [8, 128, 3072], F32, kind="ExternalInput").ap()
    wu_t_h = nc.dram_tensor("wu_t", [8, 128, 3072], F32, kind="ExternalInput").ap()
    wg_t_h = nc.dram_tensor("wg_t", [8, 128, 3072], F32, kind="ExternalInput").ap()
    wd_t_h = nc.dram_tensor("wd_t", [24, 128, 1024], F32, kind="ExternalInput").ap()
    binc_h = nc.dram_tensor("binc", [128, 32], F32, kind="ExternalInput").ap()
    convw_h = nc.dram_tensor("convw", [128, 64], F32, kind="ExternalInput").ap()
    convb_h = nc.dram_tensor("convb", [128, 16], F32, kind="ExternalInput").ap()
    boutc_h = nc.dram_tensor("boutc", [128, 8], F32, kind="ExternalInput").ap()
    dvc_h = nc.dram_tensor("dvc", [128, 16], F32, kind="ExternalInput").ap()
    bssm_h = nc.dram_tensor("bssm", [32, 1], F32, kind="ExternalInput").ap()
    amat_h = nc.dram_tensor("amat", [32, 32], F32, kind="ExternalInput").ap()
    bvc_h = nc.dram_tensor("bvc", [32, 1], F32, kind="ExternalInput").ap()
    cvc_h = nc.dram_tensor("cvc", [32, 1], F32, kind="ExternalInput").ap()

    out_h = nc.dram_tensor("out_t", [1024, 2048], F32, kind="ExternalOutput").ap()

    spup_h = nc.dram_tensor("sp_up", [24, 128, 2048], BF16, kind="Internal").ap()
    spgt_h = nc.dram_tensor("sp_gt", [24, 128, 2048], BF16, kind="Internal").ap()
    x1_h = nc.dram_tensor("sp_x1", [8, 128, 2048], F32, kind="Internal").ap()
    spg2_h = nc.dram_tensor("sp_g2", [24, 128, 2048], BF16, kind="Internal").ap()
    yd_h = nc.dram_tensor("sp_yd", [8, 128, 2048], BF16, kind="Internal").ap()
    cc_in = [nc.dram_tensor(f"cc_in{i}", [1, 8], F32, kind="Internal")
             for i in range(5)]
    cc_out = [nc.dram_tensor(f"cc_out{i}", [1, 8], F32, kind="Internal")
              for i in range(5)]

    idm = nc.inline_tensor(np.eye(128, dtype=np.float32), name="idm").ap()
    revm = nc.inline_tensor(np.eye(128, dtype=np.float32)[:, ::-1].copy(),
                            name="revm").ap()
    onesb = nc.inline_tensor(np.ones((128, 1), dtype=BFNP), name="onesb").ap()
    onesr = nc.inline_tensor(np.ones((1, 128), dtype=np.float32), name="onesr").ap()
    e4np = np.zeros((4, 128), np.float32)
    for bb in range(4):
        e4np[bb, bb * 32:(bb + 1) * 32] = 1.0
    e4i = nc.inline_tensor(e4np, name="e4i").ap()
    id4i = nc.inline_tensor(np.tile(np.eye(128, dtype=np.float32), (1, 4)),
                            name="id4i").ap()
    selnp = np.zeros((64, 1024), np.float32)
    for r in range(64):
        for j in range(8):
            p0 = (r % 32 - 4 * j) * 32
            if 0 <= p0 < 128:
                selnp[r, j * 128 + p0: j * 128 + p0 + 32] = 1.0
    selsi = nc.inline_tensor(selnp, name="selsi").ap()

    REG = {}
    sch = Sched(REG)

    from contextlib import ExitStack
    with ExitStack() as _ctx:
        def sb(name, shape, dtype):
            return _ctx.enter_context(nc.sbuf_tensor(name, shape, dtype))

        ID = sb("ID", [128, 128], F32)
        REV = sb("REV", [128, 128], F32)
        ONESB = sb("ONESB", [128, 1], BF16)
        ONESR = sb("ONESR", [1, 128], F32)
        E4 = sb("E4", [4, 128], F32)
        SELS = sb("SELS", [64, 1024], F32)
        BINC = sb("BINC", [128, 32], F32)
        CONVW = sb("CONVW", [128, 64], F32)
        CONVB = sb("CONVB", [128, 16], F32)
        BOUTC = sb("BOUTC", [128, 8], F32)
        DVC = sb("DVC", [128, 16], F32)
        BSSM = sb("BSSM", [32, 1], F32)
        AT = sb("AT", [32, 32], F32)
        BVC = sb("BVC", [32, 1], F32)
        CVC = sb("CVC", [32, 1], F32)
        WSSMB = sb("WSSMB", [128, 512], BF16)
        ROWS = sb("ROWS", [1, 6144], F32)
        MXC = sb("MXC", [128, 8], F32)
        COLS = sb("COLS", [128, 24], F32)
        SCL = sb("SCL", [1, 16], F32)
        CCSB = sb("CCSB", [1, 8], F32)
        MROW = sb("MROW", [1, 128], F32)
        UTB = sb("UTB", [128, 512], F32)
        HSB = sb("HSB", [32, 17], F32)
        GTB = sb("GTB", [32, 16], F32)
        PCB = sb("PCB", [32, 32], F32)
        QTB = sb("QTB", [32, 64], F32)
        SSB = sb("SSB", [128, 16], F32)
        SR16 = sb("SR16", [16, 128], F32)
        CBUF = sb("CBUF", [32, 132], F32)
        CRB = sb("CRB", [32, 128], F32)
        WRB = sb("WRB", [32, 256], F32)
        FSLN = sb("FSLN", [128, 512], F32)
        FSLN2 = sb("FSLN2", [128, 512], F32)
        FSLN3 = sb("FSLN3", [128, 512], F32)
        SQS = sb("SQS", [128, 512], BF16)
        SQSb = sb("SQSb", [128, 512], BF16)
        SQS2 = sb("SQS2", [128, 512], BF16)
        SQS2b = sb("SQS2b", [128, 512], BF16)
        R1G = sb("R1G", [128, 102400], U8)
        R2G = sb("R2G", [128, 73728], U8)
        PS_BIG = _ctx.enter_context(nc.psum_tensor("PS_BIG", [128, 1024], F32))
        PS_BIG2 = _ctx.enter_context(nc.psum_tensor("PS_BIG2", [128, 1024], F32))
        PS_MM = _ctx.enter_context(nc.psum_tensor("PS_MM", [128, 512], F32))
        PS_MM2 = _ctx.enter_context(nc.psum_tensor("PS_MM2", [128, 512], F32))
        PS_SM = _ctx.enter_context(nc.psum_tensor("PS_SM", [128, 512], F32))
        PS_SM2 = _ctx.enter_context(nc.psum_tensor("PS_SM2", [128, 512], F32))
        sems = {e: _ctx.enter_context(nc.semaphore(f"s_{e}")) for e in ENGS
                if e != "sync"}
        NLANES = 6
        for _li in range(NLANES):
            sems[f"lane{_li}"] = _ctx.enter_context(nc.semaphore(f"lane{_li}"))
        for _li in range(2):
            sems[f"glane{_li}"] = _ctx.enter_context(nc.semaphore(f"glane{_li}"))
        ccsem = _ctx.enter_context(nc.semaphore("ccsem"))
        block = _ctx.enter_context(nc.Block())

        r1a = nc.lookup_mloc(R1G).addr
        r2a = nc.lookup_mloc(R2G).addr
        _n = [0]

        def overlay(region, base, off, shape, dtype, nbytes):
            from concourse.bass_types import SBTensorHandle
            name = f"ovl{_n[0]}"
            _n[0] += 1
            nc._tensor(name, list(shape), dtype, type="SB")
            mloc = nc.lookup_mloc(name)
            mloc.allocated = True
            mloc.addr = base + off
            REG[name] = (region, off, off + nbytes)
            return SBTensorHandle(name, tuple(shape), dtype).ap()

        def o1(off, shape, dtype):
            n = int(np.prod(shape[1:])) * (2 if dtype == BF16 else 4)
            return overlay("r1", r1a, off, shape, dtype, n)

        def o2(off, shape, dtype):
            n = int(np.prod(shape[1:])) * (2 if dtype == BF16 else 4)
            return overlay("r2", r2a, off, shape, dtype, n)

        LN1Bs = [o1(dc * 4096, (128, 2048), BF16) for dc in range(8)]
        XF = o1(32768, (128, 16384), F32)
        YPRs = [o1(32768 + s * 4096, (128, 2048), BF16) for s in range(17)]
        X2Fs = [o1(dc * 8192, (128, 2048), F32) for dc in range(8)]
        X2Q = o1(65536, (128, 16384), BF16)
        DEQAs = [o1(kc * 6144, (128, 3072), BF16) for kc in range(8)]
        G2Ps = [o1(ic * 4096, (128, 2048), BF16) for ic in range(24)]
        YDB = o1(0, (128, 16384), BF16)
        SWT = o1(98304, (128, 768), F32)
        FDB = o1(98304, (128, 1024), F32)
        G2Rs = [o1(49152 + j * 4096, (128, 2048), BF16) for j in range(4)]
        XHB = o2(0, (128, 2052), BF16)
        ID4 = o2(8192, (128, 512), F32)
        DG1 = o2(10240, (128, 512), BF16)
        DG2 = o2(11264, (128, 512), BF16)
        WQ3 = o2(12288, (128, 1024), F32)
        WQ4 = o2(16384, (128, 1024), F32)
        WQB3 = o2(20480, (128, 1024), BF16)
        WQB4 = o2(22528, (128, 1024), BF16)
        XS = o2(24576, (128, 2048), F32)
        XSB = o2(32768, (128, 2048), BF16)
        GS = o2(36864, (128, 2048), F32)
        WST = o2(45056, (128, 1024), F32)
        WST2 = o2(49152, (128, 1024), F32)
        WSB = o2(53248, (128, 1024), BF16)
        WSB2 = o2(55296, (128, 1024), BF16)
        XSSM = o2(57344, (32, 2048), F32)
        PBUF = o2(0, (32, 4096), F32)
        PBS = o2(16384, (32, 4096), F32)
        HKT = o2(32768, (128, 4096), F32)
        PST = o2(49152, (128, 1024), F32)
        MOUTs = [o2(dc * 8192, (128, 2048), F32) for dc in range(8)]
        WOC = o2(65536, (128, 1024), F32)
        WOB1 = o2(69632, (128, 1024), BF16)
        WOB2 = o2(71680, (128, 1024), BF16)
        ST32 = o2(12288, (64, 3072), F32)
        FSA = o2(49152, (128, 1024), F32)
        FSB = o2(53248, (128, 1024), F32)
        NST = o2(57344, (128, 512), F32)
        NSTb = o2(59392, (128, 512), F32)
        SPT = o2(61440, (128, 2048), BF16)
        WSTEs = o2(65536, (128, 768), F32)
        SPT2 = o2(68608, (128, 2048), BF16)
        LDT = o2(0, (128, 2048), BF16)
        LDT2 = o2(4096, (128, 2048), BF16)
        FG1 = o2(8192, (128, 2048), F32)
        FG2 = o2(16384, (128, 2048), F32)
        FG3 = o2(24576, (128, 2048), F32)
        FG4 = o2(32768, (128, 2048), F32)
        FG5 = o2(40960, (128, 2048), F32)
        FG6 = o2(49152, (128, 2048), F32)
        NSTD = o2(57344, (128, 768), F32)
        LDG = o2(57344, (128, 2048), BF16)
        LDG2 = o2(61440, (128, 2048), BF16)
        SDW = o2(65536, (128, 768), F32)
        FDAe = o2(57344, (128, 768), F32)
        STD = o2(68608, (96, 1024), F32)
        ST4B = o2(0, (32, 1024), F32)
        ST4BI = o2(4096, (32, 1024), F32)
        DEQDs = [o2(8192 + kc * 2048, (128, 1024), BF16) for kc in range(24)]
        WDST = o2(57344, (128, 1024), F32)
        FDA = o2(61440, (128, 1024), F32)
        QF = o2(0, (128, 2048), F32)

        RM, RR, RS = 0, 1, 2

        def row(slot, lo=0, hi=2048):
            return ROWS[0:1, slot * 2048 + lo: slot * 2048 + hi]

        for dst, src in [
            (ID[:, :], idm[:, :]), (REV[:, :], revm[:, :]),
            (ONESB[:, :], onesb[:, :]), (ONESR[:, :], onesr[:, :]),
            (E4[:, :], e4i[:, :]), (SELS[:, :], selsi[:, :]),
            (BINC[:, :], binc_h[:, :]),
            (CONVW[:, :], convw_h[:, :]), (CONVB[:, :], convb_h[:, :]),
            (BOUTC[:, :], boutc_h[:, :]), (DVC[:, :], dvc_h[:, :]),
            (BSSM[:, :], bssm_h[:, :]), (AT[:, :], amat_h[:, :]),
            (BVC[:, :], bvc_h[:, :]), (CVC[:, :], cvc_h[:, :]),
        ]:
            sch.dma(dst, src)
        sch.dma(WST[:, 0:512], wssmp_h[:, :])
        sch.v(lambda e: e.tensor_copy(WSSMB[:, :], WST[:, 0:512]),
              [WST], [WSSMB])
        MAGCOL = COLS[:, 20:21]
        N128COL = COLS[:, 21:22]
        sch.v(lambda e: e.memset(MAGCOL, MAGIC), [], [COLS])
        sch.v(lambda e: e.memset(N128COL, -128.0), [], [COLS])

        def ln_rows():
            frow = FSLN[0:1, 0:512]
            rt = FSLN2[0:1, 0:512]
            for tc in range(NTC):
                mrow = row(RM, tc * 512, (tc + 1) * 512)
                rrow = row(RR, tc * 512, (tc + 1) * 512)
                sch.v(lambda e, m=mrow: e.tensor_mul(rt, m, m),
                      [ROWS], [FSLN2])
                sch.v(lambda e, r=rrow: e.tensor_sub(frow, r, rt),
                      [ROWS, FSLN2], [FSLN])
                sch.v(lambda e: e.tensor_scalar(rt, frow, EPS_LN, None,
                                                op0=ALU.add), [FSLN], [FSLN2])
                sch.s(lambda e: e.activation(frow, rt, AF.Sqrt),
                      [FSLN2], [FSLN])
                sch.v(lambda e, r=rrow: e.reciprocal(r, frow), [FSLN], [ROWS])

        def ln_stats(src_fn, n_chunks=NDC):
            inv = 1.0 / (128.0 * n_chunks)
            for tc in range(NTC):
                for ch in range(n_chunks):
                    src = src_fn(ch, tc)
                    sq = SQS if ch % 2 == 0 else SQSb
                    sch.s(lambda e, s=src, q=sq: e.activation(
                        q[:, :], s, AF.Identity), [src], [sq])
                    sch.t(lambda e, q=sq, st=(ch == 0), sp=(ch == n_chunks - 1):
                          e.matmul(PS_SM[0:1, 0:512], ONESB[:, :], q[:, :],
                                   start=st, stop=sp),
                          [ONESB, sq], [PS_SM])
                sch.s(lambda e, tc=tc: e.activation(
                    row(RM, tc * 512, (tc + 1) * 512), PS_SM[0:1, 0:512],
                    AF.Identity, bias=0.0, scale=inv), [PS_SM], [ROWS])
                for ch in range(n_chunks):
                    src = src_fn(ch, tc)
                    sq = SQS2 if ch % 2 == 0 else SQS2b
                    sch.s(lambda e, s=src, q=sq: e.activation(q[:, :], s, AF.Square),
                          [src], [sq])
                    sch.t(lambda e, q=sq, st=(ch == 0), sp=(ch == n_chunks - 1):
                          e.matmul(PS_SM[0:1, 0:512], ONESB[:, :], q[:, :],
                                   start=st, stop=sp),
                          [ONESB, sq], [PS_SM])
                sch.s(lambda e, tc=tc: e.activation(
                    row(RR, tc * 512, (tc + 1) * 512), PS_SM[0:1, 0:512],
                    AF.Identity, bias=0.0, scale=inv), [PS_SM], [ROWS])
            ln_rows()

        def bcast_mr(tc):
            sch.t(lambda e, tc=tc: e.matmul(
                PS_SM[:, 0:512], ONESR[:, :], row(RM, tc * 512, (tc + 1) * 512),
                start=True, stop=True), [ONESR, ROWS], [PS_SM])
            sch.t(lambda e, tc=tc: e.matmul(
                PS_SM2[:, 0:512], ONESR[:, :], row(RR, tc * 512, (tc + 1) * 512),
                start=True, stop=True), [ONESR, ROWS], [PS_SM2])

        for dc in range(NDC):
            sch.dma(XF[:, dc * 2048:(dc + 1) * 2048],
                    xt_h[dc * 128:(dc + 1) * 128, :])
        ln_stats(lambda dc, tc:
                 XF[:, dc * 2048 + tc * 512: dc * 2048 + (tc + 1) * 512])
        for tc in range(NTC):
            bcast_mr(tc)
            for dc in range(NDC):
                src = XF[:, dc * 2048 + tc * 512: dc * 2048 + (tc + 1) * 512]
                dst = LN1Bs[dc][:, tc * 512:(tc + 1) * 512]
                sch.v(lambda e, s=src: e.tensor_sub(FSLN[:, :], s, PS_SM[:, 0:512]),
                      [src, PS_SM], [FSLN])
                sch.v(lambda e, d=dst: e.tensor_mul(d, FSLN[:, :], PS_SM2[:, 0:512]),
                      [FSLN, PS_SM2], [LN1Bs[dc]])

        PSMMs = [PS_MM, PS_MM2]
        pidx = [0]

        def next_ps():
            pidx[0] ^= 1
            return PSMMs[pidx[0]]

        sch.dma(ID4[:, :], id4i[:, :])
        sch.v(lambda e: e.memset(XHB[:, 0:4], 0.0), [], [XHB])
        STG4 = [WST, WST2, WQ3, WQ4]
        STB4 = [WSB, WSB2, WQB3, WQB4]

        def wload(li, src_ap):
            wst, wsb = STG4[li % 4], STB4[li % 4]
            sch.dma(wst[:, :], src_ap)
            sch.s(lambda e, a=wst, b=wsb: e.activation(
                b[:, :], a[:, :], AF.Identity), [wst], [wsb])
            return wsb

        for hc in range(NHC):
            wsb = wload(2 * hc, winp_h[hc, :, :])
            dg = DG1 if hc % 2 == 0 else DG2
            for k in range(4):
                wbc = dataclasses.replace(CONVW[:, :], offset=hc * 4 + k,
                                          ap=[[64, 128], [0, 128]])
                sch.v(lambda e, k=k, w=wbc, d=dg: e.tensor_tensor(
                    d[:, k * 128:(k + 1) * 128], ID4[:, k * 128:(k + 1) * 128],
                    w, op=ALU.mult), [ID4, CONVW], [dg])
            for tc in range(NTC):
                ps = next_ps()
                for kc in range(NDC):
                    lhsT = wsb[:, kc * 128:(kc + 1) * 128]
                    rhs = LN1Bs[kc][:, tc * 512:(tc + 1) * 512]
                    sch.t(lambda e, l=lhsT, r=rhs, p=ps, st=(kc == 0),
                          sp=(kc == NDC - 1):
                          e.matmul(p[:, :], l, r, start=st, stop=sp),
                          [wsb, LN1Bs[kc]], [ps])
                sch.s(lambda e, tc=tc, hc=hc, p=ps: e.activation(
                    XHB[:, 4 + tc * 512: 4 + (tc + 1) * 512], p[:, :], AF.Identity,
                    bias=BINC[:, hc:hc + 1], scale=1.0), [ps, BINC], [XHB])
            wsb2 = wload(2 * hc + 1, winp_h[16 + hc, :, :])
            for tc in range(NTC):
                ps = next_ps()
                for kc in range(NDC):
                    lhsT = wsb2[:, kc * 128:(kc + 1) * 128]
                    rhs = LN1Bs[kc][:, tc * 512:(tc + 1) * 512]
                    sch.t(lambda e, l=lhsT, r=rhs, p=ps, st=(kc == 0),
                          sp=(kc == NDC - 1):
                          e.matmul(p[:, :], l, r, start=st, stop=sp),
                          [wsb2, LN1Bs[kc]], [ps])
                sch.s(lambda e, tc=tc, hc=hc, p=ps: e.activation(
                    GS[:, tc * 512:(tc + 1) * 512], p[:, :], AF.Silu,
                    bias=BINC[:, 16 + hc:17 + hc], scale=1.0), [ps, BINC], [GS])
            for tc in range(NTC):
                psc = PS_SM if tc % 2 == 0 else PS_SM2
                for k in range(4):
                    rhs = XHB[:, 1 + k + tc * 512: 1 + k + tc * 512 + 512]
                    sch.t(lambda e, r=rhs, k=k, p=psc, d=dg:
                          e.matmul(p[:, 0:512], d[:, k * 128:(k + 1) * 128], r,
                                   start=(k == 0), stop=(k == 3)),
                          [dg, XHB], [psc])
                sch.s(lambda e, tc=tc, hc=hc, p=psc: e.activation(
                    XS[:, tc * 512:(tc + 1) * 512], p[:, 0:512], AF.Silu,
                    bias=CONVB[:, hc:hc + 1], scale=1.0), [psc, CONVB], [XS])
                sch.s(lambda e, tc=tc, hc=hc, p=psc: e.activation(
                    XSB[:, tc * 512:(tc + 1) * 512], p[:, 0:512], AF.Silu,
                    bias=CONVB[:, hc:hc + 1], scale=1.0), [psc, CONVB], [XSB])
            for tc in range(NTC):
                ps = PS_BIG if tc < 2 else PS_BIG2
                pcol = (tc % 2) * 512
                lhsT = WSSMB[:, hc * 32:(hc + 1) * 32]
                sch.t(lambda e, l=lhsT, tc=tc, p=ps, pc=pcol,
                      st=(hc == 0), sp=(hc == NHC - 1):
                      e.matmul(p[0:32, pc:pc + 512], l,
                               XSB[:, tc * 512:(tc + 1) * 512], start=st, stop=sp),
                      [WSSMB, XSB], [ps])
            sch.v(lambda e, hc=hc: e.tensor_mul(
                YPRs[hc][:, :], XS[:, :], GS[:, :]),
                [XS, GS], [YPRs[hc]])
        sch.s(lambda e: e.activation(XSSM[:, 0:1024], PS_BIG[0:32, :],
                                     AF.Identity, bias=BSSM[:, :], scale=1.0),
              [PS_BIG, BSSM], [XSSM])
        sch.s(lambda e: e.activation(XSSM[:, 1024:2048], PS_BIG2[0:32, :],
                                     AF.Identity, bias=BSSM[:, :], scale=1.0),
              [PS_BIG2, BSSM], [XSSM])

        def ptile(d):
            return PBUF[:, (127 - d) * 32:(128 - d) * 32]

        sch.v(lambda e: e.tensor_copy(ptile(0), ID[0:32, 0:32]), [ID], [PBUF])
        sch.v(lambda e: e.tensor_copy(QTB[:, 0:32], AT[:, :]), [AT], [QTB])
        sch.v(lambda e: e.tensor_copy(CBUF[:, 0:1], CVC[:, :]), [CVC], [CBUF])
        qcur = 0
        delta = 1
        while delta <= 64:
            q = QTB[:, qcur * 32:(qcur + 1) * 32]
            width = delta * 32
            src0 = (128 - delta) * 32
            dst0 = (128 - 2 * delta) * 32
            off = 0
            while off < width:
                w = min(512, width - off)
                sch.t(lambda e, q=q, s=src0 + off, w=w: e.matmul(
                    PS_SM[0:32, 0:w], q, PBUF[:, s:s + w], start=True, stop=True),
                    [QTB, PBUF], [PS_SM])
                sch.v(lambda e, d=dst0 + off, w=w: e.tensor_copy(
                    PBUF[:, d:d + w], PS_SM[0:32, 0:w]), [PS_SM], [PBUF])
                off += w
            sch.t(lambda e, q=q, dl=delta: e.matmul(
                PS_SM[0:32, 0:dl], q, CBUF[:, 0:dl], start=True, stop=True),
                [QTB, CBUF], [PS_SM])
            sch.v(lambda e, dl=delta: e.tensor_copy(
                CBUF[:, dl:2 * dl], PS_SM[0:32, 0:dl]), [PS_SM], [CBUF])
            if delta < 64:
                sch.t(lambda e, q=q, dl=delta: e.matmul(
                    PS_SM[0:32, 32:64], ptile(dl), q, start=True, stop=True),
                    [PBUF, QTB], [PS_SM])
                sch.v(lambda e, qn=1 - qcur: e.tensor_copy(
                    QTB[:, qn * 32:(qn + 1) * 32], PS_SM[0:32, 32:64]),
                    [PS_SM], [QTB])
                qcur = 1 - qcur
            delta *= 2
        q64 = QTB[:, qcur * 32:(qcur + 1) * 32]
        sch.t(lambda e, q=q64: e.matmul(PS_SM[0:32, 0:32], q, ptile(64),
                                        start=True, stop=True),
              [QTB, PBUF], [PS_SM])
        sch.v(lambda e: e.tensor_copy(PCB[:, :], PS_SM[0:32, 0:32]),
              [PS_SM], [PCB])
        sch.t(lambda e, q=q64: e.matmul(PS_SM[0:32, 32:33], q, CBUF[:, 64:65],
                                        start=True, stop=True),
              [QTB, CBUF], [PS_SM])
        sch.v(lambda e: e.tensor_copy(CBUF[:, 128:129], PS_SM[0:32, 32:33]),
              [PS_SM], [CBUF])
        sch.v(lambda e: e.tensor_copy(
            CRB[:, :], dataclasses.replace(CBUF[:, :], offset=128,
                                           ap=[[132, 32], [-1, 128]])),
            [CBUF], [CRB], fence=True)
        sch.v(lambda e: e.tensor_scalar(
            WRB[:, 0:128], dataclasses.replace(CBUF[:, :], offset=127,
                                               ap=[[132, 32], [-1, 128]]),
            BVC[:, 0:1], None, op0=ALU.mult), [CBUF, BVC], [WRB], fence=True)
        sch.v(lambda e: e.memset(WRB[:, 128:256], 0.0), [], [WRB])
        sch.v(lambda e: e.tensor_scalar(PBS[:, :], PBUF[:, :], BVC[:, 0:1], None,
                                        op0=ALU.mult), [PBUF, BVC], [PBS],
              fence=True)
        for m in range(32):
            src = dataclasses.replace(PBS[:, :], offset=m * 4096,
                                      ap=[[4096, 1], [32, 128], [1, 32]])
            sch.dma(PST[:, m * 32:(m + 1) * 32], src)
        for m in range(32):
            src = dataclasses.replace(WRB[0:1, :], offset=m * 256,
                                      ap=[[256, 1], [1, 128], [1, 128]])
            sch.dma(HKT[:, m * 128:(m + 1) * 128], src)
        for i in range(NCH):
            sch.t(lambda e, i=i: e.transpose(
                PS_SM[:, 0:32], XSSM[:, i * 128:(i + 1) * 128], ID[0:32, 0:32]),
                [XSSM, ID], [PS_SM])
            sch.v(lambda e, i=i: e.tensor_copy(UTB[:, i * 32:(i + 1) * 32],
                                               PS_SM[:, 0:32]), [PS_SM], [UTB])
        for m in range(32):
            rhs = dataclasses.replace(UTB[:, :], offset=m, ap=[[512, 128], [32, 16]])
            sch.t(lambda e, m=m, r=rhs, st=(m == 0), sp=(m == 31): e.matmul(
                PS_SM2[0:32, 0:16], PST[:, m * 32:(m + 1) * 32], r,
                start=st, stop=sp), [PST, UTB], [PS_SM2])
        sch.v(lambda e: e.tensor_copy(GTB[:, :], PS_SM2[0:32, 0:16]),
              [PS_SM2], [GTB])
        sch.v(lambda e: e.memset(HSB[:, 0:1], 0.0), [], [HSB])
        for i in range(NCH):
            sch.t(lambda e, i=i: e.matmul(PS_SM[0:32, 0:1], PCB[:, :],
                                          HSB[:, i:i + 1], start=True, stop=False),
                  [PCB, HSB], [PS_SM])
            sch.t(lambda e, i=i: e.matmul(PS_SM[0:32, 0:1], ID[0:32, 0:32],
                                          GTB[:, i:i + 1], start=False, stop=True),
                  [ID, GTB], [PS_SM])
            sch.v(lambda e, i=i: e.tensor_copy(HSB[:, i + 1:i + 2],
                                               PS_SM[0:32, 0:1]), [PS_SM], [HSB])
        for m in range(32):
            rhs = dataclasses.replace(UTB[:, :], offset=m, ap=[[512, 128], [32, 16]])
            sch.t(lambda e, m=m, r=rhs, st=(m == 0): e.matmul(
                PS_SM2[:, 0:16], HKT[:, m * 128:(m + 1) * 128], r,
                start=st, stop=False), [HKT, UTB], [PS_SM2])
        sch.t(lambda e: e.matmul(PS_SM2[:, 0:16], CRB[:, :], HSB[:, 0:16],
                                 start=False, stop=True), [CRB, HSB], [PS_SM2])
        sch.v(lambda e: e.tensor_copy(SSB[:, :], PS_SM2[:, 0:16]),
              [PS_SM2], [SSB])
        sch.t(lambda e: e.transpose(PS_SM[0:16, 0:128], SSB[:, :], REV[:, :]),
              [SSB, REV], [PS_SM])
        sch.v(lambda e: e.tensor_copy(SR16[:, :], PS_SM[0:16, 0:128]),
              [PS_SM], [SR16])
        sch.dma(row(RS), SR16[:, :])

        def sbc(tc):
            pb = PS_BIG if tc < 2 else PS_BIG2
            return pb[:, (tc % 2) * 512:(tc % 2) * 512 + 512]

        for tc in range(NTC):
            sch.t(lambda e, tc=tc: e.matmul(
                sbc(tc), ONESR[:, :], row(RS, tc * 512, (tc + 1) * 512),
                start=True, stop=True),
                [ONESR, ROWS], [PS_BIG if tc < 2 else PS_BIG2])
        for hc in range(NHC):
            dst_slot = (hc + 16) % 17
            for tc in range(NTC):
                src = YPRs[hc][:, tc * 512:(tc + 1) * 512]
                dst = YPRs[dst_slot][:, tc * 512:(tc + 1) * 512]
                sch.v(lambda e, s=src, d=dst, hc=hc, tc=tc:
                      e.scalar_tensor_tensor(
                          d, sbc(tc), DVC[:, hc:hc + 1], s,
                          op0=ALU.add, op1=ALU.mult),
                      [PS_BIG if tc < 2 else PS_BIG2, DVC, YPRs[hc]],
                      [YPRs[dst_slot]])
        def wload2(li, src_ap):
            wsb = WOB1 if li % 2 == 0 else WOB2
            sch.dma(WOC[:, :], src_ap)
            sch.s(lambda e, b=wsb: e.activation(
                b[:, :], WOC[:, :], AF.Identity), [WOC], [wsb])
            return wsb

        for oc in range(NDC):
            wsbA = wload2(0, woutp_h[oc, :, 0:1024])
            wsbB = wload2(1, woutp_h[oc, :, 1024:2048])
            for tc in range(NTC):
                ps = next_ps()
                for hc in range(NHC):
                    slot = (hc + 16) % 17
                    wsb_ = wsbA if hc < 8 else wsbB
                    lhsT = wsb_[:, (hc % 8) * 128:(hc % 8 + 1) * 128]
                    rhs = YPRs[slot][:, tc * 512:(tc + 1) * 512]
                    sch.t(lambda e, l=lhsT, r=rhs, p=ps, st=(hc == 0),
                          sp=(hc == NHC - 1), w=wsb_:
                          e.matmul(p[:, :], l, r, start=st, stop=sp),
                          [wsb_, YPRs[slot]], [ps])
                dst = MOUTs[oc][:, tc * 512:(tc + 1) * 512]
                l1 = LN1Bs[oc][:, tc * 512:(tc + 1) * 512]
                sch.v(lambda e, d=dst, l=l1, oc=oc, p=ps: e.scalar_tensor_tensor(
                    d, p[:, :], BOUTC[:, oc:oc + 1], l, op0=ALU.add, op1=ALU.add),
                    [ps, BOUTC, LN1Bs[oc]], [MOUTs[oc]])
                sq = SQS if (oc + tc) % 2 == 0 else SQSb
                sch.s(lambda e, d=dst, q=sq: e.activation(
                    q[:, :], d, AF.Identity), [MOUTs[oc]], [sq])
                pb = PS_BIG if tc < 2 else PS_BIG2
                sch.t(lambda e, q=sq, tc=tc, pb=pb, st=(oc == 0), sp=(oc == 7):
                      e.matmul(pb[0:1, (tc % 2) * 512:(tc % 2) * 512 + 512],
                               ONESB[:, :], q[:, :], start=st, stop=sp),
                      [ONESB, sq], [pb])
        inv1k = 1.0 / 1024.0
        for tc in range(NTC):
            pb = PS_BIG if tc < 2 else PS_BIG2
            sch.s(lambda e, tc=tc, pb=pb: e.activation(
                row(RM, tc * 512, (tc + 1) * 512),
                pb[0:1, (tc % 2) * 512:(tc % 2) * 512 + 512],
                AF.Identity, bias=0.0, scale=inv1k), [pb], [ROWS])
            for ch in range(NDC):
                srcq = MOUTs[ch][:, tc * 512:(tc + 1) * 512]
                sq2 = SQS2 if ch % 2 == 0 else SQS2b
                sch.s(lambda e, s=srcq, q=sq2: e.activation(q[:, :], s,
                                                            AF.Square),
                      [MOUTs[ch]], [sq2])
                sch.t(lambda e, q=sq2, st=(ch == 0), sp=(ch == NDC - 1):
                      e.matmul(PS_SM[0:1, 0:512], ONESB[:, :], q[:, :],
                               start=st, stop=sp), [ONESB, sq2], [PS_SM])
            sch.s(lambda e, tc=tc: e.activation(
                row(RR, tc * 512, (tc + 1) * 512), PS_SM[0:1, 0:512],
                AF.Identity, bias=0.0, scale=inv1k), [PS_SM], [ROWS])
        ln_rows()
        for tc in range(NTC):
            bcast_mr(tc)
            for dc in range(NDC):
                mo = MOUTs[dc][:, tc * 512:(tc + 1) * 512]
                sch.dma(FSLN2[:, :],
                        xt_h[dc * 128:(dc + 1) * 128, tc * 512:(tc + 1) * 512])
                sch.v(lambda e, m=mo: e.tensor_sub(FSLN[:, :], m, PS_SM[:, 0:512]),
                      [MOUTs[dc], PS_SM], [FSLN])
                sch.v(lambda e: e.tensor_mul(FSLN3[:, :], FSLN[:, :],
                                             PS_SM2[:, 0:512]),
                      [FSLN, PS_SM2], [FSLN3])
                sch.v(lambda e, m=mo: e.tensor_add(m, FSLN2[:, :], FSLN3[:, :]),
                      [FSLN2, FSLN3], [MOUTs[dc]])
        for dc in range(NDC):
            sch.dma(x1_h[dc, :, :], MOUTs[dc][:, :])

        def maxred(src_ap, src_names):
            sch.v(lambda e, x=src_ap: e.tensor_reduce(
                MXC[:, 4:5], x, AX.X, ALU.max, apply_absolute_value=True),
                src_names, [MXC], fence=True)

        mstate = {}

        def maxcombine(key, pair):
            if key not in mstate:
                sch.v(lambda e, a=pair[0]: e.tensor_copy(MXC[:, a:a + 1],
                                                         MXC[:, 4:5]),
                      [MXC], [MXC], fence=True)
                mstate[key] = pair[0]
                return
            cur = mstate[key]
            nxt = pair[0] + pair[1] - cur
            sch.v(lambda e, a=cur, b=nxt: e.tensor_max(
                MXC[:, b:b + 1], MXC[:, a:a + 1], MXC[:, 4:5]), [MXC], [MXC],
                fence=True)
            mstate[key] = nxt

        CCN = [0]

        def cross_core_max(idx, srccols):
            for i, c in enumerate(srccols):
                sch.dma(MROW[:, :], MXC[:, c:c + 1])
                sch.v(lambda e, i=i: e.tensor_reduce(
                    CCSB[0:1, i:i + 1], MROW[:, :], AX.X, ALU.max),
                    [MROW], [CCSB], fence=True)
            for i in range(len(srccols), 8):
                sch.v(lambda e, i=i: e.memset(CCSB[0:1, i:i + 1], 0.0),
                      [], [CCSB])
            sch.dma(cc_in[idx].ap()[:, :], CCSB[:, :])

            def do_cc(eng, idx=idx):
                eng.collective_compute(
                    "AllReduce", ALU.max, replica_groups=[list(range(8))],
                    ins=[cc_in[idx].ap().opt()], outs=[cc_out[idx].ap().opt()],
                ).then_inc(ccsem, 1)
                eng.wait_ge(ccsem, CCN[0] + 1)
                CCN[0] += 1
                return eng.engine_nop()
            sch.g(do_cc, [cc_in[idx].ap()], [cc_out[idx].ap()])
            sch.dma(CCSB[:, :], cc_out[idx].ap()[:, :])

        def make_scale_cols(srci, colbase):
            sch.v(lambda e, i=srci: e.tensor_scalar(
                SCL[0:1, 4:5], CCSB[0:1, i:i + 1], QMAX_INV, None, op0=ALU.mult),
                [CCSB], [SCL], fence=True)
            sch.v(lambda e: e.tensor_scalar(
                SCL[0:1, 0:1], SCL[0:1, 4:5], 1e-12, None, op0=ALU.max),
                [SCL], [SCL], fence=True)
            sch.v(lambda e: e.reciprocal(SCL[0:1, 1:2], SCL[0:1, 0:1]),
                  [SCL], [SCL], fence=True)
            sch.v(lambda e: e.tensor_scalar(
                SCL[0:1, 2:3], SCL[0:1, 0:1], MAGIC, None, op0=ALU.mult),
                [SCL], [SCL], fence=True)
            sch.v(lambda e: e.tensor_scalar(
                SCL[0:1, 3:4], SCL[0:1, 0:1], -MAGIC, None, op0=ALU.mult),
                [SCL], [SCL], fence=True)
            sch.t(lambda e: e.matmul(PS_SM[:, 0:4], ONESR[:, :], SCL[0:1, 0:4],
                                     start=True, stop=True),
                  [ONESR, SCL], [PS_SM])
            sch.v(lambda e, cb=colbase: e.tensor_copy(COLS[:, cb:cb + 4],
                                                      PS_SM[:, 0:4]),
                  [PS_SM], [COLS])
            sch.v(lambda e, cb=colbase: e.tensor_copy(SCL[0:1, 4:5],
                                                      COLS[0:1, cb:cb + 1]),
                  [COLS], [SCL], fence=True)


        def scales_unit(nat_h, oc, hf):
            nst = NST if hf % 2 == 0 else NSTb
            sch.dma(nst[:, :], nat_h[oc, :, hf * 512:(hf + 1) * 512])
            src = dataclasses.replace(nst[:, :],
                                      ap=[[512, 128], [32, 16], [1, 32]])
            dst = dataclasses.replace(
                SWT[:, :], offset=oc * 32 + hf * 16,
                ap=[[768, 128], [1, 16], [0, 1]])
            sch.v(lambda e, s=src, d=dst: e.tensor_reduce(
                d, s, AX.X, ALU.max, apply_absolute_value=True),
                [nst], [SWT])

        def swt_finalize():
            sch.v(lambda e: e.tensor_scalar(FSA[:, 0:768], SWT[:, :], INV7,
                                            1e-12, op0=ALU.mult, op1=ALU.max),
                  [SWT], [FSA], fence=True)
            sch.v(lambda e: e.tensor_copy(SWT[:, :], FSA[:, 0:768]),
                  [FSA], [SWT], fence=True)
            sch.v(lambda e: e.reciprocal(FSA[:, 0:768], SWT[:, :]),
                  [SWT], [FSA], fence=True)

        def st_unit(oc):
            src = dataclasses.replace(SWT[:, :], offset=oc * 32,
                                      ap=[[768, 128], [1, 32]])
            srci = dataclasses.replace(FSA[:, :], offset=oc * 32,
                                       ap=[[1024, 128], [1, 32]])
            sch.t(lambda e, s=src: e.transpose(
                PS_SM[0:32, 0:128], s, ID[:, :]), [SWT, ID], [PS_SM])
            sch.t(lambda e, s=srci: e.transpose(
                PS_SM2[0:32, 0:128], s, ID[:, :]), [FSA, ID], [PS_SM2])
            sch.v(lambda e, oc=oc: e.tensor_copy(
                ST32[0:32, oc * 128:(oc + 1) * 128], PS_SM[0:32, 0:128]),
                [PS_SM], [ST32])
            sch.v(lambda e, oc=oc: e.tensor_copy(
                ST32[32:64, oc * 128:(oc + 1) * 128], PS_SM2[0:32, 0:128]),
                [PS_SM2], [ST32])

        def dequant_unit(wt_h, kc, sub):
            sch.dma(WSTEs[:, :], wt_h[kc, :, sub * 768:(sub + 1) * 768])
            for lo2, w2 in ((0, 512), (512, 256)):
                lo = sub * 768 + lo2
                sch.t(lambda e, lo=lo, lo2=lo2, w2=w2, kc=kc: e.matmul(
                    PS_BIG[:, lo2:lo2 + w2],
                    SELS[32:64, kc * 128:(kc + 1) * 128],
                    ST32[32:64, lo:lo + w2], start=True, stop=True),
                    [SELS, ST32], [PS_BIG])
                sch.t(lambda e, lo=lo, lo2=lo2, w2=w2, kc=kc: e.matmul(
                    PS_BIG2[:, lo2:lo2 + w2],
                    SELS[0:32, kc * 128:(kc + 1) * 128],
                    ST32[0:32, lo:lo + w2], start=True, stop=True),
                    [SELS, ST32], [PS_BIG2])
            sch.v(lambda e: e.tensor_mul(
                FSA[:, 0:768], WSTEs[:, :],
                PS_BIG[:, 0:768]), [WSTEs, PS_BIG], [FSA])
            sch.v(lambda e: e.tensor_scalar(
                FSB[:, 0:768], FSA[:, 0:768], MAGIC, -MAGIC,
                op0=ALU.add, op1=ALU.add), [FSA], [FSB])
            sch.v(lambda e, kc=kc, sub=sub: e.tensor_mul(
                DEQAs[kc][:, sub * 768:(sub + 1) * 768],
                FSB[:, 0:768], PS_BIG2[:, 0:768]),
                [FSB, PS_BIG2], [DEQAs[kc]])

        def ffn_prep_units(wn_h, wt_h):
            units = []
            for oc in range(24):
                for hf in range(2):
                    units.append(lambda oc=oc, hf=hf:
                                 scales_unit(wn_h, oc, hf))
            units.append(swt_finalize)
            for oc in range(24):
                units.append(lambda oc=oc: st_unit(oc))
            for kc in range(NDC):
                for sub in range(4):
                    units.append(lambda kc=kc, sub=sub:
                                 dequant_unit(wt_h, kc, sub))
            return units


        ln_stats(lambda dc, tc: MOUTs[dc][:, tc * 512:(tc + 1) * 512])
        scales_q = [lambda oc=oc, hf=hf: scales_unit(wg_n_h, oc, hf)
                    for oc in range(24) for hf in range(2)] + [swt_finalize]
        st_q = [lambda oc=oc: st_unit(oc) for oc in range(24)]
        deq_q = [(kc, (6144 * (kc + 1) - 1) // 8192,
                  lambda kc=kc, sub=sub: dequant_unit(wg_t_h, kc, sub))
                 for kc in range(NDC) for sub in range(4)]

        for it, dc in enumerate(range(NDC - 1, -1, -1)):
            for tc in range(NTC):
                bcast_mr(tc)
                mo = MOUTs[dc][:, tc * 512:(tc + 1) * 512]
                x2 = X2Fs[dc][:, tc * 512:(tc + 1) * 512]
                sch.v(lambda e, m=mo: e.tensor_sub(FSLN[:, :], m, PS_SM[:, 0:512]),
                      [MOUTs[dc], PS_SM], [FSLN])
                sch.v(lambda e, x=x2: e.tensor_mul(x, FSLN[:, :], PS_SM2[:, 0:512]),
                      [FSLN, PS_SM2], [X2Fs[dc]])
            maxred(X2Fs[dc][:, :], [X2Fs[dc]])
            maxcombine("x2", (0, 1))
            if it >= 1:
                for _ in range(8):
                    if scales_q:
                        scales_q.pop(0)()

        while scales_q:
            scales_q.pop(0)()
        cross_core_max(0, [mstate["x2"]])
        make_scale_cols(0, 0)
        for dc in range(NDC):
            x2 = X2Fs[dc][:, :]
            xq = X2Q[:, dc * 2048:(dc + 1) * 2048]
            sch.s(lambda e, x=x2: e.activation(
                QF[:, :], x, AF.Identity, bias=MAGCOL, scale=COLS[:, 1:2]),
                [X2Fs[dc], COLS], [QF])
            sch.v(lambda e, q=xq: e.tensor_scalar(
                q, QF[:, :], COLS[:, 0:1], COLS[:, 2:3],
                op0=ALU.mult, op1=ALU.subtract), [QF, COLS], [X2Q])
            while st_q:
                st_q.pop(0)()
            while deq_q and deq_q[0][1] <= dc:
                deq_q.pop(0)[2]()
        while st_q:
            st_q.pop(0)()
        while deq_q:
            deq_q.pop(0)[2]()

        def ffn_weight(wn_h, wt_h, sp_h, mkey, pair, after_oc=None,
                       prep_done=False):
            if not prep_done:
                for u in ffn_prep_units(wn_h, wt_h):
                    u()
            for oc in range(24):
                spt = SPT if oc % 2 == 0 else SPT2
                for tc in range(NTC):
                    ps = next_ps()
                    for j in range(NDC):
                        kc = (oc + j) % NDC
                        lhsT = DEQAs[kc][:, oc * 128:(oc + 1) * 128]
                        rhs = X2Q[:, kc * 2048 + tc * 512:
                                  kc * 2048 + (tc + 1) * 512]
                        sch.t(lambda e, l=lhsT, r=rhs, p=ps, st=(j == 0),
                              sp=(j == NDC - 1), kc=kc:
                              e.matmul(p[:, :], l, r, start=st, stop=sp),
                              [DEQAs[kc], X2Q], [ps])
                    sch.s(lambda e, tc=tc, p=ps, t=spt: e.activation(
                        t[:, tc * 512:(tc + 1) * 512], p[:, :], AF.Identity),
                        [ps], [spt])
                maxred(spt[:, :], [spt])
                maxcombine(mkey, pair)
                sch.dma(sp_h[oc, :, :], spt[:, :])
                if after_oc is not None:
                    after_oc(oc)

        ffn_weight(wg_n_h, wg_t_h, spgt_h, "gt", (2, 3), prep_done=True)
        cross_core_max(1, [mstate["gt"]])
        make_scale_cols(0, 4)

        def wd_prep_one(oc, q):
            nst = NSTD
            sch.dma(nst[:, :], wd_n_h[oc, :, q * 768:(q + 1) * 768])
            src = dataclasses.replace(nst[:, :],
                                      ap=[[768, 128], [32, 24], [1, 32]])
            dst = dataclasses.replace(SDW[:, :], offset=oc * 96 + q * 24,
                                      ap=[[768, 128], [1, 24], [0, 1]])
            sch.v(lambda e, s=src, d=dst: e.tensor_reduce(
                d, s, AX.X, ALU.max, apply_absolute_value=True),
                [nst], [SDW])

        wd_prep = [(oc, q) for oc in range(8) for q in range(4)]

        def wd_prep_finish():
            sch.v(lambda e: e.tensor_scalar(FDAe[:, :], SDW[:, :], INV7,
                                            1e-12, op0=ALU.mult, op1=ALU.max),
                  [SDW], [FDAe], fence=True)
            sch.v(lambda e: e.tensor_copy(SDW[:, :], FDAe[:, :]),
                  [FDAe], [SDW], fence=True)
            for oc in range(8):
                src = SDW[:, oc * 96:(oc + 1) * 96]
                sch.t(lambda e, s=src: e.transpose(PS_SM[0:96, 0:128], s,
                                                   ID[:, :]), [SDW, ID], [PS_SM])
                sch.v(lambda e, oc=oc: e.tensor_copy(
                    STD[:, oc * 128:(oc + 1) * 128], PS_SM[0:96, 0:128]),
                    [PS_SM], [STD])

        def g_chain(ic):
            ldt = LDT if ic % 2 == 0 else LDT2
            fga = FG1 if ic % 2 == 0 else FG3
            fgb = FG2 if ic % 2 == 0 else FG4
            fgc = FG5 if ic % 2 == 0 else FG6
            g2r = G2Rs[ic % 4]
            g2 = g2r[:, :]
            sch.dma(ldt[:, :], spgt_h[ic, :, :])
            sch.s(lambda e, l=ldt, f=fga: e.activation(
                f[:, :], l[:, :], AF.Identity, bias=MAGCOL,
                scale=COLS[:, 5:6]), [ldt, COLS], [fga])
            sch.s(lambda e, f=fga, g=fgb: e.activation(
                g[:, :], f[:, :], AF.Identity, bias=COLS[:, 7:8],
                scale=COLS[:, 4:5]), [fga, COLS], [fgb], fence=True)
            sch.s(lambda e, g=fgb, f=fgc: e.activation(
                f[:, :], g[:, :], AF.Sigmoid), [fgb], [fgc], fence=True)
            sch.v(lambda e, c=fgc, f=fga: e.tensor_scalar(
                f[:, :], c[:, :], 128.0, CMAX16, op0=ALU.add, op1=ALU.min),
                [fgc], [fga])
            sch.v(lambda e, f=fga, g=fgb, d=g2: e.scalar_tensor_tensor(
                d, f[:, :], N128COL, g[:, :], op0=ALU.add, op1=ALU.mult),
                [fga, fgb, COLS], [g2r])
            maxred(g2, [g2r])
            maxcombine("g2", (2, 3))
            sch.dma(spg2_h[ic, :, :], g2)

        ffn_weight(wu_n_h, wu_t_h, spup_h, "up", (5, 6), after_oc=g_chain)
        while wd_prep:
            wd_prep_one(*wd_prep.pop(0))
        wd_prep_finish()
        cross_core_max(4, [mstate["up"], mstate["g2"]])
        make_scale_cols(0, 8)
        make_scale_cols(1, 12)

        for ic in range(NIC):
            ldt = LDT if ic % 2 == 0 else LDT2
            fga = FG1 if ic % 2 == 0 else FG3
            fgb = FG2 if ic % 2 == 0 else FG4
            fgc = FG5 if ic % 2 == 0 else FG6
            ldg = LDG if ic % 2 == 0 else LDG2
            g2 = G2Ps[ic][:, :]
            sch.dma(ldt[:, :], spup_h[ic, :, :])
            sch.dma(ldg[:, :], spg2_h[ic, :, :])
            sch.s(lambda e, l=ldt, f=fga: e.activation(
                f[:, :], l[:, :], AF.Identity, bias=MAGCOL,
                scale=COLS[:, 9:10]), [ldt, COLS], [fga])
            sch.s(lambda e, f=fga, g=fgb: e.activation(
                g[:, :], f[:, :], AF.Identity, bias=COLS[:, 11:12],
                scale=COLS[:, 8:9]), [fga, COLS], [fgb], fence=True)
            sch.s(lambda e, g=ldg, f=fgc: e.activation(
                f[:, :], g[:, :], AF.Identity, bias=MAGCOL,
                scale=COLS[:, 13:14]), [ldg, COLS], [fgc])
            sch.v(lambda e, c=fgc, f=fga: e.tensor_scalar(
                f[:, :], c[:, :], COLS[:, 12:13], COLS[:, 14:15],
                op0=ALU.mult, op1=ALU.subtract), [fgc, COLS], [fga])
            sch.v(lambda e, a=fgb, b=fga, d=g2: e.tensor_mul(d, a[:, :], b[:, :]),
                  [fgb, fga], [G2Ps[ic]])
            maxred(g2, [G2Ps[ic]])
            maxcombine("pr", (2, 3))
        cross_core_max(3, [mstate["pr"]])
        make_scale_cols(0, 16)
        for ic in range(NIC):
            pr = G2Ps[ic][:, :]
            fg = FG1 if ic % 2 == 0 else FG3
            sch.s(lambda e, p=pr, f=fg: e.activation(
                f[:, :], p, AF.Identity, bias=MAGCOL, scale=COLS[:, 17:18]),
                [G2Ps[ic], COLS], [fg])
            sch.v(lambda e, p=pr, f=fg: e.tensor_scalar(
                p, f[:, :], COLS[:, 16:17], COLS[:, 18:19],
                op0=ALU.mult, op1=ALU.subtract), [fg, COLS], [G2Ps[ic]])

        for kc in range(24):
            if kc % 8 == 0:
                b = kc // 8
                sch.dma(ST4B[:, :], STD[32 * b:32 * b + 32, :])
                sch.v(lambda e: e.reciprocal(ST4BI[:, :], ST4B[:, :]),
                      [ST4B], [ST4BI])
            sch.dma(WDST[:, :], wd_t_h[kc, :, :])
            for half in range(2):
                j = kc % 8
                sch.t(lambda e, h=half, j=j: e.matmul(
                    PS_SM[:, 0:512], SELS[0:32, j * 128:(j + 1) * 128],
                    ST4BI[0:32, h * 512:(h + 1) * 512], start=True, stop=True),
                    [SELS, ST4BI], [PS_SM])
                sch.t(lambda e, h=half, j=j: e.matmul(
                    PS_SM2[:, 0:512], SELS[0:32, j * 128:(j + 1) * 128],
                    ST4B[0:32, h * 512:(h + 1) * 512], start=True, stop=True),
                    [SELS, ST4B], [PS_SM2])
                sch.v(lambda e, h=half: e.tensor_mul(
                    FDA[:, 0:512], WDST[:, h * 512:(h + 1) * 512],
                    PS_SM[:, 0:512]), [WDST, PS_SM], [FDA])
                sch.v(lambda e: e.tensor_scalar(
                    FDB[:, 0:512], FDA[:, 0:512], MAGIC, -MAGIC,
                    op0=ALU.add, op1=ALU.add), [FDA], [FDB])
                sch.v(lambda e, kc=kc, h=half: e.tensor_mul(
                    DEQDs[kc][:, h * 512:(h + 1) * 512], FDB[:, 0:512],
                    PS_SM2[:, 0:512]), [FDB, PS_SM2], [DEQDs[kc]])
        for oc in range(8):
            for tc in range(NTC):
                ps = next_ps()
                for kc in range(24):
                    lhsT = DEQDs[kc][:, oc * 128:(oc + 1) * 128]
                    rhs = G2Ps[kc][:, tc * 512:(tc + 1) * 512]
                    sch.t(lambda e, l=lhsT, r=rhs, p=ps, st=(kc == 0),
                          sp=(kc == 23):
                          e.matmul(p[:, :], l, r, start=st, stop=sp),
                          [DEQDs[kc], G2Ps[kc]], [ps])
                sq = [SQS, SQSb, SQS2, SQS2b][tc]
                sch.s(lambda e, p=ps, q=sq: e.activation(
                    q[:, :], p[:, :], AF.Identity), [ps], [sq])
                sch.dma(yd_h[oc, :, tc * 512:(tc + 1) * 512], sq[:, :])
                pb = PS_BIG if tc < 2 else PS_BIG2
                sch.t(lambda e, q=sq, tc=tc, pb=pb, st=(oc == 0), sp=(oc == 7):
                      e.matmul(pb[0:1, (tc % 2) * 512:(tc % 2) * 512 + 512],
                               ONESB[:, :], q[:, :], start=st, stop=sp),
                      [ONESB, sq], [pb])
        for oc in range(8):
            sch.dma(YDB[:, oc * 2048:(oc + 1) * 2048], yd_h[oc, :, :])
        inv = 1.0 / 1024.0
        for tc in range(NTC):
            pb = PS_BIG if tc < 2 else PS_BIG2
            sch.s(lambda e, tc=tc, pb=pb: e.activation(
                row(RM, tc * 512, (tc + 1) * 512),
                pb[0:1, (tc % 2) * 512:(tc % 2) * 512 + 512],
                AF.Identity, bias=0.0, scale=inv), [pb], [ROWS])
            for oc in range(8):
                ybt = YDB[:, oc * 2048 + tc * 512: oc * 2048 + (tc + 1) * 512]
                sq2 = SQS2 if oc % 2 == 0 else SQS2b
                sch.s(lambda e, q=ybt, q2=sq2: e.activation(q2[:, :], q,
                                                            AF.Square),
                      [YDB], [sq2])
                sch.t(lambda e, q2=sq2, st=(oc == 0), sp=(oc == 7): e.matmul(
                    PS_SM[0:1, 0:512], ONESB[:, :], q2[:, :], start=st, stop=sp),
                    [ONESB, sq2], [PS_SM])
            sch.s(lambda e, tc=tc: e.activation(
                row(RR, tc * 512, (tc + 1) * 512), PS_SM[0:1, 0:512],
                AF.Identity, bias=0.0, scale=inv), [PS_SM], [ROWS])
        ln_rows()
        for tc in range(NTC):
            bcast_mr(tc)
            for dc in range(NDC):
                ybt = YDB[:, dc * 2048 + tc * 512: dc * 2048 + (tc + 1) * 512]
                sch.dma(FSLN2[:, :], x1_h[dc, :, tc * 512:(tc + 1) * 512])
                sch.v(lambda e, q=ybt: e.tensor_sub(FSLN[:, :], q,
                                                    PS_SM[:, 0:512]),
                      [YDB, PS_SM], [FSLN])
                sch.v(lambda e: e.tensor_mul(FSLN3[:, :], FSLN[:, :],
                                             PS_SM2[:, 0:512]),
                      [FSLN, PS_SM2], [FSLN3])
                sch.v(lambda e: e.tensor_add(FSLN[:, :], FSLN2[:, :],
                                             FSLN3[:, :]),
                      [FSLN2, FSLN3], [FSLN])
                sch.dma(out_h[dc * 128:(dc + 1) * 128, tc * 512:(tc + 1) * 512],
                        FSLN[:, :])

        waits, totals, lanes, throttle, lane_tot, _ = sch.plan(NLANES)
        items = sch.items

        def run(eng_obj, enames):
            for i, (en, fn, ins, outs, fence) in enumerate(items):
                if en not in enames:
                    continue
                if throttle[i] is not None:
                    eng_obj.wait_ge(sems[throttle[i][0]], throttle[i][1])
                for (dkey, dunits) in waits[i]:
                    eng_obj.wait_ge(sems[dkey], dunits)
                r = fn(eng_obj)
                if lanes[i] is not None:
                    r.then_inc(sems[lanes[i]], 16)
                else:
                    r.then_inc(sems[en], 1)

        @block.tensor
        def _(eng):
            run(eng, ("tensor",))

        @block.vector
        def _(eng):
            run(eng, ("vector",))

        @block.scalar
        def _(eng):
            run(eng, ("scalar",))

        @block.gpsimd
        def _(eng):
            run(eng, ("gpsimd", "gdma"))

        @block.sync
        def _(eng):
            run(eng, ("sync",))
            for e2 in ("tensor", "vector", "scalar", "gpsimd"):
                eng.wait_ge(sems[e2], totals[e2])
            for ln, tot in lane_tot.items():
                eng.wait_ge(sems[ln], tot)

    return nc


_CACHE = {}


def kernel(**inputs):
    f32 = np.float32

    def a(x):
        return np.ascontiguousarray(np.asarray(x, dtype=f32))

    x = a(inputs["L0_attn_x"])
    W_in = a(inputs["L0_attn_W_in"])
    b_in = a(inputs["L0_attn_b_in"])
    conv_w = a(inputs["L0_attn_conv_w"])[:, 0, :]
    conv_b = a(inputs["L0_attn_conv_b"])
    W_ssm = a(inputs["L0_attn_W_ssm"])
    b_ssm = a(inputs["L0_attn_b_ssm"])
    A = a(inputs["L0_attn_A"])
    Bv = a(inputs["L0_attn_Bv"])
    Cv = a(inputs["L0_attn_Cv"])
    Dv = a(inputs["L0_attn_Dv"])
    W_out = a(inputs["L0_attn_W_out"])
    b_out = a(inputs["L0_attn_b_out"])
    Wg = a(inputs["L0_ffn_w_gate"])
    Wu = a(inputs["L0_ffn_w_up"])
    Wd = a(inputs["L0_ffn_w_down"])

    shared = {
        "winp": np.ascontiguousarray(
            W_in.reshape(32, 128, 8, 128).transpose(0, 3, 2, 1)
        ).reshape(32, 128, 1024),
        "woutp": np.ascontiguousarray(
            W_out.reshape(8, 128, 16, 128).transpose(0, 3, 2, 1)
        ).reshape(8, 128, 2048),
        "wssmp": np.ascontiguousarray(
            W_ssm.reshape(32, 16, 128).transpose(2, 1, 0)).reshape(128, 512),
        "wu_nat": Wu.reshape(24, 128, 1024).copy(),
        "wg_nat": Wg.reshape(24, 128, 1024).copy(),
        "wd_nat": Wd.reshape(8, 128, 3072).copy(),
        "wu_t": np.ascontiguousarray(
            Wu.reshape(24, 128, 8, 128).transpose(2, 3, 0, 1)
        ).reshape(8, 128, 3072),
        "wg_t": np.ascontiguousarray(
            Wg.reshape(24, 128, 8, 128).transpose(2, 3, 0, 1)
        ).reshape(8, 128, 3072),
        "wd_t": np.ascontiguousarray(
            Wd.reshape(8, 128, 24, 128).transpose(2, 3, 0, 1)
        ).reshape(24, 128, 1024),
        "binc": np.ascontiguousarray(b_in.reshape(32, 128).T),
        "convw": np.ascontiguousarray(
            conv_w.reshape(16, 128, 4).transpose(1, 0, 2)).reshape(128, 64),
        "convb": np.ascontiguousarray(conv_b.reshape(16, 128).T),
        "boutc": np.ascontiguousarray(b_out.reshape(8, 128).T),
        "dvc": np.ascontiguousarray(Dv.reshape(16, 128).T),
        "bssm": b_ssm.reshape(32, 1).copy(),
        "amat": A.copy(),
        "bvc": Bv.reshape(32, 1).copy(),
        "cvc": Cv.reshape(32, 1).copy(),
    }

    if "nc" not in _CACHE:
        _CACHE["nc"] = build_program()
    nc = _CACHE["nc"]

    in_maps = []
    for b in range(8):
        m = dict(shared)
        m["xt"] = np.ascontiguousarray(x[b].T)
        in_maps.append(m)
    res = run_bass_kernel_spmd(nc, in_maps, core_ids=list(range(8)))
    out = np.stack([np.asarray(res.results[b]["out_t"]).T for b in range(8)], axis=0)
    return np.ascontiguousarray(out)
